# version 14
# baseline (speedup 1.0000x reference)
import numpy as np
import ml_dtypes
from contextlib import ExitStack
import concourse.bass as bass
import concourse.mybir as mybir
from concourse.bass_utils import run_bass_kernel_spmd

F32 = mybir.dt.float32
BF16 = mybir.dt.bfloat16
AF = mybir.ActivationFunctionType
ALU = mybir.AluOpType
AX = mybir.AxisListType

D = 1024
S = 4096
NC = 8
NT = 32
EXP = 8
NEG = -1.0e38


def lhsT_blocks(W, mc=128):
    K, M = W.shape
    nK, nM = K // 128, M // mc
    return np.ascontiguousarray(W.reshape(nK, 128, nM, mc).transpose(2, 0, 1, 3))

def feat_cols(v):
    n = v.shape[0] // 128
    return np.ascontiguousarray(v.reshape(n, 128).T)


class EngWrap:

    def __init__(self, ctx, eng, sem, relaxed=False):
        self._ctx, self._eng, self._sem = ctx, eng, sem
        self._relaxed = relaxed

    def __getattr__(self, op):
        eng = self._eng
        f = getattr(eng, op)
        ctx, sem = self._ctx, self._sem
        relaxed = self._relaxed

        def g(*a, **k):
            if not relaxed and ctx.val(sem):
                ctx.wait(eng, sem, ctx.val(sem))
            ins = f(*a, **k)
            ctx.inc(sem, ins)
            return ins
        return g

class Scope:
    def __init__(self, ctx):
        self.ctx = ctx
        self.es = ExitStack()

    def sb(self, nm, shape, dtype):
        return self.es.enter_context(self.ctx.nc.sbuf_tensor(self.ctx.name(nm), shape, dtype))

    def ps(self, nm, shape, dtype=F32):
        return self.es.enter_context(self.ctx.nc.psum_tensor(self.ctx.name(nm), shape, dtype))

    def __enter__(self):
        return self

    def __exit__(self, *a):
        self.ctx.barrier()
        self.es.close()
        return False

class Ctx:
    def __init__(self, nc, es):
        self.nc = nc
        self.es = es
        self.sems = {}
        self.uid = 0
        self.V = EngWrap(self, nc.vector, "dve")
        self.A = EngWrap(self, nc.scalar, "act")
        self.G = EngWrap(self, nc.gpsimd, "pool")
        self.Vr = EngWrap(self, nc.vector, "dve", relaxed=True)
        self.Ar = EngWrap(self, nc.scalar, "act", relaxed=True)

    def name(self, p):
        self.uid += 1
        return f"{p}_{self.uid}"

    def scope(self):
        return Scope(self)

    def sem(self, nm):
        if nm not in self.sems:
            h = self.es.enter_context(self.nc.semaphore(nm))
            self.sems[nm] = [h, 0]
        return self.sems[nm]

    def inc(self, nm, instr, amt=1):
        s = self.sem(nm)
        instr.then_inc(s[0], amt)
        s[1] += amt
        return s[1]

    def val(self, nm):
        return self.sem(nm)[1]

    def wait(self, eng, nm, v=None):
        if isinstance(eng, EngWrap):
            eng = eng._eng
        s = self.sem(nm)
        eng.wait_ge(s[0], s[1] if v is None else v)

    def barrier(self):
        snap = [(h, c) for (h, c) in self.sems.values() if c > 0]
        nc = self.nc
        for eng in (nc.sync, nc.gpsimd, nc.tensor, nc.vector, nc.scalar):
            for h, c in snap:
                eng.wait_ge(h, c)


def host_constants():
    c = {}
    c["c_ones1_f32"] = np.ones((1, 128), np.float32)
    c["c_ones128_bf"] = np.ones((128, 1), np.float32)
    c["c_ident_f32"] = np.eye(128, dtype=np.float32)
    c["c_ident_bf"] = np.eye(128, dtype=np.float32)
    c["c_iotaT"] = np.broadcast_to(np.tile(np.arange(8, dtype=np.float32), 32), (128, 256)).copy()
    c["c_iotam1000T"] = c["c_iotaT"] - 1000.0
    tm = np.zeros((128, 128), np.float32)
    tm[np.tril_indices(128, -1)] = NEG
    c["c_trimask"] = tm
    t01 = np.ones((128, 128), np.float32)
    t01[np.tril_indices(128, -1)] = 0.0
    c["c_tri01"] = t01
    c["c_fullneg"] = np.full((128, 128), NEG, np.float32)
    inv = (10000.0 ** (-np.arange(0, 32, 2, dtype=np.float32) / 32.0))
    ang = np.arange(S, dtype=np.float32)[None, :] * inv[:, None]
    cos, sin = np.cos(ang), np.sin(ang)
    ct = np.zeros((32, S), np.float32)
    st = np.zeros((32, S), np.float32)
    ct[0::2] = cos; ct[1::2] = cos
    st[0::2] = -sin; st[1::2] = sin
    c["c_ropecs"] = np.concatenate([ct, st], axis=1)
    P = np.zeros((32, 32), np.float32)
    for i in range(32):
        P[i ^ 1, i] = 1.0
    c["c_pswap"] = P
    return c

CONST_KEYS = ["c_ones1_f32", "c_ones128_bf", "c_ident_f32", "c_ident_bf",
              "c_iotaT", "c_iotam1000T"]
KEYS_SUB0 = CONST_KEYS + ["wg0", "bg0", "w1l0r", "w2l0r", "b1l0", "b2l0"]
KEYS_SUB1 = CONST_KEYS + ["c_trimask", "c_tri01", "c_ropecs", "c_pswap", "c_fullneg",
                          "wdq", "wdkv", "wuq", "wuk", "wqr", "wuvr", "wkr", "wo"]
KEYS_SUB3 = CONST_KEYS + ["wg1", "bg1", "w1l1", "w2l1", "b1l1", "b2l1", "gam1", "bet1"]
KEYS_SUB4 = CONST_KEYS + ["wqkv", "bqkv", "wout", "bout"]

def declare_inputs(nc, host_arrays, keys=None):
    aps = {}
    for k, v in host_arrays.items():
        if keys is not None and k not in keys:
            continue
        dt = BF16 if v.dtype == ml_dtypes.bfloat16 else F32
        aps[k] = nc.dram_tensor(k, list(v.shape), dt, kind="ExternalInput").ap()
    return aps

def load_consts(ctx, sc, aps):
    nc = ctx.nc
    C = {}
    spec = [("ones1_f32", "c_ones1_f32", [1, 128], F32, False),
            ("ones128_bf", "c_ones128_bf", [128, 1], BF16, True),
            ("ident_f32", "c_ident_f32", [128, 128], F32, False),
            ("ident_bf", "c_ident_bf", [128, 128], BF16, True),
            ("iotaT", "c_iotaT", [128, 256], F32, False),
            ("iotam1000T", "c_iotam1000T", [128, 256], F32, False)]
    for nm, key, shape, dt, cast in spec:
        C[nm] = sc.sb(nm, shape, dt)
        eng, q = (nc.gpsimd, "qg") if cast else (nc.sync, "qs")
        ctx.inc(q, eng.dma_start(out=C[nm][:], in_=aps[key]), 16)
    ctx.barrier()
    return C

def load_attn_consts(ctx, ph, aps, rope=False):
    nc = ctx.nc
    C2 = {}
    spec = [("trimask", "c_trimask", [128, 128], F32, False),
            ("trimask_bf", "c_trimask", [128, 128], BF16, True),
            ("fullneg_bf", "c_fullneg", [128, 128], BF16, True)]
    if rope:
        spec += [("pswap", "c_pswap", [32, 32], BF16, True),
                 ("tri01", "c_tri01", [128, 128], BF16, True)]
    for nm, key, shape, dt, cast in spec:
        C2[nm] = ph.sb(nm, shape, dt)
        eng, q = (nc.gpsimd, "qg") if cast else (nc.sync, "qs")
        ctx.inc(q, eng.dma_start(out=C2[nm][:], in_=aps[key]), 16)
    return C2


def emit_w_convert(ctx, bufs, state, pairs):
    nc = ctx.nc
    st_after = state
    for src_ap, dst_ap in pairs:
        B = src_ap.shape[0]
        M = src_ap.shape[2]
        per = max(1, 2048 // M)
        for b0 in range(0, B, per):
            nb = min(per, B - b0)
            sl = state.setdefault("_i", 0) % 2
            state["_i"] += 1
            buf = bufs[sl][:, 0:nb*M].rearrange("k (b m) -> k b m", b=nb)
            if sl in st_after:
                ctx.wait(nc.gpsimd, "wcs", st_after[sl])
            ctx.inc("wcl", nc.gpsimd.dma_start(
                out=buf, in_=src_ap[b0:b0+nb].rearrange("b k m -> k b m")), 16)
            ctx.wait(nc.sync, "wcl", ctx.val("wcl"))
            ctx.inc("wcs", nc.sync.dma_start(
                out=dst_ap[b0:b0+nb].rearrange("b k m -> k b m"), in_=buf), 16)
            st_after[sl] = ctx.val("wcs")


class StatsKit:
    def __init__(self, ctx, sc, C, banks=None):
        self.ctx, self.C = ctx, C
        self.xb = sc.sb("st_xb", [128, NC * 512], BF16)
        self.xq = sc.sb("st_xq", [128, NC * 512], BF16)
        self.rw = sc.sb("st_rw", [1, 4 * 512], F32)
        if banks is None:
            self.ps_s = sc.ps("st_s", [1, 512])[:]
            self.ps_ss = sc.ps("st_ss", [1, 512])[:]
            self.pb_m = sc.ps("st_m", [128, 512])[:]
            self.pb_r = sc.ps("st_r", [128, 512])[:]
        else:
            self.ps_s, self.ps_ss, self.pb_m, self.pb_r = banks

    def emit(self, get_src, denom=1024.0, eps=1e-5):
        ctx, C = self.ctx, self.C
        nc, V, A = ctx.nc, ctx.V, ctx.A
        R = lambda i: self.rw[:, 512*i:512*(i+1)]
        Vr, Ar = ctx.Vr, ctx.Ar
        for c in range(NC):
            src = get_src(c)
            (V if c == 0 else Vr).tensor_copy(self.xb[:, 512*c:512*(c+1)], src)
            (A if c == 0 else Ar).activation(self.xq[:, 512*c:512*(c+1)], src, AF.Square)
        ctx.wait(nc.tensor, "dve")
        ctx.wait(nc.tensor, "act")
        for c in range(NC):
            nc.tensor.matmul(self.ps_s, C["ones128_bf"][:], self.xb[:, 512*c:512*(c+1)],
                             start=(c == 0), stop=(c == NC - 1))
            i = nc.tensor.matmul(self.ps_ss, C["ones128_bf"][:], self.xq[:, 512*c:512*(c+1)],
                                 start=(c == 0), stop=(c == NC - 1))
        pev = ctx.inc("pe", i)
        ctx.wait(nc.vector, "pe", pev)
        V.tensor_scalar_mul(R(0), self.ps_s, 1.0 / denom)
        V.tensor_scalar_mul(R(1), self.ps_ss, 1.0 / denom)
        V.tensor_mul(R(2), R(0), R(0))
        V.tensor_sub(R(1), R(1), R(2))
        V.tensor_scalar_add(R(1), R(1), eps)
        ctx.wait(nc.scalar, "dve")
        A.activation(R(2), R(1), AF.Sqrt)
        ctx.wait(nc.vector, "act")
        V.reciprocal(R(3), R(2))
        ctx.wait(nc.tensor, "dve")
        nc.tensor.matmul(self.pb_m, C["ones1_f32"][:], R(0), start=True, stop=True)
        i = nc.tensor.matmul(self.pb_r, C["ones1_f32"][:], R(3), start=True, stop=True)
        return ctx.inc("pe", i)


def emit_ln(ctx, sc, C, xsrc, xln):
    nc, V = ctx.nc, ctx.V
    with ctx.scope() as ph:
        xf = ph.sb("ln_xf", [128, 2 * NC * 512], F32)
        xo = ph.sb("ln_xo", [128, 2 * NC * 512], F32)
        st = StatsKit(ctx, ph, C)
        ldq = [nc.sync, nc.scalar]

        def ln_loads(g):
            if g >= 8:
                return
            sl = f"lnx{g % 2}"
            off = (g % 2) * NC * 512
            if g >= 2:
                dvv = ctx.val("dve")
                acv = ctx.val("act")
                ctx.wait(nc.sync, "dve", dvv)
                ctx.wait(nc.sync, "act", acv)
                ctx.wait(nc.scalar, "dve", dvv)
            for c in range(NC):
                ctx.inc(sl, ldq[c % 2].dma_start(
                    out=xf[:, off + 512*c: off + 512*(c+1)],
                    in_=xsrc[128*c:128*(c+1), 512*g:512*(g+1)]), 16)

        ln_loads(0)
        ln_loads(1)
        for g in range(8):
            sl = f"lnx{g % 2}"
            off = (g % 2) * NC * 512
            X = lambda c: xf[:, off + 512*c: off + 512*(c+1)]
            O = lambda c: xo[:, off + 512*c: off + 512*(c+1)]
            ctx.wait(nc.vector, sl)
            ctx.wait(nc.scalar, sl)
            pev = st.emit(X)
            ctx.wait(nc.vector, "pe", pev)
            ctx.wait(nc.vector, "lnst")
            for c in range(NC):
                (ctx.Vr if c else V).tensor_sub(O(c), X(c), st.pb_m)
                V.tensor_mul(O(c), O(c), st.pb_r)
            dv = ctx.val("dve")
            ln_loads(g + 2)
            ctx.wait(nc.sync, "dve", dv)
            ctx.wait(nc.scalar, "dve", dv)
            for c in range(NC):
                ctx.inc("lnst", (nc.sync if c % 2 == 0 else nc.scalar).dma_start(
                    out=xln[128*c:128*(c+1), 512*g:512*(g+1)], in_=O(c)), 16)


def emit_l1_gate(ctx, sc, C, aps, xln, comb_tm, combT,
                 wg_key="wg1", bg_key="bg1"):
    nc, V, A = ctx.nc, ctx.V, ctx.A
    with ctx.scope() as ph:
        wg = ph.sb("g_wg", [128, 8 * NC], F32)
        bgr = ph.sb("g_bg", [1, 8], F32)
        xg = ph.sb("g_xg", [128, 2 * NC * 512], F32)
        Lg = ph.sb("g_Lg", [128, 256], F32)
        tA = ph.sb("g_tA", [128, 256], F32)
        tB = ph.sb("g_tB", [128, 256], F32)
        tC = ph.sb("g_tC", [128, 256], F32)
        sm = ph.sb("g_sm", [128, 6 * 32], F32)
        pgall = ph.ps("g_pg", [128, 256])
        pt = ph.ps("g_pt", [8, 512])
        SM = lambda i: sm[:, 32*i:32*(i+1)]
        SMb = lambda i: sm[:, 32*i:32*(i+1)].rearrange("p (t o) -> p t o", o=1).broadcast_to([128, 32, 8])
        E3 = lambda ap: ap.rearrange("p (t e) -> p t e", e=8)
        ctx.inc("qs", nc.sync.dma_start(out=wg[:], in_=aps[wg_key]), 16)
        ctx.inc("qs", nc.sync.dma_start(out=bgr[:], in_=aps[bg_key]), 16)
        gpe = {}
        for g in range(8):
            sl = f"gx{g % 2}"
            off = (g % 2) * NC * 512
            X = lambda c: xg[:, off + 512*c: off + 512*(c+1)]
            if g >= 2:
                ctx.wait(nc.sync, "pe", gpe[g - 2])
            for c in range(NC):
                ctx.inc(sl, nc.sync.dma_start(
                    out=X(c), in_=xln[128*c:128*(c+1), 512*g:512*(g+1)]), 16)
            ctx.wait(nc.tensor, sl)
            ctx.wait(nc.tensor, "qs")
            for j4 in range(4):
                j = 4*g + j4
                for c in range(NC):
                    nc.tensor.matmul(pgall[:, 8*j:8*(j+1)],
                                     xg[:, off + 512*c + 128*j4: off + 512*c + 128*j4 + 128],
                                     wg[:, 8*c:8*(c+1)], start=(c == 0), stop=False)
                i = nc.tensor.matmul(pgall[:, 8*j:8*(j+1)], C["ones1_f32"][:], bgr[:],
                                     start=False, stop=True)
            gpe[g] = ctx.inc("pe", i)
        ctx.wait(nc.vector, "pe", gpe[7])
        V.tensor_copy(Lg[:], pgall[:])
        V.tensor_reduce(SM(0), E3(Lg[:]), AX.X, ALU.max)
        V.tensor_tensor(E3(tA[:]), E3(Lg[:]), SMb(0), ALU.is_ge)
        V.scalar_tensor_tensor(tB[:], tA[:], -1000.0, C["iotaT"][:], ALU.mult, ALU.add)
        V.tensor_reduce(SM(1), E3(tB[:]), AX.X, ALU.min)
        V.tensor_tensor(E3(tA[:]), E3(C["iotam1000T"][:]), SMb(1), ALU.is_equal)
        V.scalar_tensor_tensor(tB[:], tA[:], NEG, Lg[:], ALU.mult, ALU.add)
        V.tensor_reduce(SM(2), E3(tB[:]), AX.X, ALU.max)
        V.tensor_tensor(E3(tC[:]), E3(tB[:]), SMb(2), ALU.is_ge)
        V.scalar_tensor_tensor(tC[:], tC[:], -1000.0, C["iotaT"][:], ALU.mult, ALU.add)
        V.tensor_reduce(SM(3), E3(tC[:]), AX.X, ALU.min)
        V.tensor_tensor(E3(tC[:]), E3(C["iotam1000T"][:]), SMb(3), ALU.is_equal)
        V.tensor_sub(SM(4), SM(2), SM(0))
        ctx.wait(nc.scalar, "dve")
        A.activation(SM(5), SM(4), AF.Sigmoid)
        ctx.wait(nc.vector, "act")
        V.tensor_tensor(E3(tA[:]), E3(tA[:]), SMb(5), ALU.mult)
        V.tensor_tensor(E3(tC[:]), E3(tC[:]), SMb(5), ALU.mult)
        V.tensor_sub(tC[:], tC[:], tA[:])
        V.tensor_tensor(E3(tA[:]), E3(C["iotam1000T"][:]), SMb(1), ALU.is_equal)
        V.tensor_add(comb_tm[:], tC[:], tA[:])
        ctx.wait(nc.tensor, "dve")
        for j4 in range(NT // 4):
            if j4 >= 1:
                ctx.wait(nc.tensor, "dve", ctx.val("dve"))
            for u in range(4):
                j = 4*j4 + u
                i = nc.tensor.transpose(pt[:, 128*u:128*(u+1)], comb_tm[:, 8*j:8*(j+1)],
                                        C["ident_f32"][:])
            pev = ctx.inc("pe", i)
            ctx.wait(nc.vector, "pe", pev)
            for u in range(4):
                j = 4*j4 + u
                (V if u == 0 else ctx.Vr).tensor_copy(combT[:, 128*j:128*(j+1)],
                                                      pt[:, 128*u:128*(u+1)])


def emit_l1_experts(ctx, sc, C, aps, xln, comb_tm, combT, xsrc, xdst,
                    ntset=8, nexp=EXP, do_epi=1, do_b2=1, do_B=1,
                    w1src=None, w2src=None):
    nc, V, A = ctx.nc, ctx.V, ctx.A
    with ctx.scope() as ph:
        b1 = ph.sb("e1_b1", [128, EXP * 32], F32)
        b2 = ph.sb("e1_b2", [8, 1024], F32)
        gam = ph.sb("e1_gam", [128, 8], F32)
        bet = ph.sb("e1_bet", [128, 8], F32)
        xln_ts = ph.sb("e1_xts", [128, 2 * NC * 512], BF16)
        xres = ph.sb("e1_xres", [128, 2 * NC * 512], F32)
        preT = ph.sb("e1_preT", [128, 2 * NC * 512], F32)
        Hgel = ph.sb("e1_H", [128, 32 * 512], BF16)
        w1b = ph.sb("e1_w1", [128, 3 * 1024], BF16)
        w2b = ph.sb("e1_w2", [128, 3 * 4096], BF16)
        out_acc = ph.sb("e1_oa", [128, 4 * 1024], F32)
        py = [ph.ps(f"e1_py{j}", [128, 512]) for j in range(8)]
        st = StatsKit(ctx, ph, C, banks=(py[4][0:1, :], py[5][0:1, :],
                                         py[6][:], py[7][:]))
        ctx.inc("qs", nc.sync.dma_start(out=b1[:], in_=aps["b1l1"]), 16)
        ctx.inc("qs", nc.sync.dma_start(out=b2[:], in_=aps["b2l1"]), 16)
        ctx.inc("qs", nc.sync.dma_start(out=gam[:], in_=aps["gam1"]), 16)
        ctx.inc("qs", nc.sync.dma_start(out=bet[:], in_=aps["bet1"]), 16)
        resident_qs = ctx.val("qs")
        ctx.wait(nc.tensor, "qs", resident_qs)
        ctx.wait(nc.scalar, "qs", resident_qs)
        ctx.wait(nc.vector, "qs", resident_qs)

        w1ap = aps["w1l1"] if w1src is None else w1src
        w2ap = aps["w2l1"] if w2src is None else w2src
        w1eng = nc.gpsimd if w1src is None else nc.scalar
        w2eng = nc.gpsimd if w2src is None else nc.sync

        def w1_load(e, hc, guard_pe):
            sl = f"w1{hc % 3}"
            ctx.wait(w1eng, "pe", guard_pe)
            ctx.inc(sl, w1eng.dma_start(
                out=w1b[:, 1024*(hc % 3):1024*(hc % 3 + 1)],
                in_=w1ap[e, hc]), 16)
            return (sl, ctx.val(sl))

        def w2_load(e, s, guard_pe):
            sl = f"w2{s % 3}"
            dg, hg = divmod(s, 4)
            ctx.wait(w2eng, "pe", guard_pe)
            src = w2ap[e, hg, :, :, 512*dg:512*(dg+1)].rearrange("h k d -> k h d")
            o = 4096 * (s % 3)
            dst = w2b[:, o:o+4096].rearrange("k (h d) -> k h d", h=8)
            ctx.inc(sl, w2eng.dma_start(out=dst, in_=src), 16)
            return (sl, ctx.val(sl))

        def tset_loads(t):
            if t >= ntset:
                return (("qs", 0), ("qs", 0))
            to = 512 * t
            xo = (t % 2) * NC * 512
            ctx.wait(nc.gpsimd, "pe", ctx.val("pe"))
            ctx.wait(nc.sync, "dve", ctx.val("dve"))
            ctx.wait(nc.sync, "act", ctx.val("act"))
            ctx.wait(nc.sync, "qs", ctx.val("qs"))
            xs, ps = f"e1x{t % 2}", f"e1p{t % 2}"
            for c in range(NC):
                ctx.inc(xs, nc.gpsimd.dma_start(
                    out=xln_ts[:, xo + 512*c: xo + 512*(c+1)],
                    in_=xln[128*c:128*(c+1), to:to+512]), 16)
                ctx.inc(ps, nc.sync.dma_start(
                    out=preT[:, xo + 512*c: xo + 512*(c+1)],
                    in_=xln[128*c:128*(c+1), to:to+512]), 16)
                ctx.inc(ps, nc.sync.dma_start(
                    out=xres[:, xo + 512*c: xo + 512*(c+1)],
                    in_=xsrc[128*c:128*(c+1), to:to+512]), 16)
            return ((xs, ctx.val(xs)), (ps, ctx.val(ps)))

        for eng in (nc.sync, nc.gpsimd):
            for sname in ("dve", "act", "pe", "qs"):
                if sname in ctx.sems and ctx.val(sname):
                    ctx.wait(eng, sname, ctx.val(sname))
        ld_tok = {0: tset_loads(0)}
        for t in range(ntset):
            to = 512 * t
            xo = (t % 2) * NC * 512
            X = lambda c: xln_ts[:, xo + 512*c: xo + 512*(c+1)]
            PRE = lambda c: preT[:, xo + 512*c: xo + 512*(c+1)]
            XRES = lambda c: xres[:, xo + 512*c: xo + 512*(c+1)]
            ctx.wait(nc.tensor, "dve", ctx.val("dve"))
            if do_b2:
                for j8 in range(8):
                    j, dg = divmod(j8, 2)
                    jj = 4 * t + j
                    i = nc.tensor.matmul(py[j8][:], combT[:, 128*jj:128*(jj+1)],
                                         b2[:, 512*dg:512*(dg+1)], start=True, stop=True)
                pev = ctx.inc("pe", i)
                ctx.wait(nc.vector, "pe", pev)
                for j8 in range(8):
                    j, dg = divmod(j8, 2)
                    (V if j8 == 0 else ctx.Vr).tensor_copy(
                        out_acc[:, 1024*j + 512*dg: 1024*j + 512*(dg+1)], py[j8][:])
            ctx.wait(nc.tensor, "dve", ctx.val("dve"))
            ctx.wait(nc.tensor, *ld_tok[t][0])
            for e in range(nexp):
                pe_base = ctx.val("pe")
                qg_after = {0: w1_load(e, 0, pe_base), 1: w1_load(e, 1, pe_base),
                            2: w1_load(e, 2, pe_base)}
                act_after = {}
                pe_A = {}
                for hc in range(32):
                    pb = hc % 4
                    ctx.wait(nc.tensor, *qg_after[hc])
                    if hc >= 4:
                        ctx.wait(nc.tensor, "act", act_after[hc - 4])
                    for kc in range(NC):
                        i = nc.tensor.matmul(py[pb][:],
                                             w1b[:, 1024*(hc % 3) + 128*kc: 1024*(hc % 3) + 128*(kc+1)],
                                             X(kc), start=(kc == 0), stop=(kc == NC - 1))
                    pe_A[hc] = pev = ctx.inc("pe", i)
                    ctx.wait(nc.scalar, "pe", pev)
                    (A if hc == 0 else ctx.Ar).activation(
                        Hgel[:, 512*hc:512*(hc+1)], py[pb][:], AF.Gelu,
                        bias=b1[:, 32*e+hc: 32*e+hc+1])
                    act_after[hc] = ctx.val("act")
                    if hc + 3 < 32:
                        qg_after[hc + 3] = w1_load(e, hc + 3, pe_A[hc])
                ctx.wait(nc.tensor, "act", ctx.val("act"))
                if not do_B:
                    continue
                pe_base = ctx.val("pe")
                qg_after = {0: w2_load(e, 0, pe_base), 1: w2_load(e, 1, pe_base),
                            2: w2_load(e, 2, pe_base)}
                pe_B = {}
                for s in range(8):
                    dg, hg = divmod(s, 4)
                    wb = s % 3
                    ctx.wait(nc.tensor, *qg_after[s])
                    for h8 in range(8):
                        hc = 8 * hg + h8
                        for j in range(4):
                            i = nc.tensor.matmul(
                                py[4*dg + j][:],
                                Hgel[:, 512*hc + 128*j: 512*hc + 128*(j+1)],
                                w2b[:, 4096*wb + 512*h8: 4096*wb + 512*(h8+1)],
                                start=(hc == 0), stop=(hc == 31))
                    pe_B[s] = pev = ctx.inc("pe", i)
                    if s + 3 < 8:
                        qg_after[s + 3] = w2_load(e, s + 3, pe_B[s])
                    if hg == 3:
                        ctx.wait(nc.vector, "pe", pev)
                        for j in range(4):
                            jj = 4 * t + j
                            sl = out_acc[:, 1024*j + 512*dg: 1024*j + 512*(dg+1)]
                            (V if j == 0 else ctx.Vr).scalar_tensor_tensor(
                                sl, py[4*dg + j][:], comb_tm[:, 8*jj+e: 8*jj+e+1],
                                sl, ALU.mult, ALU.add)
                ctx.wait(nc.tensor, "dve", ctx.val("dve"))
                if e == 0:
                    ld_tok[t + 1] = tset_loads(t + 1)
            if not do_epi:
                continue
            ctx.wait(nc.vector, *ld_tok[t][1])
            grp_dve = {}
            for grp in range(8):
                tiles = [(j, dc) for j in range(4) for dc in range(NC)][4*grp:4*grp+4]
                if grp >= 1:
                    ctx.wait(nc.tensor, "dve", grp_dve.get(grp - 1, 0))
                for b4, (j, dc) in enumerate(tiles):
                    i = nc.tensor.transpose(
                        py[b4][:, 0:128],
                        out_acc[:, 1024*j + 128*dc: 1024*j + 128*(dc+1)],
                        C["ident_f32"][:])
                pev = ctx.inc("pe", i)
                ctx.wait(nc.vector, "pe", pev)
                for b4, (j, dc) in enumerate(tiles):
                    dst = preT[:, xo + 512*dc + 128*j: xo + 512*dc + 128*(j+1)]
                    (V if b4 == 0 else ctx.Vr).tensor_add(dst, py[b4][:, 0:128], dst)
                grp_dve[grp] = ctx.val("dve")
            ctx.wait(nc.scalar, "dve")
            ctx.wait(nc.tensor, "dve")
            pev = st.emit(lambda c: PRE(c))
            ctx.wait(nc.vector, "pe", pev)
            for c in range(NC):
                PT = PRE(c)
                V.tensor_sub(PT, PT, st.pb_m)
                V.tensor_mul(PT, PT, st.pb_r)
                V.tensor_scalar(PT, PT, gam[:, c:c+1], bet[:, c:c+1], ALU.mult, ALU.add)
                XR = XRES(c)
                V.scalar_tensor_tensor(XR, PT, 0.5, XR, ALU.mult, ALU.add)
                dv = ctx.val("dve")
                ctx.wait(nc.sync, "dve", dv)
                ctx.inc("qs", nc.sync.dma_start(
                    out=xdst[128*c:128*(c+1), to:to+512], in_=XR), 16)


def emit_l1_attn(ctx, sc, C, aps, xln, xsrc, xdst, npair=8, nqg=8):
    nc, V, A = ctx.nc, ctx.V, ctx.A
    with ctx.scope() as ph:
        bqkv = ph.sb("a1_bqkv", [128, 24], F32)
        bout = ph.sb("a1_bout", [128, 8], F32)
        xbf = ph.sb("a1_xbf", [128, 2 * NC * 512], BF16)
        wq = ph.sb("a1_wq", [128, 2 * 24 * 128], BF16)
        qT2 = ph.sb("a1_q", [128, S], BF16)
        kT2 = ph.sb("a1_k", [128, S], BF16)
        vT2 = ph.sb("a1_v", [128, S], BF16)
        vtok = [ph.sb(f"a1_vt{i}", [128, 32 * 65], BF16) for i in range(2)]
        pT = ph.sb("a1_pT", [128, 4 * 1024], BF16)
        pvs = ph.sb("a1_pvs", [128, 512], F32)
        rrow = ph.sb("a1_rr", [1, 512], F32)
        att = [ph.sb(f"a1_att{i}", [128, S], BF16) for i in range(8)]
        xres = ph.sb("a1_xres", [128, 2 * NC * 512], F32)
        ps_s = [ph.ps(f"a1_s{i}", [128, 1024]) for i in range(2)]
        ps_pv = ph.ps("a1_pv", [65, 512])
        ps_pb = ph.ps("a1_pb", [64, 512])
        ps_vt = ph.ps("a1_vtp", [128, 128], BF16)
        ps_qkv = ph.ps("a1_qkv", [128, 512])
        woutb = ph.sb("a1_wo", [128, 64 * 128], BF16)
        ctx.inc("qs", nc.sync.dma_start(out=bqkv[:], in_=aps["bqkv"]), 16)
        ctx.inc("qs", nc.sync.dma_start(out=bout[:], in_=aps["bout"]), 16)
        ctx.inc("qg", nc.gpsimd.dma_start(
            out=woutb[:].rearrange("k (m c f) -> k m c f", m=8, c=8),
            in_=aps["wout"].rearrange("m c k f -> k m c f")), 16)
        ctx.wait(nc.vector, "qs", ctx.val("qs"))
        V.memset(vtok[0][:], 1.0)
        V.memset(vtok[1][:], 1.0)
        bank_dve = {}
        for p in range(npair):
            wsl = f"a1w{p % 2}"
            ctx.wait(nc.gpsimd, "pe", ctx.val("pe"))
            ctx.inc(wsl, nc.gpsimd.dma_start(
                out=wq[:, 3072*(p % 2):3072*(p % 2 + 1)].rearrange("k (o c m) -> k o c m", o=3, c=8),
                in_=aps["wqkv"][3*p:3*p+3].rearrange("o c k m -> k o c m")), 16)
            ctx.wait(nc.tensor, wsl)
            for g in range(8):
                xsl = f"a1x{g % 2}"
                xo = (g % 2) * NC * 512
                if g >= 2:
                    ctx.wait(nc.gpsimd, "pe", ctx.val("pe"))
                for c in range(NC):
                    ctx.inc(xsl, nc.gpsimd.dma_start(
                        out=xbf[:, xo + 512*c: xo + 512*(c+1)],
                        in_=xln[128*c:128*(c+1), 512*g:512*(g+1)]), 16)
                ctx.wait(nc.tensor, xsl)
                for o3 in range(3):
                    oc = 3*p + o3
                    if "qkv" in bank_dve:
                        ctx.wait(nc.tensor, "dve", bank_dve["qkv"])
                    for dk in range(NC):
                        i = nc.tensor.matmul(
                            ps_qkv[:],
                            wq[:, 3072*(p % 2) + 1024*o3 + 128*dk: 3072*(p % 2) + 1024*o3 + 128*(dk+1)],
                            xbf[:, xo + 512*dk: xo + 512*(dk+1)],
                            start=(dk == 0), stop=(dk == NC - 1))
                    pev = ctx.inc("pe", i)
                    ctx.wait(nc.vector, "pe", pev)
                    for seg in range(2):
                        f0 = 128*oc + 64*seg
                        h = f0 // 192
                        j = f0 % 192
                        kind = j // 64
                        dstbuf = (qT2, kT2, vT2)[kind]
                        r0 = 64 * (h % 2)
                        V.tensor_scalar(dstbuf[r0:r0+64, 512*g:512*(g+1)],
                                        ps_qkv[64*seg:64*seg+64, :],
                                        bqkv[64*seg:64*seg+64, oc:oc+1], None, ALU.add)
                    bank_dve["qkv"] = ctx.val("dve")
            for hh in range(2):
                for ti in range(NT):
                    if "vt" in bank_dve:
                        ctx.wait(nc.tensor, "dve", bank_dve["vt"])
                    i = nc.tensor.transpose(ps_vt[:, 0:64],
                                            vT2[64*hh:64*hh+64, 128*ti:128*(ti+1)],
                                            C["ident_bf"][64*hh:64*hh+64, 64*hh:64*hh+64])
                    pev = ctx.inc("pe", i)
                    ctx.wait(nc.vector, "pe", pev)
                    V.tensor_copy(vtok[hh][:, 65*ti:65*ti+64], ps_vt[:, 0:64])
                    bank_dve["vt"] = ctx.val("dve")
            for hh in range(2):
                r0 = 64 * hh
                for qg in range(nqg):
                    act_exp = {}
                    pv_pe = {}

                    def emit_S(kp):
                        sb = kp % 2
                        pb = kp % 4
                        if kp >= 2:
                            ctx.wait(nc.tensor, "act", act_exp[kp - 2])
                        if kp >= 4:
                            ctx.wait(nc.scalar, "pe", pv_pe[kp - 4])
                        for u in range(2):
                            kt = 2*kp + u
                            i = nc.tensor.matmul(ps_s[sb][:, 512*u:512*(u+1)],
                                                 kT2[r0:r0+64, 128*kt:128*(kt+1)],
                                                 qT2[r0:r0+64, 512*qg:512*(qg+1)],
                                                 start=True, stop=True)
                        pev = ctx.inc("pe", i)
                        ctx.wait(nc.scalar, "pe", pev)
                        A.activation(pT[:, 1024*pb:1024*(pb+1)], ps_s[sb][:],
                                     AF.Exp, scale=0.125)
                        act_exp[kp] = ctx.val("act")

                    def emit_PV(kp):
                        pb = kp % 4
                        ctx.wait(nc.tensor, "act", act_exp[kp])
                        for u in range(2):
                            kt = 2*kp + u
                            i = nc.tensor.matmul(ps_pv[:], vtok[hh][:, 65*kt:65*(kt+1)],
                                                 pT[:, 1024*pb + 512*u: 1024*pb + 512*(u+1)],
                                                 start=(kt == 0), stop=(kt == NT - 1))
                        pv_pe[kp] = ctx.inc("pe", i)
                        return pv_pe[kp]

                    emit_S(0)
                    for kp in range(1, NT // 2):
                        emit_S(kp)
                        emit_PV(kp - 1)
                    pev = emit_PV(NT // 2 - 1)
                    ctx.wait(nc.vector, "pe", pev)
                    V.reciprocal(rrow[:], ps_pv[64:65, :])
                    dv = ctx.val("dve")
                    V.tensor_copy(pvs[0:64, :], ps_pv[0:64, :])
                    ctx.wait(nc.tensor, "dve", dv)
                    i = nc.tensor.matmul(ps_pb[:], C["ones1_f32"][:, 0:64], rrow[:],
                                         start=True, stop=True)
                    pev = ctx.inc("pe", i)
                    ctx.wait(nc.vector, "pe", pev)
                    V.tensor_mul(att[p][r0:r0+64, 512*qg:512*(qg+1)], pvs[0:64, :], ps_pb[:])
                    ctx.wait(nc.tensor, "dve", ctx.val("dve"))
        ctx.barrier()
        for g in range(8):
            xo = (g % 2) * NC * 512
            if g >= 2:
                ctx.wait(nc.sync, "dve", ctx.val("dve"))
                ctx.wait(nc.sync, "qs", ctx.val("qs"))
            for c in range(NC):
                ctx.inc("qs", nc.sync.dma_start(
                    out=xres[:, xo + 512*c: xo + 512*(c+1)],
                    in_=xsrc[128*c:128*(c+1), 512*g:512*(g+1)]), 16)
            ctx.wait(nc.vector, "qs")
            if g == 0:
                ctx.wait(nc.tensor, "qg", ctx.val("qg"))
            for dm in range(NC):
                ctx.wait(nc.tensor, "dve", ctx.val("dve"))
                for hd in range(npair):
                    i = nc.tensor.matmul(ps_qkv[:],
                                         woutb[:, 1024*dm + 128*hd: 1024*dm + 128*(hd+1)],
                                         att[hd][:, 512*g:512*(g+1)],
                                         start=(hd == 0), stop=(hd == npair - 1))
                pev = ctx.inc("pe", i)
                ctx.wait(nc.vector, "pe", pev)
                XR = xres[:, xo + 512*dm: xo + 512*(dm+1)]
                V.tensor_scalar(pvs[:], ps_qkv[:], bout[:, dm:dm+1], None, ALU.add)
                V.tensor_add(XR, XR, pvs[:])
                dv = ctx.val("dve")
                ctx.wait(nc.sync, "dve", dv)
                ctx.inc("qs", nc.sync.dma_start(
                    out=xdst[128*dm:128*(dm+1), 512*g:512*(g+1)], in_=XR), 16)


def emit_l0_sel(ctx, sc, C, aps, xln, cc_in, cc_out, sel05_tm, sel05T, ncores=8,
                Kcap=2048):
    nc, V, A = ctx.nc, ctx.V, ctx.A
    with ctx.scope() as ph:
        wg = ph.sb("s0_wg", [128, 8 * NC], F32)
        bgr = ph.sb("s0_bg", [1, 8], F32)
        xg = ph.sb("s0_xg", [128, 2 * NC * 512], F32)
        Lg = ph.sb("s0_Lg", [128, 256], F32)
        tA = ph.sb("s0_tA", [128, 256], F32)
        tB = ph.sb("s0_tB", [128, 256], F32)
        tC = ph.sb("s0_tC", [128, 256], F32)
        sm = ph.sb("s0_sm", [128, 6 * 32], F32)
        mT = ph.sb("s0_mT", [8, S], F32)
        routeT = ph.sb("s0_rt", [8, S], F32)
        scr = ph.sb("s0_scr", [8, S], F32)
        tieT = ph.sb("s0_tie", [8, S], F32)
        ones8 = ph.sb("s0_on", [8, S], F32)
        bis = ph.sb("s0_bis", [8, 10], F32)
        pgall = ph.ps("s0_pg", [128, 256])
        pt = ph.ps("s0_pt", [8, 512])
        ptB = ph.ps("s0_ptB", [128, 32])
        SM = lambda i: sm[:, 32*i:32*(i+1)]
        SMb = lambda i: sm[:, 32*i:32*(i+1)].rearrange("p (t o) -> p t o", o=1).broadcast_to([128, 32, 8])
        E3 = lambda ap: ap.rearrange("p (t e) -> p t e", e=8)
        B1 = lambda i: bis[:, i:i+1]
        ctx.inc("qs", nc.sync.dma_start(out=wg[:], in_=aps["wg0"]), 16)
        ctx.inc("qs", nc.sync.dma_start(out=bgr[:], in_=aps["bg0"]), 16)
        gpe = {}
        for g in range(8):
            sl = f"s0x{g % 2}"
            off = (g % 2) * NC * 512
            X = lambda c: xg[:, off + 512*c: off + 512*(c+1)]
            if g >= 2:
                ctx.wait(nc.sync, "pe", gpe[g - 2])
            for c in range(NC):
                ctx.inc(sl, nc.sync.dma_start(
                    out=X(c), in_=xln[128*c:128*(c+1), 512*g:512*(g+1)]), 16)
            ctx.wait(nc.tensor, sl)
            ctx.wait(nc.tensor, "qs")
            for j4 in range(4):
                j = 4*g + j4
                for c in range(NC):
                    nc.tensor.matmul(pgall[:, 8*j:8*(j+1)],
                                     xg[:, off + 512*c + 128*j4: off + 512*c + 128*j4 + 128],
                                     wg[:, 8*c:8*(c+1)], start=(c == 0), stop=False)
                i = nc.tensor.matmul(pgall[:, 8*j:8*(j+1)], C["ones1_f32"][:], bgr[:],
                                     start=False, stop=True)
            gpe[g] = ctx.inc("pe", i)
        ctx.wait(nc.vector, "pe", gpe[7])
        V.tensor_copy(Lg[:], pgall[:])
        V.tensor_reduce(SM(0), E3(Lg[:]), AX.X, ALU.max)
        V.tensor_tensor(E3(tA[:]), E3(Lg[:]), SMb(0), ALU.subtract)
        ctx.wait(nc.scalar, "dve")
        A.activation(tA[:], tA[:], AF.Exp)
        ctx.wait(nc.vector, "act")
        V.tensor_reduce(SM(1), E3(tA[:]), AX.X, ALU.add)
        V.reciprocal(SM(2), SM(1))
        V.tensor_tensor(E3(tA[:]), E3(tA[:]), SMb(2), ALU.mult)
        V.tensor_tensor(E3(tB[:]), E3(Lg[:]), SMb(0), ALU.is_ge)
        V.scalar_tensor_tensor(tB[:], tB[:], -1000.0, C["iotaT"][:], ALU.mult, ALU.add)
        V.tensor_reduce(SM(3), E3(tB[:]), AX.X, ALU.min)
        V.tensor_tensor(E3(tB[:]), E3(C["iotam1000T"][:]), SMb(3), ALU.is_equal)
        V.scalar_tensor_tensor(tC[:], tB[:], NEG, Lg[:], ALU.mult, ALU.add)
        V.tensor_reduce(SM(4), E3(tC[:]), AX.X, ALU.max)
        V.tensor_tensor(E3(tC[:]), E3(tC[:]), SMb(4), ALU.is_ge)
        V.scalar_tensor_tensor(tC[:], tC[:], -1000.0, C["iotaT"][:], ALU.mult, ALU.add)
        V.tensor_reduce(SM(5), E3(tC[:]), AX.X, ALU.min)
        V.tensor_tensor(E3(tC[:]), E3(C["iotam1000T"][:]), SMb(5), ALU.is_equal)
        V.tensor_add(tB[:], tB[:], tC[:])
        V.tensor_mul(tA[:], tA[:], tB[:])
        ctx.wait(nc.tensor, "dve")
        for j4 in range(NT // 4):
            if j4 >= 1:
                ctx.wait(nc.tensor, "dve", ctx.val("dve"))
            for u in range(4):
                j = 4*j4 + u
                i = nc.tensor.transpose(pt[:, 128*u:128*(u+1)], tA[:, 8*j:8*(j+1)],
                                        C["ident_f32"][:])
            pev = ctx.inc("pe", i)
            ctx.wait(nc.vector, "pe", pev)
            for u in range(4):
                j = 4*j4 + u
                (V if u == 0 else ctx.Vr).tensor_copy(mT[:, 128*j:128*(j+1)],
                                                      pt[:, 128*u:128*(u+1)])
        ctx.wait(nc.sync, "dve")
        ctx.inc("qs", nc.sync.dma_start(out=cc_in, in_=mT[:]), 16)
        ctx.wait(nc.gpsimd, "qs")
        ctx.inc("cc", nc.gpsimd.collective_compute(
            "AllReduce", ALU.add, [list(range(ncores))],
            [cc_in.opt()], [cc_out.opt()]), 1)
        ctx.wait(nc.sync, "cc")
        ctx.inc("qs", nc.sync.dma_start(out=scr[:], in_=cc_out), 16)
        ctx.wait(nc.vector, "qs")
        V.tensor_scalar_add(scr[:], scr[:], 1e-6)
        V.reciprocal(scr[:], scr[:])
        V.tensor_mul(routeT[:], mT[:], scr[:])
        V.tensor_scalar_mul(routeT[:], routeT[:], float(ncores))
        V.memset(bis[:], 0.0)
        V.memset(bis[:, 1:2], 8.5)
        for it in range(34):
            V.tensor_add(B1(2), B1(0), B1(1))
            V.tensor_scalar_mul(B1(2), B1(2), 0.5)
            V.tensor_scalar(scr[:], routeT[:], B1(2), None, ALU.is_gt,
                            op1=ALU.add, accum_out=B1(3))
            V.tensor_scalar(B1(4), B1(3), float(Kcap), None, ALU.is_ge)
            V.tensor_sub(B1(5), B1(2), B1(0))
            V.scalar_tensor_tensor(B1(0), B1(5), B1(4), B1(0), ALU.mult, ALU.add)
            V.tensor_sub(B1(5), B1(1), B1(2))
            V.scalar_tensor_tensor(B1(1), B1(5), B1(4), B1(2), ALU.mult, ALU.add)
        V.tensor_scalar(scr[:], routeT[:], B1(1), None, ALU.is_le)
        V.tensor_mul(scr[:], scr[:], routeT[:])
        V.tensor_reduce(B1(6), scr[:], AX.X, ALU.max)
        V.tensor_scalar(scr[:], routeT[:], B1(6), None, ALU.is_gt,
                        op1=ALU.add, accum_out=B1(7))
        V.tensor_scalar(B1(7), B1(7), -1.0, float(Kcap), ALU.mult, ALU.add)
        V.tensor_scalar(tieT[:], routeT[:], B1(6), None, ALU.is_equal)
        V.memset(ones8[:], 1.0)
        V.tensor_tensor_scan(routeT[:], ones8[:], tieT[:], 0.0, ALU.mult, ALU.add)
        V.tensor_scalar(routeT[:], routeT[:], B1(7), None, ALU.is_le)
        V.tensor_mul(tieT[:], tieT[:], routeT[:])
        V.tensor_add(sel05T[:], scr[:], tieT[:])
        V.tensor_scalar_mul(sel05T[:], sel05T[:], 0.5)
        ctx.wait(nc.tensor, "dve")
        for j4 in range(NT // 4):
            if j4 >= 1:
                ctx.wait(nc.tensor, "dve", ctx.val("dve"))
            for u in range(4):
                j = 4*j4 + u
                i = nc.tensor.transpose(ptB[:, 8*u:8*(u+1)], sel05T[:, 128*j:128*(j+1)],
                                        C["ident_f32"][0:8, 0:8])
            pev = ctx.inc("pe", i)
            ctx.wait(nc.vector, "pe", pev)
            for u in range(4):
                j = 4*j4 + u
                (V if u == 0 else ctx.Vr).tensor_copy(sel05_tm[:, 8*j:8*(j+1)],
                                                      ptB[:, 8*u:8*(u+1)])


def emit_l0_experts(ctx, sc, C, aps, xln, sel05_tm, sel05T, xsrc, xdst,
                    ntset=8, nexp=EXP):
    nc, V, A = ctx.nc, ctx.V, ctx.A
    with ctx.scope() as ph:
        b1 = ph.sb("e0_b1", [128, EXP * 2], F32)
        b2 = ph.sb("e0_b2", [8, 1024], F32)
        w1r = ph.sb("e0_w1", [128, EXP * 2 * 1024], BF16)
        w2r = ph.sb("e0_w2", [128, EXP * 2 * 1024], BF16)
        xln_ts = ph.sb("e0_xts", [128, 2 * NC * 512], BF16)
        xres = ph.sb("e0_xres", [128, 2 * NC * 512], F32)
        Hgel = ph.sb("e0_H", [128, 2 * 512], BF16)
        out_acc = ph.sb("e0_oa", [128, 4 * 1024], F32)
        py = [ph.ps(f"e0_py{j}", [128, 512]) for j in range(8)]
        ctx.inc("qs", nc.sync.dma_start(out=b1[:], in_=aps["b1l0"]), 16)
        ctx.inc("qs", nc.sync.dma_start(out=b2[:], in_=aps["b2l0"]), 16)
        ctx.inc("qg", nc.gpsimd.dma_start(out=w1r[:], in_=aps["w1l0r"]), 16)
        ctx.inc("qg", nc.gpsimd.dma_start(out=w2r[:], in_=aps["w2l0r"]), 16)
        ctx.wait(nc.tensor, "qg", ctx.val("qg"))
        def tset_loads0(t):
            if t >= ntset:
                return (("qs", 0), ("qs", 0))
            to = 512 * t
            xo = (t % 2) * NC * 512
            ctx.wait(nc.gpsimd, "pe", ctx.val("pe"))
            ctx.wait(nc.sync, "dve", ctx.val("dve"))
            ctx.wait(nc.sync, "qs", ctx.val("qs"))
            xs, ps = f"e0x{t % 2}", f"e0p{t % 2}"
            for c in range(NC):
                ctx.inc(xs, nc.gpsimd.dma_start(
                    out=xln_ts[:, xo + 512*c: xo + 512*(c+1)],
                    in_=xln[128*c:128*(c+1), to:to+512]), 16)
                ctx.inc(ps, nc.sync.dma_start(
                    out=xres[:, xo + 512*c: xo + 512*(c+1)],
                    in_=xsrc[128*c:128*(c+1), to:to+512]), 16)
            return ((xs, ctx.val(xs)), (ps, ctx.val(ps)))

        for eng in (nc.sync, nc.gpsimd):
            for sname in ("dve", "act", "pe", "qs"):
                if sname in ctx.sems and ctx.val(sname):
                    ctx.wait(eng, sname, ctx.val(sname))
        ld_tok = {0: tset_loads0(0)}
        for t in range(ntset):
            to = 512 * t
            xo = (t % 2) * NC * 512
            X0 = lambda c: xln_ts[:, xo + 512*c: xo + 512*(c+1)]
            XRES0 = lambda c: xres[:, xo + 512*c: xo + 512*(c+1)]
            ctx.wait(nc.tensor, "dve", ctx.val("dve"))
            if t == 0:
                ctx.wait(nc.tensor, "qs", ctx.val("qs"))
            for j8 in range(8):
                j, dg = divmod(j8, 2)
                jj = 4 * t + j
                i = nc.tensor.matmul(py[j8][:], sel05T[:, 128*jj:128*(jj+1)],
                                     b2[:, 512*dg:512*(dg+1)], start=True, stop=True)
            pev = ctx.inc("pe", i)
            ctx.wait(nc.vector, "pe", pev)
            for j8 in range(8):
                j, dg = divmod(j8, 2)
                (V if j8 == 0 else ctx.Vr).tensor_copy(
                    out_acc[:, 1024*j + 512*dg: 1024*j + 512*(dg+1)], py[j8][:])
            ctx.wait(nc.tensor, "dve", ctx.val("dve"))
            ctx.wait(nc.tensor, *ld_tok[t][0])
            for e in range(nexp):
                if e == 1:
                    ld_tok[t + 1] = tset_loads0(t + 1)
                act_after = {}
                for hc in range(2):
                    if e + hc > 0:
                        ctx.wait(nc.tensor, "act", ctx.val("act") - (1 if hc == 0 else 0))
                    for kc in range(NC):
                        i = nc.tensor.matmul(
                            py[hc][:],
                            w1r[:, 2048*e + 1024*hc + 128*kc: 2048*e + 1024*hc + 128*(kc+1)],
                            X0(kc), start=(kc == 0), stop=(kc == NC - 1))
                    pev = ctx.inc("pe", i)
                    ctx.wait(nc.scalar, "pe", pev)
                    (A if hc == 0 else ctx.Ar).activation(
                        Hgel[:, 512*hc:512*(hc+1)], py[hc][:], AF.Gelu,
                        bias=b1[:, 2*e+hc: 2*e+hc+1])
                    act_after[hc] = ctx.val("act")
                ctx.wait(nc.tensor, "act", act_after[1])
                for dg in range(2):
                    for hc in range(2):
                        for j in range(4):
                            i = nc.tensor.matmul(
                                py[4*dg + j][:],
                                Hgel[:, 512*hc + 128*j: 512*hc + 128*(j+1)],
                                w2r[:, 2048*e + 1024*hc + 512*dg: 2048*e + 1024*hc + 512*(dg+1)],
                                start=(hc == 0), stop=(hc == 1))
                    pev = ctx.inc("pe", i)
                    ctx.wait(nc.vector, "pe", pev)
                    for j in range(4):
                        jj = 4 * t + j
                        sl = out_acc[:, 1024*j + 512*dg: 1024*j + 512*(dg+1)]
                        (V if j == 0 else ctx.Vr).scalar_tensor_tensor(
                            sl, py[4*dg + j][:], sel05_tm[:, 8*jj+e: 8*jj+e+1],
                            sl, ALU.mult, ALU.add)
                ctx.wait(nc.tensor, "dve", ctx.val("dve"))
            ctx.wait(nc.vector, *ld_tok[t][1])
            grp_dve = {}
            for grp in range(8):
                tiles = [(j, dc) for j in range(4) for dc in range(NC)][4*grp:4*grp+4]
                if grp >= 1:
                    ctx.wait(nc.tensor, "dve", grp_dve.get(grp - 1, 0))
                for b4, (j, dc) in enumerate(tiles):
                    i = nc.tensor.transpose(
                        py[b4][:, 0:128],
                        out_acc[:, 1024*j + 128*dc: 1024*j + 128*(dc+1)],
                        C["ident_f32"][:])
                pev = ctx.inc("pe", i)
                ctx.wait(nc.vector, "pe", pev)
                for b4, (j, dc) in enumerate(tiles):
                    dst = xres[:, xo + 512*dc + 128*j: xo + 512*dc + 128*(j+1)]
                    (V if b4 == 0 else ctx.Vr).tensor_add(dst, py[b4][:, 0:128], dst)
                grp_dve[grp] = ctx.val("dve")
            dv = ctx.val("dve")
            ctx.wait(nc.sync, "dve", dv)
            for c in range(NC):
                ctx.inc("qs", nc.sync.dma_start(
                    out=xdst[128*c:128*(c+1), to:to+512], in_=XRES0(c)), 16)


RQK = 1.0 / float(np.sqrt(96.0))
CLIP_HI = float(np.exp(80.0 / np.sqrt(96.0)))
CLIP_LO = float(np.exp(-80.0 / np.sqrt(96.0)))

def emit_l0_attn(ctx, sc, C, aps, xln, xsrc, xdst, npair=8, nqg=8):
    nc, V, A = ctx.nc, ctx.V, ctx.A
    with ctx.scope() as ph:
        C2 = load_attn_consts(ctx, ph, aps, rope=True)
        cqT = ph.sb("a0_cq", [128, 2 * S], BF16)
        ckvT = ph.sb("a0_ckv", [128, 2 * S], BF16)
        ropb = ph.sb("a0_ropb", [32, 4 * S], BF16)
        att = [ph.sb(f"a0_att{i}", [128, S], BF16) for i in range(8)]
        rrow = ph.sb("a0_rr", [1, 512], F32)
        ps_j = [ph.ps(f"a0_pj{i}", [128, 512]) for i in range(2)]
        ps_kb = ph.ps("a0_pkb", [64, 512])
        ps_k = ps_kb[0:32, :]
        ps_pb = ps_kb[:]
        ps_s = [ph.ps(f"a0_ps{i}", [128, 1024]) for i in range(2)]
        ps_pv = ph.ps("a0_pv", [65, 512])
        ropesrc = ropb[:, 0:S]
        krT = ropb[:, S:2*S]
        cosT = ropb[:, 2*S:3*S]
        sinT = ropb[:, 3*S:4*S]
        ctx.wait(nc.gpsimd, "qg", ctx.val("qg"))
        ctx.inc("qg", nc.gpsimd.dma_start(out=cosT, in_=aps["c_ropecs"][:, 0:S]), 16)
        ctx.inc("qg", nc.gpsimd.dma_start(out=sinT, in_=aps["c_ropecs"][:, S:2*S]), 16)
        ctx.wait(nc.vector, "qg", ctx.val("qg"))

        def rope_inplace(dst, rsc):
            for g in range(8):
                ctx.wait(nc.tensor, "dve", ctx.val("dve"))
                i = nc.tensor.matmul(ps_k, C2["pswap"][:],
                                     ropesrc[:, 512*g:512*(g+1)], start=True, stop=True)
                pev = ctx.inc("pe", i)
                ctx.wait(nc.vector, "pe", pev)
                V.tensor_mul(rsc[:, 512*g:512*(g+1)], ps_k[:],
                             sinT[:, 512*g:512*(g+1)])
            V.tensor_mul(ropesrc[:], ropesrc[:], cosT[:])
            V.tensor_add(dst, ropesrc[:], rsc[:])

        with ctx.scope() as p0:
            xbf = p0.sb("a0_xbf", [128, 2 * NC * 512], BF16)
            wdqr = p0.sb("a0_wdq", [128, 16 * 128], BF16)
            wdkvr = p0.sb("a0_wdkv", [128, 16 * 128], BF16)
            wkrr = p0.sb("a0_wkr", [128, 8 * 32], BF16)
            krp = ropesrc
            ctx.wait(nc.gpsimd, "qg", ctx.val("qg"))
            ctx.inc("qg", nc.gpsimd.dma_start(
                out=wdqr[:].rearrange("k (m c f) -> k m c f", m=2, c=8),
                in_=aps["wdq"].rearrange("m c k f -> k m c f")), 16)
            ctx.inc("qg", nc.gpsimd.dma_start(
                out=wdkvr[:].rearrange("k (m c f) -> k m c f", m=2, c=8),
                in_=aps["wdkv"].rearrange("m c k f -> k m c f")), 16)
            ctx.inc("qg", nc.gpsimd.dma_start(
                out=wkrr[:].rearrange("k (c f) -> k c f", c=8),
                in_=aps["wkr"].rearrange("o c k f -> k (o c) f")), 16)
            ctx.wait(nc.tensor, "qg", ctx.val("qg"))
            bank_dve = {}
            for g in range(8):
                sl = f"a0x{g % 2}"
                xo = (g % 2) * NC * 512
                if g >= 2:
                    ctx.wait(nc.gpsimd, "pe", ctx.val("pe"))
                for c in range(NC):
                    ctx.inc(sl, nc.gpsimd.dma_start(
                        out=xbf[:, xo + 512*c: xo + 512*(c+1)],
                        in_=xln[128*c:128*(c+1), 512*g:512*(g+1)]), 16)
                ctx.wait(nc.tensor, sl)
                for q4, (lc, (w, dstb)) in enumerate(
                        [(lc, wd) for lc in range(2)
                         for wd in ((wdqr, cqT), (wdkvr, ckvT))]):
                    bk = q4 % 2
                    pj = ps_j[bk]
                    if bk in bank_dve:
                        ctx.wait(nc.tensor, "dve", bank_dve[bk])
                    for kc in range(NC):
                        i = nc.tensor.matmul(
                            pj[:], w[:, (lc*8+kc)*128:(lc*8+kc)*128+128],
                            xbf[:, xo + 512*kc: xo + 512*(kc+1)],
                            start=(kc == 0), stop=(kc == NC - 1))
                    pev = ctx.inc("pe", i)
                    ctx.wait(nc.vector, "pe", pev)
                    V.tensor_copy(dstb[:, S*lc + 512*g: S*lc + 512*(g+1)], pj[:])
                    bank_dve[bk] = ctx.val("dve")
                if "k" in bank_dve:
                    ctx.wait(nc.tensor, "dve", bank_dve["k"])
                for kc in range(NC):
                    i = nc.tensor.matmul(ps_k, wkrr[:, 32*kc:32*(kc+1)],
                                         xbf[:, xo + 512*kc: xo + 512*(kc+1)],
                                         start=(kc == 0), stop=(kc == NC - 1))
                pev = ctx.inc("pe", i)
                ctx.wait(nc.vector, "pe", pev)
                V.tensor_copy(krp[:, 512*g:512*(g+1)], ps_k)
                bank_dve["k"] = ctx.val("dve")
            rope_inplace(krT, xbf[0:32, 0:S])
            ctx.wait(nc.tensor, "dve", ctx.val("dve"))
        with ctx.scope() as p1:
            wuqr = p1.sb("a0_wuq", [128, 16 * 128], BF16)
            wukr = p1.sb("a0_wuk", [128, 16 * 128], BF16)
            wqrr = p1.sb("a0_wqr", [128, 8 * 128], BF16)
            wuvr = p1.sb("a0_wuv", [128, 2 * 1024], BF16)
            qh = [p1.sb(f"a0_qh{i}", [128, S], BF16) for i in range(2)]
            kh = [p1.sb(f"a0_kh{i}", [128, S], BF16) for i in range(2)]
            vtok = [p1.sb(f"a0_vt{i}", [128, 32 * 65], BF16) for i in range(2)]
            pT = p1.sb("a0_pT", [128, 4 * 1024], BF16)
            pvs = p1.sb("a0_pvs", [128, 512], F32)
            qrp = ropesrc
            ctx.wait(nc.gpsimd, "qg", ctx.val("qg"))
            ctx.inc("qg", nc.gpsimd.dma_start(
                out=wuqr[:].rearrange("k (m c f) -> k m c f", m=8, c=2),
                in_=aps["wuq"].rearrange("m c k f -> k m c f")), 16)
            ctx.inc("qg", nc.gpsimd.dma_start(
                out=wukr[:].rearrange("k (m c f) -> k m c f", m=8, c=2),
                in_=aps["wuk"].rearrange("m c k f -> k m c f")), 16)
            ctx.inc("qg", nc.gpsimd.dma_start(
                out=wqrr[:].rearrange("k (m c f) -> k m c f", m=4, c=2),
                in_=aps["wqr"].rearrange("m c k f -> k m c f")), 16)
            ctx.inc("qg", nc.gpsimd.dma_start(out=wuvr[:], in_=aps["wuvr"]), 16)
            ctx.wait(nc.tensor, "qg", ctx.val("qg"))
            ctx.wait(nc.vector, "qg", ctx.val("qg"))
            V.memset(vtok[0][:], 1.0)
            V.memset(vtok[1][:], 1.0)
            for p in range(npair):
                bank_dve = {}
                for hh in range(2):
                    h = 2*p + hh
                    for w, dstb in ((wuqr, qh[hh]), (wukr, kh[hh])):
                        for g in range(8):
                            bk = g % 2
                            pj = ps_j[bk]
                            if bk in bank_dve:
                                ctx.wait(nc.tensor, "dve", bank_dve[bk])
                            for lc in range(2):
                                i = nc.tensor.matmul(
                                    pj[0:64, :],
                                    w[:, ((h//2)*2+lc)*128 + 64*(h % 2):
                                         ((h//2)*2+lc)*128 + 64*(h % 2) + 64],
                                    cqT[:, S*lc + 512*g: S*lc + 512*(g+1)] if w is wuqr
                                    else ckvT[:, S*lc + 512*g: S*lc + 512*(g+1)],
                                    start=(lc == 0), stop=(lc == 1))
                            pev = ctx.inc("pe", i)
                            ctx.wait(nc.vector, "pe", pev)
                            V.tensor_copy(dstb[0:64, 512*g:512*(g+1)], pj[0:64, :])
                            bank_dve[bk] = ctx.val("dve")
                    for g in range(8):
                        if "k" in bank_dve:
                            ctx.wait(nc.tensor, "dve", bank_dve["k"])
                        for lc in range(2):
                            i = nc.tensor.matmul(
                                ps_k,
                                wqrr[:, ((h//4)*2+lc)*128 + 32*(h % 4):
                                       ((h//4)*2+lc)*128 + 32*(h % 4) + 32],
                                cqT[:, S*lc + 512*g: S*lc + 512*(g+1)],
                                start=(lc == 0), stop=(lc == 1))
                        pev = ctx.inc("pe", i)
                        ctx.wait(nc.vector, "pe", pev)
                        V.tensor_copy(qrp[:, 512*g:512*(g+1)], ps_k)
                        bank_dve["k"] = ctx.val("dve")
                    rope_inplace(qh[hh][64:96, :], pT[0:32, 0:S])
                    bank_dve["k"] = ctx.val("dve")
                    V.tensor_copy(kh[hh][64:96, :], krT[:])
                    for st in range(NT):
                        bk = st % 2
                        pj = ps_j[bk]
                        if bk in bank_dve:
                            ctx.wait(nc.tensor, "dve", bank_dve[bk])
                        for lc in range(2):
                            i = nc.tensor.matmul(
                                pj[:, 0:64],
                                ckvT[:, S*lc + 128*st: S*lc + 128*(st+1)],
                                wuvr[:, 1024*lc + 64*h: 1024*lc + 64*h + 64],
                                start=(lc == 0), stop=(lc == 1))
                        pev = ctx.inc("pe", i)
                        ctx.wait(nc.vector, "pe", pev)
                        V.tensor_copy(vtok[hh][:, 65*st:65*st+64], pj[:, 0:64])
                        bank_dve[bk] = ctx.val("dve")
                for hh in range(2):
                    h = 2*p + hh
                    for qg in range(nqg):
                        act_exp = {}
                        pv_pe = {}
                        dve_fix = {}
                        kt_max = 4*qg + 3
                        npairk = (kt_max + 1) // 2

                        def emit_S(kp):
                            sb = kp % 2
                            pb = kp % 4
                            if kp >= 2:
                                ctx.wait(nc.scalar, "dve", dve_fix[kp - 2])
                            if kp >= 4:
                                ctx.wait(nc.scalar, "pe", pv_pe.get(kp - 4, 0))
                            for u in range(2):
                                kt = 2*kp + u
                                masks = []
                                if kt >= 4*qg:
                                    for jq in range(4):
                                        qt = 4*qg + jq
                                        if qt < kt:
                                            masks.append((jq, C2["fullneg_bf"]))
                                        elif qt == kt:
                                            masks.append((jq, C2["trimask_bf"]))
                                i = nc.tensor.matmul(ps_s[sb][:, 512*u:512*(u+1)],
                                                     kh[hh][0:96, 128*kt:128*(kt+1)],
                                                     qh[hh][0:96, 512*qg:512*(qg+1)],
                                                     start=True, stop=(not masks))
                                for mi, (jq, mk) in enumerate(masks):
                                    slp = ps_s[sb][:, 512*u + 128*jq: 512*u + 128*(jq+1)]
                                    i = nc.tensor.matmul(slp, C["ident_bf"][:], mk[:],
                                                         start=False,
                                                         stop=(mi == len(masks) - 1))
                            pev = ctx.inc("pe", i)
                            ctx.wait(nc.scalar, "pe", pev)
                            A.activation(pT[:, 1024*pb:1024*(pb+1)], ps_s[sb][:],
                                         AF.Exp, scale=RQK)
                            act_exp[kp] = ctx.val("act")
                            ctx.wait(nc.vector, "act", act_exp[kp])
                            V.tensor_scalar(pT[:, 1024*pb:1024*(pb+1)],
                                            pT[:, 1024*pb:1024*(pb+1)],
                                            CLIP_HI, None, ALU.min)
                            dve_fix[kp] = ctx.val("dve")

                        def emit_PV(kp):
                            pb = kp % 4
                            ctx.wait(nc.tensor, "dve", dve_fix[kp])
                            for u in range(2):
                                kt = 2*kp + u
                                i = nc.tensor.matmul(ps_pv[:], vtok[hh][:, 65*kt:65*(kt+1)],
                                                     pT[:, 1024*pb + 512*u: 1024*pb + 512*(u+1)],
                                                     start=(kt == 0), stop=(kt == kt_max))
                            pv_pe[kp] = ctx.inc("pe", i)
                            return pv_pe[kp]

                        emit_S(0)
                        for kp in range(1, npairk):
                            emit_S(kp)
                            emit_PV(kp - 1)
                        pev = emit_PV(npairk - 1)
                        ctx.wait(nc.vector, "pe", pev)
                        V.reciprocal(rrow[:], ps_pv[64:65, :])
                        dv = ctx.val("dve")
                        V.tensor_copy(pvs[0:64, :], ps_pv[0:64, :])
                        ctx.wait(nc.tensor, "dve", dv)
                        i = nc.tensor.matmul(ps_pb, C["ones1_f32"][:, 0:64], rrow[:],
                                             start=True, stop=True)
                        pev = ctx.inc("pe", i)
                        ctx.wait(nc.vector, "pe", pev)
                        V.tensor_mul(att[p][64*hh:64*hh+64, 512*qg:512*(qg+1)],
                                     pvs[0:64, :], ps_pb)
                        ctx.wait(nc.tensor, "dve", ctx.val("dve"))
        with ctx.scope() as p2:
            xres = p2.sb("a0_xres", [128, NC * 512], F32)
            wor = p2.sb("a0_wo", [128, 64 * 128], BF16)
            ctx.wait(nc.gpsimd, "qg", ctx.val("qg"))
            ctx.inc("qg", nc.gpsimd.dma_start(
                out=wor[:].rearrange("k (m c f) -> k m c f", m=8, c=8),
                in_=aps["wo"].rearrange("m c k f -> k m c f")), 16)
            for g in range(8):
                if g >= 1:
                    ctx.wait(nc.sync, "dve", ctx.val("dve"))
                    ctx.wait(nc.sync, "qs", ctx.val("qs"))
                for c in range(NC):
                    ctx.inc("qs", nc.sync.dma_start(
                        out=xres[:, 512*c:512*(c+1)],
                        in_=xsrc[128*c:128*(c+1), 512*g:512*(g+1)]), 16)
                ctx.wait(nc.vector, "qs")
                if g == 0:
                    ctx.wait(nc.tensor, "qg", ctx.val("qg"))
                for dm in range(NC):
                    ctx.wait(nc.tensor, "dve", ctx.val("dve"))
                    for hd in range(npair):
                        i = nc.tensor.matmul(ps_j[0][:],
                                             wor[:, 1024*dm + 128*hd: 1024*dm + 128*(hd+1)],
                                             att[hd][:, 512*g:512*(g+1)],
                                             start=(hd == 0), stop=(hd == npair - 1))
                    pev = ctx.inc("pe", i)
                    ctx.wait(nc.vector, "pe", pev)
                    XR = xres[:, 512*dm:512*(dm+1)]
                    V.tensor_add(XR, XR, ps_j[0][:])
                    dv = ctx.val("dve")
                    ctx.wait(nc.sync, "dve", dv)
                    ctx.inc("qs", nc.sync.dma_start(
                        out=xdst[128*dm:128*(dm+1), 512*g:512*(g+1)], in_=XR), 16)


def build_sub3(H, debug=False, phases=("ln", "gate", "experts")):
    nc = bass.Bass("TRN2", target_bir_lowering=False)
    aps = declare_inputs(nc, H, KEYS_SUB3)
    xT = nc.dram_tensor("xT", [D, S], F32, kind="ExternalInput").ap()
    xout = nc.dram_tensor("xout", [D, S], F32, kind="ExternalOutput").ap()
    xlnd = nc.dram_tensor("xln", [D, S], F32,
                          kind="ExternalOutput" if debug else "Internal").ap()
    dbg = {}
    if debug:
        dbg["comb_dbg"] = nc.dram_tensor("comb_dbg", [128, 8 * NT], F32, kind="ExternalOutput").ap()
    with ExitStack() as es:
        ctx = Ctx(nc, es)
        blk = es.enter_context(nc.Block())
        with ctx.scope() as sc:
            C = load_consts(ctx, sc, aps)
            comb_tm = sc.sb("comb_tm", [128, 8 * NT], F32)
            combT = sc.sb("combT", [8, S], F32)
            if "ln" in phases:
                emit_ln(ctx, sc, C, xT, xlnd)
            if "gate" in phases:
                emit_l1_gate(ctx, sc, C, aps, xlnd, comb_tm, combT)
            if "experts" in phases:
                kw = {}
                for p in phases:
                    if p.startswith("k:"):
                        k, v = p[2:].split("=")
                        kw[k] = int(v)
                emit_l1_experts(ctx, sc, C, aps, xlnd, comb_tm, combT, xT, xout, **kw)
            else:
                for c in range(NC):
                    ctx.inc("qs", nc.sync.dma_start(
                        out=xout[128*c:128*(c+1), :], in_=xlnd[128*c:128*(c+1), :]), 16)
            if debug:
                ctx.barrier()
                ctx.inc("qs", nc.sync.dma_start(out=dbg["comb_dbg"], in_=comb_tm[:]), 16)
        ctx.barrier()
    return nc

def build_sub0(H, ncores=8, phases=("ln", "sel", "experts")):
    nc = bass.Bass("TRN2", target_bir_lowering=False)
    aps = declare_inputs(nc, H, KEYS_SUB0)
    xT = nc.dram_tensor("xT", [D, S], F32, kind="ExternalInput").ap()
    xout = nc.dram_tensor("xout", [D, S], F32, kind="ExternalOutput").ap()
    xlnd = nc.dram_tensor("xln", [D, S], F32, kind="Internal").ap()
    cc_in = nc.dram_tensor("cc_in", [8, S], F32, kind="Internal").ap()
    cc_out = nc.dram_tensor("cc_out", [8, S], F32, kind="Internal").ap()
    dbg = {}
    if "debug" in phases:
        dbg["sel_dbg"] = nc.dram_tensor("sel_dbg", [8, S], F32, kind="ExternalOutput").ap()
    with ExitStack() as es:
        ctx = Ctx(nc, es)
        blk = es.enter_context(nc.Block())
        with ctx.scope() as sc:
            C = load_consts(ctx, sc, aps)
            sel05_tm = sc.sb("sel_tm", [128, 8 * NT], F32)
            sel05T = sc.sb("sel05T", [8, S], F32)
            if "ln" in phases:
                emit_ln(ctx, sc, C, xT, xlnd)
            if "sel" in phases:
                emit_l0_sel(ctx, sc, C, aps, xlnd, cc_in, cc_out, sel05_tm, sel05T,
                            ncores=ncores)
            kw = {}
            for p in phases:
                if p.startswith("k:"):
                    k, v = p[2:].split("=")
                    kw[k] = int(v)
            if "experts" in phases:
                emit_l0_experts(ctx, sc, C, aps, xlnd, sel05_tm, sel05T, xT, xout, **kw)
            if "debug" in phases:
                ctx.barrier()
                ctx.inc("qs", nc.sync.dma_start(out=dbg["sel_dbg"], in_=sel05T[:]), 16)
        ctx.barrier()
    return nc

def build_sub1(H, phases=("ln", "attn")):
    nc = bass.Bass("TRN2", target_bir_lowering=False)
    aps = declare_inputs(nc, H, KEYS_SUB1)
    xT = nc.dram_tensor("xT", [D, S], F32, kind="ExternalInput").ap()
    xout = nc.dram_tensor("xout", [D, S], F32, kind="ExternalOutput").ap()
    xlnd = nc.dram_tensor("xln", [D, S], F32, kind="Internal").ap()
    with ExitStack() as es:
        ctx = Ctx(nc, es)
        blk = es.enter_context(nc.Block())
        with ctx.scope() as sc:
            C = load_consts(ctx, sc, aps)
            if "ln" in phases:
                emit_ln(ctx, sc, C, xT, xlnd)
            kw = {}
            for p in phases:
                if p.startswith("k:"):
                    k, v = p[2:].split("=")
                    kw[k] = int(v)
            emit_l0_attn(ctx, sc, C, aps, xlnd, xT, xout, **kw)
        ctx.barrier()
    return nc

def build_sub4(H, phases=("ln", "attn")):
    nc = bass.Bass("TRN2", target_bir_lowering=False)
    aps = declare_inputs(nc, H, KEYS_SUB4)
    xT = nc.dram_tensor("xT", [D, S], F32, kind="ExternalInput").ap()
    xout = nc.dram_tensor("xout", [D, S], F32, kind="ExternalOutput").ap()
    xlnd = nc.dram_tensor("xln", [D, S], F32, kind="Internal").ap()
    with ExitStack() as es:
        ctx = Ctx(nc, es)
        blk = es.enter_context(nc.Block())
        with ctx.scope() as sc:
            C = load_consts(ctx, sc, aps)
            if "ln" in phases:
                emit_ln(ctx, sc, C, xT, xlnd)
            kw = {}
            for p in phases:
                if p.startswith("k:"):
                    k, v = p[2:].split("=")
                    kw[k] = int(v)
            emit_l1_attn(ctx, sc, C, aps, xlnd, xT, xout, **kw)
        ctx.barrier()
    return nc


def host_prepare(I):
    H = {}
    H.update(host_constants())
    H["wg1"] = np.ascontiguousarray(np.asarray(I["L1_ffn_Wg"], np.float32).reshape(8, 128, 8).transpose(1, 0, 2).reshape(128, 64))
    H["bg1"] = np.ascontiguousarray(np.asarray(I["L1_ffn_bg"], np.float32).reshape(1, 8))
    W1 = np.asarray(I["L1_ffn_W1"], np.float32)
    H["w1l1"] = np.stack([
        W1[e].reshape(8, 128, 32, 128).transpose(2, 1, 0, 3).reshape(32, 128, 1024)
        for e in range(EXP)])
    W2 = np.asarray(I["L1_ffn_W2"], np.float32)
    H["w2l1"] = np.ascontiguousarray(W2.reshape(EXP, 4, 8, 128, 1024))
    H["b1l1"] = np.ascontiguousarray(np.stack(
        [feat_cols(np.asarray(I["L1_ffn_b1"], np.float32)[e]) for e in range(EXP)]
    ).transpose(1, 0, 2).reshape(128, EXP * 32))
    H["b2l1"] = np.ascontiguousarray(np.asarray(I["L1_ffn_b2"], np.float32))
    H["gam1"] = feat_cols(np.asarray(I["L1_ffn_gamma"], np.float32))
    H["bet1"] = feat_cols(np.asarray(I["L1_ffn_beta"], np.float32))
    H["wg0"] = np.ascontiguousarray(np.asarray(I["L0_ffn_w_gate_w"], np.float32).reshape(8, 128, 8).transpose(1, 0, 2).reshape(128, 64))
    H["bg0"] = np.ascontiguousarray(np.asarray(I["L0_ffn_w_gate_b"], np.float32).reshape(1, 8))
    W1 = np.asarray(I["L0_ffn_W1"], np.float32)
    w1 = np.stack([
        W1[e].reshape(8, 128, 2, 128).transpose(2, 1, 0, 3).reshape(2, 128, 1024)
        for e in range(EXP)])
    H["w1l0r"] = np.ascontiguousarray(w1.transpose(2, 0, 1, 3).reshape(128, EXP * 2 * 1024))
    W2 = np.asarray(I["L0_ffn_W2"], np.float32)
    w2 = np.stack([W2[e].reshape(2, 128, 1024).transpose(1, 0, 2) for e in range(EXP)])
    H["w2l0r"] = np.ascontiguousarray(w2.transpose(1, 0, 2, 3).reshape(128, EXP * 2 * 1024))
    H["b1l0"] = np.ascontiguousarray(np.stack(
        [feat_cols(np.asarray(I["L0_ffn_b1"], np.float32)[e]) for e in range(EXP)]
    ).transpose(1, 0, 2).reshape(128, EXP * 2))
    H["b2l0"] = np.ascontiguousarray(np.asarray(I["L0_ffn_b2"], np.float32))
    H["wdq"] = lhsT_blocks(np.asarray(I["L0_attn_W_DQ"], np.float32))
    H["wdkv"] = lhsT_blocks(np.asarray(I["L0_attn_W_DKV"], np.float32))
    H["wuq"] = lhsT_blocks(np.asarray(I["L0_attn_W_UQ"], np.float32))
    H["wuk"] = lhsT_blocks(np.asarray(I["L0_attn_W_UK"], np.float32))
    H["wqr"] = lhsT_blocks(np.asarray(I["L0_attn_W_QR"], np.float32))
    wuv = np.asarray(I["L0_attn_W_UV"], np.float32)
    H["wuvr"] = np.ascontiguousarray(
        wuv.reshape(2, 128, 1024).transpose(1, 0, 2).reshape(128, 2048))
    H["wkr"] = lhsT_blocks(np.asarray(I["L0_attn_W_KR"], np.float32), mc=32)
    H["wo"] = lhsT_blocks(np.asarray(I["L0_attn_W_O"], np.float32))
    H["wqkv"] = lhsT_blocks(np.asarray(I["L1_attn_W_qkv"], np.float32))
    H["bqkv"] = feat_cols(np.asarray(I["L1_attn_b_qkv"], np.float32))
    H["wout"] = lhsT_blocks(np.asarray(I["L1_attn_W_out"], np.float32))
    H["bout"] = feat_cols(np.asarray(I["L1_attn_b_out"], np.float32))
    return H

def per_core_inputs(I, H):
    x = np.asarray(I["L0_attn_x"], np.float32)
    maps = []
    for b in range(8):
        m = dict(H)
        m["xT"] = np.ascontiguousarray(x[b].T)
        maps.append(m)
    return maps

def build_full(H, ncores=8):
    nc = bass.Bass("TRN2", target_bir_lowering=False)
    keys = sorted(set(KEYS_SUB0 + KEYS_SUB1 + KEYS_SUB3 + KEYS_SUB4))
    aps = declare_inputs(nc, H, keys)
    xT = nc.dram_tensor("xT", [D, S], F32, kind="ExternalInput").ap()
    xout = nc.dram_tensor("xout", [D, S], F32, kind="ExternalOutput").ap()
    xw = nc.dram_tensor("xw", [D, S], F32, kind="Internal").ap()
    xlnd = nc.dram_tensor("xln", [D, S], F32, kind="Internal").ap()
    cc_in = nc.dram_tensor("cc_in", [8, S], F32, kind="Internal").ap()
    cc_out = nc.dram_tensor("cc_out", [8, S], F32, kind="Internal").ap()
    w1bf = nc.dram_tensor("w1bf", [EXP, 32, 128, 1024], BF16, kind="Internal").ap()
    w2bf = nc.dram_tensor("w2bf", [EXP, 4, 8, 128, 1024], BF16, kind="Internal").ap()
    with ExitStack() as es:
        ctx = Ctx(nc, es)
        blk = es.enter_context(nc.Block())
        with ctx.scope() as sc:
            C = load_consts(ctx, sc, aps)
            wcstate = {}
            w1f = aps["w1l1"].rearrange("e h k m -> (e h) k m")
            w1d = w1bf.rearrange("e h k m -> (e h) k m")
            w2f = aps["w2l1"].rearrange("e g h k m -> (e g h) k m")
            w2d = w2bf.rearrange("e g h k m -> (e g h) k m")

            def ffn0(xsrc, xdst):
                with ctx.scope() as ss:
                    tm = ss.sb("sel_tm", [128, 8 * NT], F32)
                    sT = ss.sb("sel05T", [8, S], F32)
                    emit_l0_sel(ctx, sc, C, aps, xlnd, cc_in, cc_out, tm, sT, ncores=ncores)
                    emit_l0_experts(ctx, sc, C, aps, xlnd, tm, sT, xsrc, xdst)

            def ffn1(xsrc, xdst):
                with ctx.scope() as ss:
                    tm = ss.sb("comb_tm", [128, 8 * NT], F32)
                    cT = ss.sb("combT", [8, S], F32)
                    emit_l1_gate(ctx, sc, C, aps, xlnd, tm, cT)
                    emit_l1_experts(ctx, sc, C, aps, xlnd, tm, cT, xsrc, xdst,
                                    w1src=w1bf, w2src=w2bf)

            with ctx.scope() as early:
                wcbufs = [early.sb(f"wc{i}", [128, 2048], BF16) for i in range(2)]
                emit_ln(ctx, sc, C, xT, xlnd)
                emit_w_convert(ctx, wcbufs, wcstate, [(w1f[0:128], w1d[0:128])])
                ffn0(xT, xw)
                emit_w_convert(ctx, wcbufs, wcstate, [(w1f[128:256], w1d[128:256])])
                emit_ln(ctx, sc, C, xw, xlnd)
                emit_l0_attn(ctx, sc, C, aps, xlnd, xw, xw)
                emit_w_convert(ctx, wcbufs, wcstate, [(w2f[0:128], w2d[0:128])])
                emit_ln(ctx, sc, C, xw, xlnd)
                ffn0(xw, xw)
                emit_w_convert(ctx, wcbufs, wcstate, [(w2f[128:256], w2d[128:256])])
            emit_ln(ctx, sc, C, xw, xlnd)
            ffn1(xw, xw)
            emit_ln(ctx, sc, C, xw, xlnd)
            emit_l1_attn(ctx, sc, C, aps, xlnd, xw, xw)
            emit_ln(ctx, sc, C, xw, xlnd)
            ffn1(xw, xout)
        ctx.barrier()
    return nc

def kernel_multilaunch(**inputs):
    H = host_prepare(inputs)
    x = np.asarray(inputs["L0_attn_x"], np.float32)
    cur = [np.ascontiguousarray(x[b].T) for b in range(8)]
    progs = {
        "s0": (build_sub0(H), KEYS_SUB0),
        "s1": (build_sub1(H), KEYS_SUB1),
        "s3": (build_sub3(H), KEYS_SUB3),
        "s4": (build_sub4(H), KEYS_SUB4),
    }
    for tag in ["s0", "s1", "s0", "s3", "s4", "s3"]:
        prog, keys = progs[tag]
        maps = []
        for b in range(8):
            m = {k: H[k] for k in keys}
            m["xT"] = cur[b]
            maps.append(m)
        res = run_bass_kernel_spmd(prog, maps, core_ids=list(range(8)))
        cur = [np.ascontiguousarray(np.asarray(res.results[b]["xout"], np.float32))
               for b in range(8)]
    return np.stack([c.T for c in cur]).astype(np.float32)

def kernel(**inputs):
    H = host_prepare(inputs)
    x = np.asarray(inputs["L0_attn_x"], np.float32)
    keys = sorted(set(KEYS_SUB0 + KEYS_SUB1 + KEYS_SUB3 + KEYS_SUB4))
    prog = build_full(H)
    maps = []
    for b in range(8):
        m = {k: H[k] for k in keys}
        m["xT"] = np.ascontiguousarray(x[b].T)
        maps.append(m)
    res = run_bass_kernel_spmd(prog, maps, core_ids=list(range(8)))
    out = np.stack([np.asarray(res.results[b]["xout"], np.float32).T for b in range(8)])
    return np.ascontiguousarray(out).astype(np.float32)


# version 15
# speedup vs baseline: 1.0933x; 1.0933x over previous
import numpy as np
import ml_dtypes
from contextlib import ExitStack
import concourse.bass as bass
import concourse.mybir as mybir
from concourse.bass_utils import run_bass_kernel_spmd

F32 = mybir.dt.float32
BF16 = mybir.dt.bfloat16
AF = mybir.ActivationFunctionType
ALU = mybir.AluOpType
AX = mybir.AxisListType

D = 1024
S = 4096
NC = 8
NT = 32
EXP = 8
NEG = -1.0e38


def lhsT_blocks(W, mc=128):
    K, M = W.shape
    nK, nM = K // 128, M // mc
    return np.ascontiguousarray(W.reshape(nK, 128, nM, mc).transpose(2, 0, 1, 3))

def feat_cols(v):
    n = v.shape[0] // 128
    return np.ascontiguousarray(v.reshape(n, 128).T)


class EngWrap:

    def __init__(self, ctx, eng, sem, relaxed=False):
        self._ctx, self._eng, self._sem = ctx, eng, sem
        self._relaxed = relaxed

    def __getattr__(self, op):
        eng = self._eng
        f = getattr(eng, op)
        ctx, sem = self._ctx, self._sem
        relaxed = self._relaxed

        def g(*a, **k):
            if not relaxed and ctx.val(sem):
                ctx.wait(eng, sem, ctx.val(sem))
            ins = f(*a, **k)
            ctx.inc(sem, ins)
            return ins
        return g

class Scope:
    def __init__(self, ctx):
        self.ctx = ctx
        self.es = ExitStack()

    def sb(self, nm, shape, dtype):
        return self.es.enter_context(self.ctx.nc.sbuf_tensor(self.ctx.name(nm), shape, dtype))

    def ps(self, nm, shape, dtype=F32):
        return self.es.enter_context(self.ctx.nc.psum_tensor(self.ctx.name(nm), shape, dtype))

    def __enter__(self):
        return self

    def __exit__(self, *a):
        self.ctx.barrier()
        self.es.close()
        return False

class Ctx:
    def __init__(self, nc, es):
        self.nc = nc
        self.es = es
        self.sems = {}
        self.uid = 0
        self.V = EngWrap(self, nc.vector, "dve")
        self.A = EngWrap(self, nc.scalar, "act")
        self.G = EngWrap(self, nc.gpsimd, "pool")
        self.Vr = EngWrap(self, nc.vector, "dve", relaxed=True)
        self.Ar = EngWrap(self, nc.scalar, "act", relaxed=True)

    def name(self, p):
        self.uid += 1
        return f"{p}_{self.uid}"

    def scope(self):
        return Scope(self)

    def sem(self, nm):
        if nm not in self.sems:
            h = self.es.enter_context(self.nc.semaphore(nm))
            self.sems[nm] = [h, 0]
        return self.sems[nm]

    def inc(self, nm, instr, amt=1):
        s = self.sem(nm)
        instr.then_inc(s[0], amt)
        s[1] += amt
        return s[1]

    def val(self, nm):
        return self.sem(nm)[1]

    def wait(self, eng, nm, v=None):
        if isinstance(eng, EngWrap):
            eng = eng._eng
        s = self.sem(nm)
        eng.wait_ge(s[0], s[1] if v is None else v)

    def barrier(self):
        snap = [(h, c) for (h, c) in self.sems.values() if c > 0]
        nc = self.nc
        for eng in (nc.sync, nc.gpsimd, nc.tensor, nc.vector, nc.scalar):
            for h, c in snap:
                eng.wait_ge(h, c)


def host_constants():
    c = {}
    c["c_ones1_f32"] = np.ones((1, 128), np.float32)
    c["c_ones128_bf"] = np.ones((128, 1), np.float32)
    c["c_ident_f32"] = np.eye(128, dtype=np.float32)
    c["c_ident_bf"] = np.eye(128, dtype=np.float32)
    c["c_iotaT"] = np.broadcast_to(np.tile(np.arange(8, dtype=np.float32), 32), (128, 256)).copy()
    c["c_iotam1000T"] = c["c_iotaT"] - 1000.0
    tm = np.zeros((128, 128), np.float32)
    tm[np.tril_indices(128, -1)] = NEG
    c["c_trimask"] = tm
    t01 = np.ones((128, 128), np.float32)
    t01[np.tril_indices(128, -1)] = 0.0
    c["c_tri01"] = t01
    c["c_fullneg"] = np.full((128, 128), NEG, np.float32)
    inv = (10000.0 ** (-np.arange(0, 32, 2, dtype=np.float32) / 32.0))
    ang = np.arange(S, dtype=np.float32)[None, :] * inv[:, None]
    cos, sin = np.cos(ang), np.sin(ang)
    ct = np.zeros((32, S), np.float32)
    st = np.zeros((32, S), np.float32)
    ct[0::2] = cos; ct[1::2] = cos
    st[0::2] = -sin; st[1::2] = sin
    c["c_ropecs"] = np.concatenate([ct, st], axis=1)
    P = np.zeros((32, 32), np.float32)
    for i in range(32):
        P[i ^ 1, i] = 1.0
    c["c_pswap"] = P
    return c

CONST_KEYS = ["c_ones1_f32", "c_ones128_bf", "c_ident_f32", "c_ident_bf",
              "c_iotaT", "c_iotam1000T"]
KEYS_SUB0 = CONST_KEYS + ["wg0", "bg0", "w1l0r", "w2l0r", "b1l0", "b2l0"]
KEYS_SUB1 = CONST_KEYS + ["c_trimask", "c_tri01", "c_ropecs", "c_pswap", "c_fullneg",
                          "wdq", "wdkv", "wuq", "wuk", "wqr", "wuvr", "wkr", "wo"]
KEYS_SUB3 = CONST_KEYS + ["wg1", "bg1", "w1l1", "w2l1", "b1l1", "b2l1", "gam1", "bet1"]
KEYS_SUB4 = CONST_KEYS + ["wqkv", "bqkv", "wout", "bout"]

def declare_inputs(nc, host_arrays, keys=None):
    aps = {}
    for k, v in host_arrays.items():
        if keys is not None and k not in keys:
            continue
        dt = BF16 if v.dtype == ml_dtypes.bfloat16 else F32
        aps[k] = nc.dram_tensor(k, list(v.shape), dt, kind="ExternalInput").ap()
    return aps

def load_consts(ctx, sc, aps):
    nc = ctx.nc
    C = {}
    spec = [("ones1_f32", "c_ones1_f32", [1, 128], F32, False),
            ("ones128_bf", "c_ones128_bf", [128, 1], BF16, True),
            ("ident_f32", "c_ident_f32", [128, 128], F32, False),
            ("ident_bf", "c_ident_bf", [128, 128], BF16, True),
            ("iotaT", "c_iotaT", [128, 256], F32, False),
            ("iotam1000T", "c_iotam1000T", [128, 256], F32, False)]
    for nm, key, shape, dt, cast in spec:
        C[nm] = sc.sb(nm, shape, dt)
        eng, q = (nc.gpsimd, "qg") if cast else (nc.sync, "qs")
        ctx.inc(q, eng.dma_start(out=C[nm][:], in_=aps[key]), 16)
    ctx.barrier()
    return C

def load_attn_consts(ctx, ph, aps, rope=False):
    nc = ctx.nc
    C2 = {}
    spec = [("trimask", "c_trimask", [128, 128], F32, False),
            ("trimask_bf", "c_trimask", [128, 128], BF16, True),
            ("fullneg_bf", "c_fullneg", [128, 128], BF16, True)]
    if rope:
        spec += [("pswap", "c_pswap", [32, 32], BF16, True),
                 ("tri01", "c_tri01", [128, 128], BF16, True)]
    for nm, key, shape, dt, cast in spec:
        C2[nm] = ph.sb(nm, shape, dt)
        eng, q = (nc.gpsimd, "qg") if cast else (nc.sync, "qs")
        ctx.inc(q, eng.dma_start(out=C2[nm][:], in_=aps[key]), 16)
    return C2


def emit_w_convert(ctx, bufs, state, pairs):
    nc = ctx.nc
    st_after = state
    for src_ap, dst_ap in pairs:
        B = src_ap.shape[0]
        M = src_ap.shape[2]
        per = max(1, 2048 // M)
        for b0 in range(0, B, per):
            nb = min(per, B - b0)
            sl = state.setdefault("_i", 0) % 2
            state["_i"] += 1
            buf = bufs[sl][:, 0:nb*M].rearrange("k (b m) -> k b m", b=nb)
            if sl in st_after:
                ctx.wait(nc.gpsimd, "wcs", st_after[sl])
            ctx.inc("wcl", nc.gpsimd.dma_start(
                out=buf, in_=src_ap[b0:b0+nb].rearrange("b k m -> k b m")), 16)
            ctx.wait(nc.sync, "wcl", ctx.val("wcl"))
            ctx.inc("wcs", nc.sync.dma_start(
                out=dst_ap[b0:b0+nb].rearrange("b k m -> k b m"), in_=buf), 16)
            st_after[sl] = ctx.val("wcs")


class StatsKit:
    def __init__(self, ctx, sc, C, banks=None):
        self.ctx, self.C = ctx, C
        self.xb = sc.sb("st_xb", [128, NC * 512], BF16)
        self.xq = sc.sb("st_xq", [128, NC * 512], BF16)
        self.rw = sc.sb("st_rw", [1, 4 * 512], F32)
        if banks is None:
            self.ps_s = sc.ps("st_s", [1, 512])[:]
            self.ps_ss = sc.ps("st_ss", [1, 512])[:]
            self.pb_m = sc.ps("st_m", [128, 512])[:]
            self.pb_r = sc.ps("st_r", [128, 512])[:]
        else:
            self.ps_s, self.ps_ss, self.pb_m, self.pb_r = banks

    def emit(self, get_src, denom=1024.0, eps=1e-5):
        ctx, C = self.ctx, self.C
        nc, V, A = ctx.nc, ctx.V, ctx.A
        R = lambda i: self.rw[:, 512*i:512*(i+1)]
        Vr, Ar = ctx.Vr, ctx.Ar
        for c in range(NC):
            src = get_src(c)
            (V if c == 0 else Vr).tensor_copy(self.xb[:, 512*c:512*(c+1)], src)
            (A if c == 0 else Ar).activation(self.xq[:, 512*c:512*(c+1)], src, AF.Square)
        ctx.wait(nc.tensor, "dve")
        ctx.wait(nc.tensor, "act")
        for c in range(NC):
            nc.tensor.matmul(self.ps_s, C["ones128_bf"][:], self.xb[:, 512*c:512*(c+1)],
                             start=(c == 0), stop=(c == NC - 1))
            i = nc.tensor.matmul(self.ps_ss, C["ones128_bf"][:], self.xq[:, 512*c:512*(c+1)],
                                 start=(c == 0), stop=(c == NC - 1))
        pev = ctx.inc("pe", i)
        ctx.wait(nc.vector, "pe", pev)
        V.tensor_scalar_mul(R(0), self.ps_s, 1.0 / denom)
        V.tensor_scalar_mul(R(1), self.ps_ss, 1.0 / denom)
        V.tensor_mul(R(2), R(0), R(0))
        V.tensor_sub(R(1), R(1), R(2))
        V.tensor_scalar_add(R(1), R(1), eps)
        ctx.wait(nc.scalar, "dve")
        A.activation(R(2), R(1), AF.Sqrt)
        ctx.wait(nc.vector, "act")
        V.reciprocal(R(3), R(2))
        ctx.wait(nc.tensor, "dve")
        nc.tensor.matmul(self.pb_m, C["ones1_f32"][:], R(0), start=True, stop=True)
        i = nc.tensor.matmul(self.pb_r, C["ones1_f32"][:], R(3), start=True, stop=True)
        return ctx.inc("pe", i)


def emit_ln(ctx, sc, C, xsrc, xln):
    nc, V = ctx.nc, ctx.V
    with ctx.scope() as ph:
        xf = ph.sb("ln_xf", [128, 2 * NC * 512], F32)
        xo = ph.sb("ln_xo", [128, 2 * NC * 512], F32)
        st = StatsKit(ctx, ph, C)
        def ln_loads(g):
            if g >= 8:
                return
            sl = f"lnx{g % 2}"
            off = (g % 2) * NC * 512
            if g >= 2:
                ctx.wait(nc.sync, "dve", ctx.val("dve"))
                ctx.wait(nc.sync, "act", ctx.val("act"))
            for c in range(NC):
                ctx.inc(sl, nc.sync.dma_start(
                    out=xf[:, off + 512*c: off + 512*(c+1)],
                    in_=xsrc[128*c:128*(c+1), 512*g:512*(g+1)]), 16)

        ln_loads(0)
        ln_loads(1)
        for g in range(8):
            sl = f"lnx{g % 2}"
            off = (g % 2) * NC * 512
            X = lambda c: xf[:, off + 512*c: off + 512*(c+1)]
            O = lambda c: xo[:, off + 512*c: off + 512*(c+1)]
            ctx.wait(nc.vector, sl)
            ctx.wait(nc.scalar, sl)
            pev = st.emit(X)
            ctx.wait(nc.vector, "pe", pev)
            ctx.wait(nc.vector, "lnst")
            for c in range(NC):
                (ctx.Vr if c else V).tensor_sub(O(c), X(c), st.pb_m)
                V.tensor_mul(O(c), O(c), st.pb_r)
            dv = ctx.val("dve")
            ln_loads(g + 2)
            ctx.wait(nc.sync, "dve", dv)
            for c in range(NC):
                ctx.inc("lnst", nc.sync.dma_start(
                    out=xln[128*c:128*(c+1), 512*g:512*(g+1)], in_=O(c)), 16)


def emit_l1_gate(ctx, sc, C, aps, xln, comb_tm, combT,
                 wg_key="wg1", bg_key="bg1"):
    nc, V, A = ctx.nc, ctx.V, ctx.A
    with ctx.scope() as ph:
        wg = ph.sb("g_wg", [128, 8 * NC], F32)
        bgr = ph.sb("g_bg", [1, 8], F32)
        xg = ph.sb("g_xg", [128, 2 * NC * 512], F32)
        Lg = ph.sb("g_Lg", [128, 256], F32)
        tA = ph.sb("g_tA", [128, 256], F32)
        tB = ph.sb("g_tB", [128, 256], F32)
        tC = ph.sb("g_tC", [128, 256], F32)
        sm = ph.sb("g_sm", [128, 6 * 32], F32)
        pgall = ph.ps("g_pg", [128, 256])
        pt = ph.ps("g_pt", [8, 512])
        SM = lambda i: sm[:, 32*i:32*(i+1)]
        SMb = lambda i: sm[:, 32*i:32*(i+1)].rearrange("p (t o) -> p t o", o=1).broadcast_to([128, 32, 8])
        E3 = lambda ap: ap.rearrange("p (t e) -> p t e", e=8)
        ctx.inc("qs", nc.sync.dma_start(out=wg[:], in_=aps[wg_key]), 16)
        ctx.inc("qs", nc.sync.dma_start(out=bgr[:], in_=aps[bg_key]), 16)
        gpe = {}
        for g in range(8):
            sl = f"gx{g % 2}"
            off = (g % 2) * NC * 512
            X = lambda c: xg[:, off + 512*c: off + 512*(c+1)]
            if g >= 2:
                ctx.wait(nc.sync, "pe", gpe[g - 2])
            for c in range(NC):
                ctx.inc(sl, nc.sync.dma_start(
                    out=X(c), in_=xln[128*c:128*(c+1), 512*g:512*(g+1)]), 16)
            ctx.wait(nc.tensor, sl)
            ctx.wait(nc.tensor, "qs")
            for j4 in range(4):
                j = 4*g + j4
                for c in range(NC):
                    nc.tensor.matmul(pgall[:, 8*j:8*(j+1)],
                                     xg[:, off + 512*c + 128*j4: off + 512*c + 128*j4 + 128],
                                     wg[:, 8*c:8*(c+1)], start=(c == 0), stop=False)
                i = nc.tensor.matmul(pgall[:, 8*j:8*(j+1)], C["ones1_f32"][:], bgr[:],
                                     start=False, stop=True)
            gpe[g] = ctx.inc("pe", i)
        ctx.wait(nc.vector, "pe", gpe[7])
        V.tensor_copy(Lg[:], pgall[:])
        V.tensor_reduce(SM(0), E3(Lg[:]), AX.X, ALU.max)
        V.tensor_tensor(E3(tA[:]), E3(Lg[:]), SMb(0), ALU.is_ge)
        V.scalar_tensor_tensor(tB[:], tA[:], -1000.0, C["iotaT"][:], ALU.mult, ALU.add)
        V.tensor_reduce(SM(1), E3(tB[:]), AX.X, ALU.min)
        V.tensor_tensor(E3(tA[:]), E3(C["iotam1000T"][:]), SMb(1), ALU.is_equal)
        V.scalar_tensor_tensor(tB[:], tA[:], NEG, Lg[:], ALU.mult, ALU.add)
        V.tensor_reduce(SM(2), E3(tB[:]), AX.X, ALU.max)
        V.tensor_tensor(E3(tC[:]), E3(tB[:]), SMb(2), ALU.is_ge)
        V.scalar_tensor_tensor(tC[:], tC[:], -1000.0, C["iotaT"][:], ALU.mult, ALU.add)
        V.tensor_reduce(SM(3), E3(tC[:]), AX.X, ALU.min)
        V.tensor_tensor(E3(tC[:]), E3(C["iotam1000T"][:]), SMb(3), ALU.is_equal)
        V.tensor_sub(SM(4), SM(2), SM(0))
        ctx.wait(nc.scalar, "dve")
        A.activation(SM(5), SM(4), AF.Sigmoid)
        ctx.wait(nc.vector, "act")
        V.tensor_tensor(E3(tA[:]), E3(tA[:]), SMb(5), ALU.mult)
        V.tensor_tensor(E3(tC[:]), E3(tC[:]), SMb(5), ALU.mult)
        V.tensor_sub(tC[:], tC[:], tA[:])
        V.tensor_tensor(E3(tA[:]), E3(C["iotam1000T"][:]), SMb(1), ALU.is_equal)
        V.tensor_add(comb_tm[:], tC[:], tA[:])
        ctx.wait(nc.tensor, "dve")
        for j4 in range(NT // 4):
            if j4 >= 1:
                ctx.wait(nc.tensor, "dve", ctx.val("dve"))
            for u in range(4):
                j = 4*j4 + u
                i = nc.tensor.transpose(pt[:, 128*u:128*(u+1)], comb_tm[:, 8*j:8*(j+1)],
                                        C["ident_f32"][:])
            pev = ctx.inc("pe", i)
            ctx.wait(nc.vector, "pe", pev)
            for u in range(4):
                j = 4*j4 + u
                (V if u == 0 else ctx.Vr).tensor_copy(combT[:, 128*j:128*(j+1)],
                                                      pt[:, 128*u:128*(u+1)])


def emit_l1_experts(ctx, sc, C, aps, xln, comb_tm, combT, xsrc, xdst,
                    ntset=8, nexp=EXP, do_epi=1, do_b2=1, do_B=1,
                    w1src=None, w2src=None):
    nc, V, A = ctx.nc, ctx.V, ctx.A
    with ctx.scope() as ph:
        b1 = ph.sb("e1_b1", [128, EXP * 32], F32)
        b2 = ph.sb("e1_b2", [8, 1024], F32)
        gam = ph.sb("e1_gam", [128, 8], F32)
        bet = ph.sb("e1_bet", [128, 8], F32)
        xln_ts = ph.sb("e1_xts", [128, 2 * NC * 512], BF16)
        xres = ph.sb("e1_xres", [128, 2 * NC * 512], F32)
        preT = ph.sb("e1_preT", [128, 2 * NC * 512], F32)
        Hgel = ph.sb("e1_H", [128, 32 * 512], BF16)
        w1b = ph.sb("e1_w1", [128, 3 * 1024], BF16)
        w2b = ph.sb("e1_w2", [128, 3 * 4096], BF16)
        out_acc = ph.sb("e1_oa", [128, 4 * 1024], F32)
        py = [ph.ps(f"e1_py{j}", [128, 512]) for j in range(8)]
        st = StatsKit(ctx, ph, C, banks=(py[4][0:1, :], py[5][0:1, :],
                                         py[6][:], py[7][:]))
        ctx.inc("qs", nc.sync.dma_start(out=b1[:], in_=aps["b1l1"]), 16)
        ctx.inc("qs", nc.sync.dma_start(out=b2[:], in_=aps["b2l1"]), 16)
        ctx.inc("qs", nc.sync.dma_start(out=gam[:], in_=aps["gam1"]), 16)
        ctx.inc("qs", nc.sync.dma_start(out=bet[:], in_=aps["bet1"]), 16)
        resident_qs = ctx.val("qs")
        ctx.wait(nc.tensor, "qs", resident_qs)
        ctx.wait(nc.scalar, "qs", resident_qs)
        ctx.wait(nc.vector, "qs", resident_qs)

        w1ap = aps["w1l1"] if w1src is None else w1src
        w2ap = aps["w2l1"] if w2src is None else w2src
        w1eng = nc.gpsimd if w1src is None else nc.scalar
        w2eng = nc.gpsimd if w2src is None else nc.sync

        def w1_load(e, hc, guard_pe):
            sl = f"w1{hc % 3}"
            ctx.wait(w1eng, "pe", guard_pe)
            ctx.inc(sl, w1eng.dma_start(
                out=w1b[:, 1024*(hc % 3):1024*(hc % 3 + 1)],
                in_=w1ap[e, hc]), 16)
            return (sl, ctx.val(sl))

        def w2_load(e, s, guard_pe):
            sl = f"w2{s % 3}"
            dg, hg = divmod(s, 4)
            ctx.wait(w2eng, "pe", guard_pe)
            src = w2ap[e, hg, :, :, 512*dg:512*(dg+1)].rearrange("h k d -> k h d")
            o = 4096 * (s % 3)
            dst = w2b[:, o:o+4096].rearrange("k (h d) -> k h d", h=8)
            ctx.inc(sl, w2eng.dma_start(out=dst, in_=src), 16)
            return (sl, ctx.val(sl))

        def tset_loads(t):
            if t >= ntset:
                return (("qs", 0), ("qs", 0))
            to = 512 * t
            xo = (t % 2) * NC * 512
            ctx.wait(nc.gpsimd, "pe", ctx.val("pe"))
            ctx.wait(nc.sync, "dve", ctx.val("dve"))
            ctx.wait(nc.sync, "act", ctx.val("act"))
            ctx.wait(nc.sync, "qs", ctx.val("qs"))
            xs, ps = f"e1x{t % 2}", f"e1p{t % 2}"
            for c in range(NC):
                ctx.inc(xs, nc.gpsimd.dma_start(
                    out=xln_ts[:, xo + 512*c: xo + 512*(c+1)],
                    in_=xln[128*c:128*(c+1), to:to+512]), 16)
                ctx.inc(ps, nc.sync.dma_start(
                    out=preT[:, xo + 512*c: xo + 512*(c+1)],
                    in_=xln[128*c:128*(c+1), to:to+512]), 16)
                ctx.inc(ps, nc.sync.dma_start(
                    out=xres[:, xo + 512*c: xo + 512*(c+1)],
                    in_=xsrc[128*c:128*(c+1), to:to+512]), 16)
            return ((xs, ctx.val(xs)), (ps, ctx.val(ps)))

        for eng in (nc.sync, nc.gpsimd):
            for sname in ("dve", "act", "pe", "qs"):
                if sname in ctx.sems and ctx.val(sname):
                    ctx.wait(eng, sname, ctx.val(sname))
        ld_tok = {0: tset_loads(0)}
        for t in range(ntset):
            to = 512 * t
            xo = (t % 2) * NC * 512
            X = lambda c: xln_ts[:, xo + 512*c: xo + 512*(c+1)]
            PRE = lambda c: preT[:, xo + 512*c: xo + 512*(c+1)]
            XRES = lambda c: xres[:, xo + 512*c: xo + 512*(c+1)]
            ctx.wait(nc.tensor, "dve", ctx.val("dve"))
            if do_b2:
                for j8 in range(8):
                    j, dg = divmod(j8, 2)
                    jj = 4 * t + j
                    i = nc.tensor.matmul(py[j8][:], combT[:, 128*jj:128*(jj+1)],
                                         b2[:, 512*dg:512*(dg+1)], start=True, stop=True)
                pev = ctx.inc("pe", i)
                ctx.wait(nc.vector, "pe", pev)
                for j8 in range(8):
                    j, dg = divmod(j8, 2)
                    (V if j8 == 0 else ctx.Vr).tensor_copy(
                        out_acc[:, 1024*j + 512*dg: 1024*j + 512*(dg+1)], py[j8][:])
            ctx.wait(nc.tensor, "dve", ctx.val("dve"))
            ctx.wait(nc.tensor, *ld_tok[t][0])
            for e in range(nexp):
                pe_base = ctx.val("pe")
                qg_after = {0: w1_load(e, 0, pe_base), 1: w1_load(e, 1, pe_base),
                            2: w1_load(e, 2, pe_base)}
                act_after = {}
                pe_A = {}
                for hc in range(32):
                    pb = hc % 4
                    ctx.wait(nc.tensor, *qg_after[hc])
                    if hc >= 4:
                        ctx.wait(nc.tensor, "act", act_after[hc - 4])
                    for kc in range(NC):
                        i = nc.tensor.matmul(py[pb][:],
                                             w1b[:, 1024*(hc % 3) + 128*kc: 1024*(hc % 3) + 128*(kc+1)],
                                             X(kc), start=(kc == 0), stop=(kc == NC - 1))
                    pe_A[hc] = pev = ctx.inc("pe", i)
                    ctx.wait(nc.scalar, "pe", pev)
                    (A if hc == 0 else ctx.Ar).activation(
                        Hgel[:, 512*hc:512*(hc+1)], py[pb][:], AF.Gelu,
                        bias=b1[:, 32*e+hc: 32*e+hc+1])
                    act_after[hc] = ctx.val("act")
                    if hc + 3 < 32:
                        qg_after[hc + 3] = w1_load(e, hc + 3, pe_A[hc])
                ctx.wait(nc.tensor, "act", ctx.val("act"))
                if not do_B:
                    continue
                pe_base = ctx.val("pe")
                qg_after = {0: w2_load(e, 0, pe_base), 1: w2_load(e, 1, pe_base),
                            2: w2_load(e, 2, pe_base)}
                pe_B = {}
                for s in range(8):
                    dg, hg = divmod(s, 4)
                    wb = s % 3
                    ctx.wait(nc.tensor, *qg_after[s])
                    for h8 in range(8):
                        hc = 8 * hg + h8
                        for j in range(4):
                            i = nc.tensor.matmul(
                                py[4*dg + j][:],
                                Hgel[:, 512*hc + 128*j: 512*hc + 128*(j+1)],
                                w2b[:, 4096*wb + 512*h8: 4096*wb + 512*(h8+1)],
                                start=(hc == 0), stop=(hc == 31))
                    pe_B[s] = pev = ctx.inc("pe", i)
                    if s + 3 < 8:
                        qg_after[s + 3] = w2_load(e, s + 3, pe_B[s])
                    if hg == 3:
                        ctx.wait(nc.vector, "pe", pev)
                        for j in range(4):
                            jj = 4 * t + j
                            sl = out_acc[:, 1024*j + 512*dg: 1024*j + 512*(dg+1)]
                            (V if j == 0 else ctx.Vr).scalar_tensor_tensor(
                                sl, py[4*dg + j][:], comb_tm[:, 8*jj+e: 8*jj+e+1],
                                sl, ALU.mult, ALU.add)
                ctx.wait(nc.tensor, "dve", ctx.val("dve"))
                if e == 0:
                    ld_tok[t + 1] = tset_loads(t + 1)
            if not do_epi:
                continue
            ctx.wait(nc.vector, *ld_tok[t][1])
            grp_dve = {}
            for grp in range(8):
                tiles = [(j, dc) for j in range(4) for dc in range(NC)][4*grp:4*grp+4]
                if grp >= 1:
                    ctx.wait(nc.tensor, "dve", grp_dve.get(grp - 1, 0))
                for b4, (j, dc) in enumerate(tiles):
                    i = nc.tensor.transpose(
                        py[b4][:, 0:128],
                        out_acc[:, 1024*j + 128*dc: 1024*j + 128*(dc+1)],
                        C["ident_f32"][:])
                pev = ctx.inc("pe", i)
                ctx.wait(nc.vector, "pe", pev)
                for b4, (j, dc) in enumerate(tiles):
                    dst = preT[:, xo + 512*dc + 128*j: xo + 512*dc + 128*(j+1)]
                    (V if b4 == 0 else ctx.Vr).tensor_add(dst, py[b4][:, 0:128], dst)
                grp_dve[grp] = ctx.val("dve")
            ctx.wait(nc.scalar, "dve")
            ctx.wait(nc.tensor, "dve")
            pev = st.emit(lambda c: PRE(c))
            ctx.wait(nc.vector, "pe", pev)
            for c in range(NC):
                PT = PRE(c)
                V.tensor_sub(PT, PT, st.pb_m)
                V.tensor_mul(PT, PT, st.pb_r)
                V.tensor_scalar(PT, PT, gam[:, c:c+1], bet[:, c:c+1], ALU.mult, ALU.add)
                XR = XRES(c)
                V.scalar_tensor_tensor(XR, PT, 0.5, XR, ALU.mult, ALU.add)
                dv = ctx.val("dve")
                ctx.wait(nc.sync, "dve", dv)
                ctx.inc("qs", nc.sync.dma_start(
                    out=xdst[128*c:128*(c+1), to:to+512], in_=XR), 16)


def emit_l1_attn(ctx, sc, C, aps, xln, xsrc, xdst, npair=8, nqg=8):
    nc, V, A = ctx.nc, ctx.V, ctx.A
    with ctx.scope() as ph:
        bqkv = ph.sb("a1_bqkv", [128, 24], F32)
        bout = ph.sb("a1_bout", [128, 8], F32)
        xbf = ph.sb("a1_xbf", [128, 2 * NC * 512], BF16)
        wq = ph.sb("a1_wq", [128, 2 * 24 * 128], BF16)
        qT2 = ph.sb("a1_q", [128, S], BF16)
        kT2 = ph.sb("a1_k", [128, S], BF16)
        vT2 = ph.sb("a1_v", [128, S], BF16)
        vtok = [ph.sb(f"a1_vt{i}", [128, 32 * 65], BF16) for i in range(2)]
        pT = ph.sb("a1_pT", [128, 4 * 1024], BF16)
        pvs = ph.sb("a1_pvs", [128, 512], F32)
        rrow = ph.sb("a1_rr", [1, 512], F32)
        att = [ph.sb(f"a1_att{i}", [128, S], BF16) for i in range(8)]
        xres = ph.sb("a1_xres", [128, 2 * NC * 512], F32)
        ps_s = [ph.ps(f"a1_s{i}", [128, 1024]) for i in range(2)]
        ps_pv = ph.ps("a1_pv", [65, 512])
        ps_pb = ph.ps("a1_pb", [64, 512])
        ps_vt = ph.ps("a1_vtp", [128, 128], BF16)
        ps_qkv = ph.ps("a1_qkv", [128, 512])
        woutb = ph.sb("a1_wo", [128, 64 * 128], BF16)
        ctx.inc("qs", nc.sync.dma_start(out=bqkv[:], in_=aps["bqkv"]), 16)
        ctx.inc("qs", nc.sync.dma_start(out=bout[:], in_=aps["bout"]), 16)
        ctx.inc("qg", nc.gpsimd.dma_start(
            out=woutb[:].rearrange("k (m c f) -> k m c f", m=8, c=8),
            in_=aps["wout"].rearrange("m c k f -> k m c f")), 16)
        ctx.wait(nc.vector, "qs", ctx.val("qs"))
        V.memset(vtok[0][:], 1.0)
        V.memset(vtok[1][:], 1.0)
        bank_dve = {}
        for p in range(npair):
            wsl = f"a1w{p % 2}"
            ctx.wait(nc.gpsimd, "pe", ctx.val("pe"))
            ctx.inc(wsl, nc.gpsimd.dma_start(
                out=wq[:, 3072*(p % 2):3072*(p % 2 + 1)].rearrange("k (o c m) -> k o c m", o=3, c=8),
                in_=aps["wqkv"][3*p:3*p+3].rearrange("o c k m -> k o c m")), 16)
            ctx.wait(nc.tensor, wsl)
            for g in range(8):
                xsl = f"a1x{g % 2}"
                xo = (g % 2) * NC * 512
                if g >= 2:
                    ctx.wait(nc.gpsimd, "pe", ctx.val("pe"))
                for c in range(NC):
                    ctx.inc(xsl, nc.gpsimd.dma_start(
                        out=xbf[:, xo + 512*c: xo + 512*(c+1)],
                        in_=xln[128*c:128*(c+1), 512*g:512*(g+1)]), 16)
                ctx.wait(nc.tensor, xsl)
                for o3 in range(3):
                    oc = 3*p + o3
                    if "qkv" in bank_dve:
                        ctx.wait(nc.tensor, "dve", bank_dve["qkv"])
                    for dk in range(NC):
                        i = nc.tensor.matmul(
                            ps_qkv[:],
                            wq[:, 3072*(p % 2) + 1024*o3 + 128*dk: 3072*(p % 2) + 1024*o3 + 128*(dk+1)],
                            xbf[:, xo + 512*dk: xo + 512*(dk+1)],
                            start=(dk == 0), stop=(dk == NC - 1))
                    pev = ctx.inc("pe", i)
                    ctx.wait(nc.vector, "pe", pev)
                    for seg in range(2):
                        f0 = 128*oc + 64*seg
                        h = f0 // 192
                        j = f0 % 192
                        kind = j // 64
                        dstbuf = (qT2, kT2, vT2)[kind]
                        r0 = 64 * (h % 2)
                        V.tensor_scalar(dstbuf[r0:r0+64, 512*g:512*(g+1)],
                                        ps_qkv[64*seg:64*seg+64, :],
                                        bqkv[64*seg:64*seg+64, oc:oc+1], None, ALU.add)
                    bank_dve["qkv"] = ctx.val("dve")
            for hh in range(2):
                for ti in range(NT):
                    if "vt" in bank_dve:
                        ctx.wait(nc.tensor, "dve", bank_dve["vt"])
                    i = nc.tensor.transpose(ps_vt[:, 0:64],
                                            vT2[64*hh:64*hh+64, 128*ti:128*(ti+1)],
                                            C["ident_bf"][64*hh:64*hh+64, 64*hh:64*hh+64])
                    pev = ctx.inc("pe", i)
                    ctx.wait(nc.vector, "pe", pev)
                    V.tensor_copy(vtok[hh][:, 65*ti:65*ti+64], ps_vt[:, 0:64])
                    bank_dve["vt"] = ctx.val("dve")
            for hh in range(2):
                r0 = 64 * hh
                norm = None
                for qg in range(nqg):
                    act_exp = {}
                    pv_pe = {}

                    def emit_S(kp):
                        sb = kp % 2
                        pb = kp % 4
                        if kp >= 2:
                            ctx.wait(nc.tensor, "act", act_exp[kp - 2])
                        if kp >= 4:
                            ctx.wait(nc.scalar, "pe", pv_pe[kp - 4])
                        for u in range(2):
                            kt = 2*kp + u
                            i = nc.tensor.matmul(ps_s[sb][:, 512*u:512*(u+1)],
                                                 kT2[r0:r0+64, 128*kt:128*(kt+1)],
                                                 qT2[r0:r0+64, 512*qg:512*(qg+1)],
                                                 start=True, stop=True)
                        pev = ctx.inc("pe", i)
                        ctx.wait(nc.scalar, "pe", pev)
                        A.activation(pT[:, 1024*pb:1024*(pb+1)], ps_s[sb][:],
                                     AF.Exp, scale=0.125)
                        act_exp[kp] = ctx.val("act")

                    def emit_PV(kp):
                        pb = kp % 4
                        ctx.wait(nc.tensor, "act", act_exp[kp])
                        for u in range(2):
                            kt = 2*kp + u
                            i = nc.tensor.matmul(ps_pv[:], vtok[hh][:, 65*kt:65*(kt+1)],
                                                 pT[:, 1024*pb + 512*u: 1024*pb + 512*(u+1)],
                                                 start=(kt == 0), stop=(kt == NT - 1))
                        pv_pe[kp] = ctx.inc("pe", i)
                        return pv_pe[kp]

                    emit_S(0)
                    if norm is not None:
                        ctx.wait(nc.vector, "pe", norm[0])
                        V.reciprocal(rrow[:], ps_pv[64:65, :])
                        V.tensor_copy(pvs[0:64, :], ps_pv[0:64, :])
                        dv_copy = ctx.val("dve")
                    emit_S(1)
                    if norm is not None:
                        ctx.wait(nc.tensor, "dve", dv_copy)
                    emit_PV(0)
                    if norm is not None:
                        i = nc.tensor.matmul(ps_pb[:], C["ones1_f32"][:, 0:64],
                                             rrow[:], start=True, stop=True)
                        pev = ctx.inc("pe", i)
                        ctx.wait(nc.vector, "pe", pev)
                        V.tensor_mul(att[p][r0:r0+64, 512*norm[1]:512*(norm[1]+1)],
                                     pvs[0:64, :], ps_pb[:])
                    for kp in range(2, NT // 2):
                        emit_S(kp)
                        emit_PV(kp - 1)
                    norm = (emit_PV(NT // 2 - 1), qg)
                ctx.wait(nc.vector, "pe", norm[0])
                V.reciprocal(rrow[:], ps_pv[64:65, :])
                dv = ctx.val("dve")
                V.tensor_copy(pvs[0:64, :], ps_pv[0:64, :])
                ctx.wait(nc.tensor, "dve", dv)
                i = nc.tensor.matmul(ps_pb[:], C["ones1_f32"][:, 0:64], rrow[:],
                                     start=True, stop=True)
                pev = ctx.inc("pe", i)
                ctx.wait(nc.vector, "pe", pev)
                V.tensor_mul(att[p][r0:r0+64, 512*norm[1]:512*(norm[1]+1)],
                             pvs[0:64, :], ps_pb[:])
                ctx.wait(nc.tensor, "dve", ctx.val("dve"))
        ctx.barrier()
        for g in range(8):
            xo = (g % 2) * NC * 512
            if g >= 2:
                ctx.wait(nc.sync, "dve", ctx.val("dve"))
                ctx.wait(nc.sync, "qs", ctx.val("qs"))
            for c in range(NC):
                ctx.inc("qs", nc.sync.dma_start(
                    out=xres[:, xo + 512*c: xo + 512*(c+1)],
                    in_=xsrc[128*c:128*(c+1), 512*g:512*(g+1)]), 16)
            ctx.wait(nc.vector, "qs")
            if g == 0:
                ctx.wait(nc.tensor, "qg", ctx.val("qg"))
            for dm in range(NC):
                ctx.wait(nc.tensor, "dve", ctx.val("dve"))
                for hd in range(npair):
                    i = nc.tensor.matmul(ps_qkv[:],
                                         woutb[:, 1024*dm + 128*hd: 1024*dm + 128*(hd+1)],
                                         att[hd][:, 512*g:512*(g+1)],
                                         start=(hd == 0), stop=(hd == npair - 1))
                pev = ctx.inc("pe", i)
                ctx.wait(nc.vector, "pe", pev)
                XR = xres[:, xo + 512*dm: xo + 512*(dm+1)]
                V.tensor_scalar(pvs[:], ps_qkv[:], bout[:, dm:dm+1], None, ALU.add)
                V.tensor_add(XR, XR, pvs[:])
                dv = ctx.val("dve")
                ctx.wait(nc.sync, "dve", dv)
                ctx.inc("qs", nc.sync.dma_start(
                    out=xdst[128*dm:128*(dm+1), 512*g:512*(g+1)], in_=XR), 16)


def emit_l0_sel(ctx, sc, C, aps, xln, cc_in, cc_out, sel05_tm, sel05T, ncores=8,
                Kcap=2048):
    nc, V, A = ctx.nc, ctx.V, ctx.A
    with ctx.scope() as ph:
        wg = ph.sb("s0_wg", [128, 8 * NC], F32)
        bgr = ph.sb("s0_bg", [1, 8], F32)
        xg = ph.sb("s0_xg", [128, 2 * NC * 512], F32)
        Lg = ph.sb("s0_Lg", [128, 256], F32)
        tA = ph.sb("s0_tA", [128, 256], F32)
        tB = ph.sb("s0_tB", [128, 256], F32)
        tC = ph.sb("s0_tC", [128, 256], F32)
        sm = ph.sb("s0_sm", [128, 6 * 32], F32)
        mT = ph.sb("s0_mT", [8, S], F32)
        routeT = ph.sb("s0_rt", [8, S], F32)
        scr = ph.sb("s0_scr", [8, S], F32)
        tieT = ph.sb("s0_tie", [8, S], F32)
        ones8 = ph.sb("s0_on", [8, S], F32)
        bis = ph.sb("s0_bis", [8, 10], F32)
        pgall = ph.ps("s0_pg", [128, 256])
        pt = ph.ps("s0_pt", [8, 512])
        ptB = ph.ps("s0_ptB", [128, 32])
        SM = lambda i: sm[:, 32*i:32*(i+1)]
        SMb = lambda i: sm[:, 32*i:32*(i+1)].rearrange("p (t o) -> p t o", o=1).broadcast_to([128, 32, 8])
        E3 = lambda ap: ap.rearrange("p (t e) -> p t e", e=8)
        B1 = lambda i: bis[:, i:i+1]
        ctx.inc("qs", nc.sync.dma_start(out=wg[:], in_=aps["wg0"]), 16)
        ctx.inc("qs", nc.sync.dma_start(out=bgr[:], in_=aps["bg0"]), 16)
        gpe = {}
        for g in range(8):
            sl = f"s0x{g % 2}"
            off = (g % 2) * NC * 512
            X = lambda c: xg[:, off + 512*c: off + 512*(c+1)]
            if g >= 2:
                ctx.wait(nc.sync, "pe", gpe[g - 2])
            for c in range(NC):
                ctx.inc(sl, nc.sync.dma_start(
                    out=X(c), in_=xln[128*c:128*(c+1), 512*g:512*(g+1)]), 16)
            ctx.wait(nc.tensor, sl)
            ctx.wait(nc.tensor, "qs")
            for j4 in range(4):
                j = 4*g + j4
                for c in range(NC):
                    nc.tensor.matmul(pgall[:, 8*j:8*(j+1)],
                                     xg[:, off + 512*c + 128*j4: off + 512*c + 128*j4 + 128],
                                     wg[:, 8*c:8*(c+1)], start=(c == 0), stop=False)
                i = nc.tensor.matmul(pgall[:, 8*j:8*(j+1)], C["ones1_f32"][:], bgr[:],
                                     start=False, stop=True)
            gpe[g] = ctx.inc("pe", i)
        ctx.wait(nc.vector, "pe", gpe[7])
        V.tensor_copy(Lg[:], pgall[:])
        V.tensor_reduce(SM(0), E3(Lg[:]), AX.X, ALU.max)
        V.tensor_tensor(E3(tA[:]), E3(Lg[:]), SMb(0), ALU.subtract)
        ctx.wait(nc.scalar, "dve")
        A.activation(tA[:], tA[:], AF.Exp)
        ctx.wait(nc.vector, "act")
        V.tensor_reduce(SM(1), E3(tA[:]), AX.X, ALU.add)
        V.reciprocal(SM(2), SM(1))
        V.tensor_tensor(E3(tA[:]), E3(tA[:]), SMb(2), ALU.mult)
        V.tensor_tensor(E3(tB[:]), E3(Lg[:]), SMb(0), ALU.is_ge)
        V.scalar_tensor_tensor(tB[:], tB[:], -1000.0, C["iotaT"][:], ALU.mult, ALU.add)
        V.tensor_reduce(SM(3), E3(tB[:]), AX.X, ALU.min)
        V.tensor_tensor(E3(tB[:]), E3(C["iotam1000T"][:]), SMb(3), ALU.is_equal)
        V.scalar_tensor_tensor(tC[:], tB[:], NEG, Lg[:], ALU.mult, ALU.add)
        V.tensor_reduce(SM(4), E3(tC[:]), AX.X, ALU.max)
        V.tensor_tensor(E3(tC[:]), E3(tC[:]), SMb(4), ALU.is_ge)
        V.scalar_tensor_tensor(tC[:], tC[:], -1000.0, C["iotaT"][:], ALU.mult, ALU.add)
        V.tensor_reduce(SM(5), E3(tC[:]), AX.X, ALU.min)
        V.tensor_tensor(E3(tC[:]), E3(C["iotam1000T"][:]), SMb(5), ALU.is_equal)
        V.tensor_add(tB[:], tB[:], tC[:])
        V.tensor_mul(tA[:], tA[:], tB[:])
        ctx.wait(nc.tensor, "dve")
        for j4 in range(NT // 4):
            if j4 >= 1:
                ctx.wait(nc.tensor, "dve", ctx.val("dve"))
            for u in range(4):
                j = 4*j4 + u
                i = nc.tensor.transpose(pt[:, 128*u:128*(u+1)], tA[:, 8*j:8*(j+1)],
                                        C["ident_f32"][:])
            pev = ctx.inc("pe", i)
            ctx.wait(nc.vector, "pe", pev)
            for u in range(4):
                j = 4*j4 + u
                (V if u == 0 else ctx.Vr).tensor_copy(mT[:, 128*j:128*(j+1)],
                                                      pt[:, 128*u:128*(u+1)])
        ctx.wait(nc.sync, "dve")
        ctx.inc("qs", nc.sync.dma_start(out=cc_in, in_=mT[:]), 16)
        ctx.wait(nc.gpsimd, "qs")
        ctx.inc("cc", nc.gpsimd.collective_compute(
            "AllReduce", ALU.add, [list(range(ncores))],
            [cc_in.opt()], [cc_out.opt()]), 1)
        ctx.wait(nc.sync, "cc")
        ctx.inc("qs", nc.sync.dma_start(out=scr[:], in_=cc_out), 16)
        ctx.wait(nc.vector, "qs")
        V.tensor_scalar_add(scr[:], scr[:], 1e-6)
        V.reciprocal(scr[:], scr[:])
        V.tensor_mul(routeT[:], mT[:], scr[:])
        V.tensor_scalar_mul(routeT[:], routeT[:], float(ncores))
        V.memset(bis[:], 0.0)
        V.memset(bis[:, 1:2], 8.5)
        for it in range(34):
            V.tensor_add(B1(2), B1(0), B1(1))
            V.tensor_scalar_mul(B1(2), B1(2), 0.5)
            V.tensor_scalar(scr[:], routeT[:], B1(2), None, ALU.is_gt,
                            op1=ALU.add, accum_out=B1(3))
            V.tensor_scalar(B1(4), B1(3), float(Kcap), None, ALU.is_ge)
            V.tensor_sub(B1(5), B1(2), B1(0))
            V.scalar_tensor_tensor(B1(0), B1(5), B1(4), B1(0), ALU.mult, ALU.add)
            V.tensor_sub(B1(5), B1(1), B1(2))
            V.scalar_tensor_tensor(B1(1), B1(5), B1(4), B1(2), ALU.mult, ALU.add)
        V.tensor_scalar(scr[:], routeT[:], B1(1), None, ALU.is_le)
        V.tensor_mul(scr[:], scr[:], routeT[:])
        V.tensor_reduce(B1(6), scr[:], AX.X, ALU.max)
        V.tensor_scalar(scr[:], routeT[:], B1(6), None, ALU.is_gt,
                        op1=ALU.add, accum_out=B1(7))
        V.tensor_scalar(B1(7), B1(7), -1.0, float(Kcap), ALU.mult, ALU.add)
        V.tensor_scalar(tieT[:], routeT[:], B1(6), None, ALU.is_equal)
        V.memset(ones8[:], 1.0)
        V.tensor_tensor_scan(routeT[:], ones8[:], tieT[:], 0.0, ALU.mult, ALU.add)
        V.tensor_scalar(routeT[:], routeT[:], B1(7), None, ALU.is_le)
        V.tensor_mul(tieT[:], tieT[:], routeT[:])
        V.tensor_add(sel05T[:], scr[:], tieT[:])
        V.tensor_scalar_mul(sel05T[:], sel05T[:], 0.5)
        ctx.wait(nc.tensor, "dve")
        for j4 in range(NT // 4):
            if j4 >= 1:
                ctx.wait(nc.tensor, "dve", ctx.val("dve"))
            for u in range(4):
                j = 4*j4 + u
                i = nc.tensor.transpose(ptB[:, 8*u:8*(u+1)], sel05T[:, 128*j:128*(j+1)],
                                        C["ident_f32"][0:8, 0:8])
            pev = ctx.inc("pe", i)
            ctx.wait(nc.vector, "pe", pev)
            for u in range(4):
                j = 4*j4 + u
                (V if u == 0 else ctx.Vr).tensor_copy(sel05_tm[:, 8*j:8*(j+1)],
                                                      ptB[:, 8*u:8*(u+1)])


def emit_l0_experts(ctx, sc, C, aps, xln, sel05_tm, sel05T, xsrc, xdst,
                    ntset=8, nexp=EXP):
    nc, V, A = ctx.nc, ctx.V, ctx.A
    with ctx.scope() as ph:
        b1 = ph.sb("e0_b1", [128, EXP * 2], F32)
        b2 = ph.sb("e0_b2", [8, 1024], F32)
        w1r = ph.sb("e0_w1", [128, EXP * 2 * 1024], BF16)
        w2r = ph.sb("e0_w2", [128, EXP * 2 * 1024], BF16)
        xln_ts = ph.sb("e0_xts", [128, 2 * NC * 512], BF16)
        xres = ph.sb("e0_xres", [128, 2 * NC * 512], F32)
        Hgel = ph.sb("e0_H", [128, 2 * 512], BF16)
        out_acc = ph.sb("e0_oa", [128, 4 * 1024], F32)
        py = [ph.ps(f"e0_py{j}", [128, 512]) for j in range(8)]
        ctx.inc("qs", nc.sync.dma_start(out=b1[:], in_=aps["b1l0"]), 16)
        ctx.inc("qs", nc.sync.dma_start(out=b2[:], in_=aps["b2l0"]), 16)
        ctx.inc("qg", nc.gpsimd.dma_start(out=w1r[:], in_=aps["w1l0r"]), 16)
        ctx.inc("qg", nc.gpsimd.dma_start(out=w2r[:], in_=aps["w2l0r"]), 16)
        ctx.wait(nc.tensor, "qg", ctx.val("qg"))
        def tset_loads0(t):
            if t >= ntset:
                return (("qs", 0), ("qs", 0))
            to = 512 * t
            xo = (t % 2) * NC * 512
            ctx.wait(nc.gpsimd, "pe", ctx.val("pe"))
            ctx.wait(nc.sync, "dve", ctx.val("dve"))
            ctx.wait(nc.sync, "qs", ctx.val("qs"))
            xs, ps = f"e0x{t % 2}", f"e0p{t % 2}"
            for c in range(NC):
                ctx.inc(xs, nc.gpsimd.dma_start(
                    out=xln_ts[:, xo + 512*c: xo + 512*(c+1)],
                    in_=xln[128*c:128*(c+1), to:to+512]), 16)
                ctx.inc(ps, nc.sync.dma_start(
                    out=xres[:, xo + 512*c: xo + 512*(c+1)],
                    in_=xsrc[128*c:128*(c+1), to:to+512]), 16)
            return ((xs, ctx.val(xs)), (ps, ctx.val(ps)))

        for eng in (nc.sync, nc.gpsimd):
            for sname in ("dve", "act", "pe", "qs"):
                if sname in ctx.sems and ctx.val(sname):
                    ctx.wait(eng, sname, ctx.val(sname))
        ld_tok = {0: tset_loads0(0)}
        for t in range(ntset):
            to = 512 * t
            xo = (t % 2) * NC * 512
            X0 = lambda c: xln_ts[:, xo + 512*c: xo + 512*(c+1)]
            XRES0 = lambda c: xres[:, xo + 512*c: xo + 512*(c+1)]
            ctx.wait(nc.tensor, "dve", ctx.val("dve"))
            if t == 0:
                ctx.wait(nc.tensor, "qs", ctx.val("qs"))
            for j8 in range(8):
                j, dg = divmod(j8, 2)
                jj = 4 * t + j
                i = nc.tensor.matmul(py[j8][:], sel05T[:, 128*jj:128*(jj+1)],
                                     b2[:, 512*dg:512*(dg+1)], start=True, stop=True)
            pev = ctx.inc("pe", i)
            ctx.wait(nc.vector, "pe", pev)
            for j8 in range(8):
                j, dg = divmod(j8, 2)
                (V if j8 == 0 else ctx.Vr).tensor_copy(
                    out_acc[:, 1024*j + 512*dg: 1024*j + 512*(dg+1)], py[j8][:])
            ctx.wait(nc.tensor, "dve", ctx.val("dve"))
            ctx.wait(nc.tensor, *ld_tok[t][0])
            for e in range(nexp):
                if e == 1:
                    ld_tok[t + 1] = tset_loads0(t + 1)
                act_after = {}
                for hc in range(2):
                    if e + hc > 0:
                        ctx.wait(nc.tensor, "act", ctx.val("act") - (1 if hc == 0 else 0))
                    for kc in range(NC):
                        i = nc.tensor.matmul(
                            py[hc][:],
                            w1r[:, 2048*e + 1024*hc + 128*kc: 2048*e + 1024*hc + 128*(kc+1)],
                            X0(kc), start=(kc == 0), stop=(kc == NC - 1))
                    pev = ctx.inc("pe", i)
                    ctx.wait(nc.scalar, "pe", pev)
                    (A if hc == 0 else ctx.Ar).activation(
                        Hgel[:, 512*hc:512*(hc+1)], py[hc][:], AF.Gelu,
                        bias=b1[:, 2*e+hc: 2*e+hc+1])
                    act_after[hc] = ctx.val("act")
                ctx.wait(nc.tensor, "act", act_after[1])
                for dg in range(2):
                    for hc in range(2):
                        for j in range(4):
                            i = nc.tensor.matmul(
                                py[4*dg + j][:],
                                Hgel[:, 512*hc + 128*j: 512*hc + 128*(j+1)],
                                w2r[:, 2048*e + 1024*hc + 512*dg: 2048*e + 1024*hc + 512*(dg+1)],
                                start=(hc == 0), stop=(hc == 1))
                    pev = ctx.inc("pe", i)
                    ctx.wait(nc.vector, "pe", pev)
                    for j in range(4):
                        jj = 4 * t + j
                        sl = out_acc[:, 1024*j + 512*dg: 1024*j + 512*(dg+1)]
                        (V if j == 0 else ctx.Vr).scalar_tensor_tensor(
                            sl, py[4*dg + j][:], sel05_tm[:, 8*jj+e: 8*jj+e+1],
                            sl, ALU.mult, ALU.add)
                ctx.wait(nc.tensor, "dve", ctx.val("dve"))
            ctx.wait(nc.vector, *ld_tok[t][1])
            grp_dve = {}
            for grp in range(8):
                tiles = [(j, dc) for j in range(4) for dc in range(NC)][4*grp:4*grp+4]
                if grp >= 1:
                    ctx.wait(nc.tensor, "dve", grp_dve.get(grp - 1, 0))
                for b4, (j, dc) in enumerate(tiles):
                    i = nc.tensor.transpose(
                        py[b4][:, 0:128],
                        out_acc[:, 1024*j + 128*dc: 1024*j + 128*(dc+1)],
                        C["ident_f32"][:])
                pev = ctx.inc("pe", i)
                ctx.wait(nc.vector, "pe", pev)
                for b4, (j, dc) in enumerate(tiles):
                    dst = xres[:, xo + 512*dc + 128*j: xo + 512*dc + 128*(j+1)]
                    (V if b4 == 0 else ctx.Vr).tensor_add(dst, py[b4][:, 0:128], dst)
                grp_dve[grp] = ctx.val("dve")
            dv = ctx.val("dve")
            ctx.wait(nc.sync, "dve", dv)
            for c in range(NC):
                ctx.inc("qs", nc.sync.dma_start(
                    out=xdst[128*c:128*(c+1), to:to+512], in_=XRES0(c)), 16)


RQK = 1.0 / float(np.sqrt(96.0))
CLIP_HI = float(np.exp(80.0 / np.sqrt(96.0)))
CLIP_LO = float(np.exp(-80.0 / np.sqrt(96.0)))

def emit_l0_attn(ctx, sc, C, aps, xln, xsrc, xdst, npair=8, nqg=8):
    nc, V, A = ctx.nc, ctx.V, ctx.A
    with ctx.scope() as ph:
        C2 = load_attn_consts(ctx, ph, aps, rope=True)
        cqT = ph.sb("a0_cq", [128, 2 * S], BF16)
        ckvT = ph.sb("a0_ckv", [128, 2 * S], BF16)
        ropb = ph.sb("a0_ropb", [32, 4 * S], BF16)
        att = [ph.sb(f"a0_att{i}", [128, S], BF16) for i in range(8)]
        rrow = ph.sb("a0_rr", [1, 512], F32)
        ps_j = [ph.ps(f"a0_pj{i}", [128, 512]) for i in range(2)]
        ps_kb = ph.ps("a0_pkb", [64, 512])
        ps_k = ps_kb[0:32, :]
        ps_pb = ps_kb[:]
        ps_s = [ph.ps(f"a0_ps{i}", [128, 1024]) for i in range(2)]
        ps_pv = ph.ps("a0_pv", [65, 512])
        ropesrc = ropb[:, 0:S]
        krT = ropb[:, S:2*S]
        cosT = ropb[:, 2*S:3*S]
        sinT = ropb[:, 3*S:4*S]
        ctx.wait(nc.gpsimd, "qg", ctx.val("qg"))
        ctx.inc("qg", nc.gpsimd.dma_start(out=cosT, in_=aps["c_ropecs"][:, 0:S]), 16)
        ctx.inc("qg", nc.gpsimd.dma_start(out=sinT, in_=aps["c_ropecs"][:, S:2*S]), 16)
        ctx.wait(nc.vector, "qg", ctx.val("qg"))

        def rope_inplace(dst, rsc):
            for g in range(8):
                ctx.wait(nc.tensor, "dve", ctx.val("dve"))
                i = nc.tensor.matmul(ps_k, C2["pswap"][:],
                                     ropesrc[:, 512*g:512*(g+1)], start=True, stop=True)
                pev = ctx.inc("pe", i)
                ctx.wait(nc.vector, "pe", pev)
                V.tensor_mul(rsc[:, 512*g:512*(g+1)], ps_k[:],
                             sinT[:, 512*g:512*(g+1)])
            V.tensor_mul(ropesrc[:], ropesrc[:], cosT[:])
            V.tensor_add(dst, ropesrc[:], rsc[:])

        with ctx.scope() as p0:
            xbf = p0.sb("a0_xbf", [128, 2 * NC * 512], BF16)
            wdqr = p0.sb("a0_wdq", [128, 16 * 128], BF16)
            wdkvr = p0.sb("a0_wdkv", [128, 16 * 128], BF16)
            wkrr = p0.sb("a0_wkr", [128, 8 * 32], BF16)
            krp = ropesrc
            ctx.wait(nc.gpsimd, "qg", ctx.val("qg"))
            ctx.inc("qg", nc.gpsimd.dma_start(
                out=wdqr[:].rearrange("k (m c f) -> k m c f", m=2, c=8),
                in_=aps["wdq"].rearrange("m c k f -> k m c f")), 16)
            ctx.inc("qg", nc.gpsimd.dma_start(
                out=wdkvr[:].rearrange("k (m c f) -> k m c f", m=2, c=8),
                in_=aps["wdkv"].rearrange("m c k f -> k m c f")), 16)
            ctx.inc("qg", nc.gpsimd.dma_start(
                out=wkrr[:].rearrange("k (c f) -> k c f", c=8),
                in_=aps["wkr"].rearrange("o c k f -> k (o c) f")), 16)
            ctx.wait(nc.tensor, "qg", ctx.val("qg"))
            bank_dve = {}
            for g in range(8):
                sl = f"a0x{g % 2}"
                xo = (g % 2) * NC * 512
                if g >= 2:
                    ctx.wait(nc.gpsimd, "pe", ctx.val("pe"))
                for c in range(NC):
                    ctx.inc(sl, nc.gpsimd.dma_start(
                        out=xbf[:, xo + 512*c: xo + 512*(c+1)],
                        in_=xln[128*c:128*(c+1), 512*g:512*(g+1)]), 16)
                ctx.wait(nc.tensor, sl)
                for q4, (lc, (w, dstb)) in enumerate(
                        [(lc, wd) for lc in range(2)
                         for wd in ((wdqr, cqT), (wdkvr, ckvT))]):
                    bk = q4 % 2
                    pj = ps_j[bk]
                    if bk in bank_dve:
                        ctx.wait(nc.tensor, "dve", bank_dve[bk])
                    for kc in range(NC):
                        i = nc.tensor.matmul(
                            pj[:], w[:, (lc*8+kc)*128:(lc*8+kc)*128+128],
                            xbf[:, xo + 512*kc: xo + 512*(kc+1)],
                            start=(kc == 0), stop=(kc == NC - 1))
                    pev = ctx.inc("pe", i)
                    ctx.wait(nc.vector, "pe", pev)
                    V.tensor_copy(dstb[:, S*lc + 512*g: S*lc + 512*(g+1)], pj[:])
                    bank_dve[bk] = ctx.val("dve")
                if "k" in bank_dve:
                    ctx.wait(nc.tensor, "dve", bank_dve["k"])
                for kc in range(NC):
                    i = nc.tensor.matmul(ps_k, wkrr[:, 32*kc:32*(kc+1)],
                                         xbf[:, xo + 512*kc: xo + 512*(kc+1)],
                                         start=(kc == 0), stop=(kc == NC - 1))
                pev = ctx.inc("pe", i)
                ctx.wait(nc.vector, "pe", pev)
                V.tensor_copy(krp[:, 512*g:512*(g+1)], ps_k)
                bank_dve["k"] = ctx.val("dve")
            rope_inplace(krT, xbf[0:32, 0:S])
            ctx.wait(nc.tensor, "dve", ctx.val("dve"))
        with ctx.scope() as p1:
            wuqr = p1.sb("a0_wuq", [128, 16 * 128], BF16)
            wukr = p1.sb("a0_wuk", [128, 16 * 128], BF16)
            wqrr = p1.sb("a0_wqr", [128, 8 * 128], BF16)
            wuvr = p1.sb("a0_wuv", [128, 2 * 1024], BF16)
            qh = [p1.sb(f"a0_qh{i}", [128, S], BF16) for i in range(2)]
            kh = [p1.sb(f"a0_kh{i}", [128, S], BF16) for i in range(2)]
            vtok = [p1.sb(f"a0_vt{i}", [128, 32 * 65], BF16) for i in range(2)]
            pT = p1.sb("a0_pT", [128, 4 * 1024], BF16)
            pvs = p1.sb("a0_pvs", [128, 512], F32)
            qrp = ropesrc
            ctx.wait(nc.gpsimd, "qg", ctx.val("qg"))
            ctx.inc("qg", nc.gpsimd.dma_start(
                out=wuqr[:].rearrange("k (m c f) -> k m c f", m=8, c=2),
                in_=aps["wuq"].rearrange("m c k f -> k m c f")), 16)
            ctx.inc("qg", nc.gpsimd.dma_start(
                out=wukr[:].rearrange("k (m c f) -> k m c f", m=8, c=2),
                in_=aps["wuk"].rearrange("m c k f -> k m c f")), 16)
            ctx.inc("qg", nc.gpsimd.dma_start(
                out=wqrr[:].rearrange("k (m c f) -> k m c f", m=4, c=2),
                in_=aps["wqr"].rearrange("m c k f -> k m c f")), 16)
            ctx.inc("qg", nc.gpsimd.dma_start(out=wuvr[:], in_=aps["wuvr"]), 16)
            ctx.wait(nc.tensor, "qg", ctx.val("qg"))
            ctx.wait(nc.vector, "qg", ctx.val("qg"))
            V.memset(vtok[0][:], 1.0)
            V.memset(vtok[1][:], 1.0)
            for p in range(npair):
                bank_dve = {}
                for hh in range(2):
                    h = 2*p + hh
                    for w, dstb in ((wuqr, qh[hh]), (wukr, kh[hh])):
                        for g in range(8):
                            bk = g % 2
                            pj = ps_j[bk]
                            if bk in bank_dve:
                                ctx.wait(nc.tensor, "dve", bank_dve[bk])
                            for lc in range(2):
                                i = nc.tensor.matmul(
                                    pj[0:64, :],
                                    w[:, ((h//2)*2+lc)*128 + 64*(h % 2):
                                         ((h//2)*2+lc)*128 + 64*(h % 2) + 64],
                                    cqT[:, S*lc + 512*g: S*lc + 512*(g+1)] if w is wuqr
                                    else ckvT[:, S*lc + 512*g: S*lc + 512*(g+1)],
                                    start=(lc == 0), stop=(lc == 1))
                            pev = ctx.inc("pe", i)
                            ctx.wait(nc.vector, "pe", pev)
                            V.tensor_copy(dstb[0:64, 512*g:512*(g+1)], pj[0:64, :])
                            bank_dve[bk] = ctx.val("dve")
                    for g in range(8):
                        if "k" in bank_dve:
                            ctx.wait(nc.tensor, "dve", bank_dve["k"])
                        for lc in range(2):
                            i = nc.tensor.matmul(
                                ps_k,
                                wqrr[:, ((h//4)*2+lc)*128 + 32*(h % 4):
                                       ((h//4)*2+lc)*128 + 32*(h % 4) + 32],
                                cqT[:, S*lc + 512*g: S*lc + 512*(g+1)],
                                start=(lc == 0), stop=(lc == 1))
                        pev = ctx.inc("pe", i)
                        ctx.wait(nc.vector, "pe", pev)
                        V.tensor_copy(qrp[:, 512*g:512*(g+1)], ps_k)
                        bank_dve["k"] = ctx.val("dve")
                    rope_inplace(qh[hh][64:96, :], pT[0:32, 0:S])
                    bank_dve["k"] = ctx.val("dve")
                    V.tensor_copy(kh[hh][64:96, :], krT[:])
                    for st in range(NT):
                        bk = st % 2
                        pj = ps_j[bk]
                        if bk in bank_dve:
                            ctx.wait(nc.tensor, "dve", bank_dve[bk])
                        for lc in range(2):
                            i = nc.tensor.matmul(
                                pj[:, 0:64],
                                ckvT[:, S*lc + 128*st: S*lc + 128*(st+1)],
                                wuvr[:, 1024*lc + 64*h: 1024*lc + 64*h + 64],
                                start=(lc == 0), stop=(lc == 1))
                        pev = ctx.inc("pe", i)
                        ctx.wait(nc.vector, "pe", pev)
                        V.tensor_copy(vtok[hh][:, 65*st:65*st+64], pj[:, 0:64])
                        bank_dve[bk] = ctx.val("dve")
                for hh in range(2):
                    h = 2*p + hh
                    norm = None
                    for qg in range(nqg):
                        act_exp = {}
                        pv_pe = {}
                        dve_fix = {}
                        kt_max = 4*qg + 3
                        npairk = (kt_max + 1) // 2

                        def emit_S(kp):
                            sb = kp % 2
                            pb = kp % 4
                            if kp >= 2:
                                ctx.wait(nc.scalar, "dve", dve_fix[kp - 2])
                            if kp >= 4:
                                ctx.wait(nc.scalar, "pe", pv_pe.get(kp - 4, 0))
                            for u in range(2):
                                kt = 2*kp + u
                                masks = []
                                if kt >= 4*qg:
                                    for jq in range(4):
                                        qt = 4*qg + jq
                                        if qt < kt:
                                            masks.append((jq, C2["fullneg_bf"]))
                                        elif qt == kt:
                                            masks.append((jq, C2["trimask_bf"]))
                                i = nc.tensor.matmul(ps_s[sb][:, 512*u:512*(u+1)],
                                                     kh[hh][0:96, 128*kt:128*(kt+1)],
                                                     qh[hh][0:96, 512*qg:512*(qg+1)],
                                                     start=True, stop=(not masks))
                                for mi, (jq, mk) in enumerate(masks):
                                    slp = ps_s[sb][:, 512*u + 128*jq: 512*u + 128*(jq+1)]
                                    i = nc.tensor.matmul(slp, C["ident_bf"][:], mk[:],
                                                         start=False,
                                                         stop=(mi == len(masks) - 1))
                            pev = ctx.inc("pe", i)
                            ctx.wait(nc.scalar, "pe", pev)
                            A.activation(pT[:, 1024*pb:1024*(pb+1)], ps_s[sb][:],
                                         AF.Exp, scale=RQK)
                            act_exp[kp] = ctx.val("act")
                            ctx.wait(nc.vector, "act", act_exp[kp])
                            V.tensor_scalar(pT[:, 1024*pb:1024*(pb+1)],
                                            pT[:, 1024*pb:1024*(pb+1)],
                                            CLIP_HI, None, ALU.min)
                            dve_fix[kp] = ctx.val("dve")

                        def emit_PV(kp):
                            pb = kp % 4
                            ctx.wait(nc.tensor, "dve", dve_fix[kp])
                            for u in range(2):
                                kt = 2*kp + u
                                i = nc.tensor.matmul(ps_pv[:], vtok[hh][:, 65*kt:65*(kt+1)],
                                                     pT[:, 1024*pb + 512*u: 1024*pb + 512*(u+1)],
                                                     start=(kt == 0), stop=(kt == kt_max))
                            pv_pe[kp] = ctx.inc("pe", i)
                            return pv_pe[kp]

                        emit_S(0)
                        if norm is not None:
                            ctx.wait(nc.vector, "pe", norm[0])
                            V.reciprocal(rrow[:], ps_pv[64:65, :])
                            V.tensor_copy(pvs[0:64, :], ps_pv[0:64, :])
                            dv_copy = ctx.val("dve")
                        emit_S(1)
                        if norm is not None:
                            ctx.wait(nc.tensor, "dve", dv_copy)
                        emit_PV(0)
                        if norm is not None:
                            i = nc.tensor.matmul(ps_pb, C["ones1_f32"][:, 0:64],
                                                 rrow[:], start=True, stop=True)
                            pev = ctx.inc("pe", i)
                            ctx.wait(nc.vector, "pe", pev)
                            V.tensor_mul(att[p][64*hh:64*hh+64,
                                                512*norm[1]:512*(norm[1]+1)],
                                         pvs[0:64, :], ps_pb)
                        for kp in range(2, npairk):
                            emit_S(kp)
                            emit_PV(kp - 1)
                        norm = (emit_PV(npairk - 1), qg)
                    ctx.wait(nc.vector, "pe", norm[0])
                    V.reciprocal(rrow[:], ps_pv[64:65, :])
                    dv = ctx.val("dve")
                    V.tensor_copy(pvs[0:64, :], ps_pv[0:64, :])
                    ctx.wait(nc.tensor, "dve", dv)
                    i = nc.tensor.matmul(ps_pb, C["ones1_f32"][:, 0:64], rrow[:],
                                         start=True, stop=True)
                    pev = ctx.inc("pe", i)
                    ctx.wait(nc.vector, "pe", pev)
                    V.tensor_mul(att[p][64*hh:64*hh+64, 512*norm[1]:512*(norm[1]+1)],
                                 pvs[0:64, :], ps_pb)
                    ctx.wait(nc.tensor, "dve", ctx.val("dve"))
        with ctx.scope() as p2:
            xres = p2.sb("a0_xres", [128, NC * 512], F32)
            wor = p2.sb("a0_wo", [128, 64 * 128], BF16)
            ctx.wait(nc.gpsimd, "qg", ctx.val("qg"))
            ctx.inc("qg", nc.gpsimd.dma_start(
                out=wor[:].rearrange("k (m c f) -> k m c f", m=8, c=8),
                in_=aps["wo"].rearrange("m c k f -> k m c f")), 16)
            for g in range(8):
                if g >= 1:
                    ctx.wait(nc.sync, "dve", ctx.val("dve"))
                    ctx.wait(nc.sync, "qs", ctx.val("qs"))
                for c in range(NC):
                    ctx.inc("qs", nc.sync.dma_start(
                        out=xres[:, 512*c:512*(c+1)],
                        in_=xsrc[128*c:128*(c+1), 512*g:512*(g+1)]), 16)
                ctx.wait(nc.vector, "qs")
                if g == 0:
                    ctx.wait(nc.tensor, "qg", ctx.val("qg"))
                for dm in range(NC):
                    ctx.wait(nc.tensor, "dve", ctx.val("dve"))
                    for hd in range(npair):
                        i = nc.tensor.matmul(ps_j[0][:],
                                             wor[:, 1024*dm + 128*hd: 1024*dm + 128*(hd+1)],
                                             att[hd][:, 512*g:512*(g+1)],
                                             start=(hd == 0), stop=(hd == npair - 1))
                    pev = ctx.inc("pe", i)
                    ctx.wait(nc.vector, "pe", pev)
                    XR = xres[:, 512*dm:512*(dm+1)]
                    V.tensor_add(XR, XR, ps_j[0][:])
                    dv = ctx.val("dve")
                    ctx.wait(nc.sync, "dve", dv)
                    ctx.inc("qs", nc.sync.dma_start(
                        out=xdst[128*dm:128*(dm+1), 512*g:512*(g+1)], in_=XR), 16)


def build_sub3(H, debug=False, phases=("ln", "gate", "experts")):
    nc = bass.Bass("TRN2", target_bir_lowering=False)
    aps = declare_inputs(nc, H, KEYS_SUB3)
    xT = nc.dram_tensor("xT", [D, S], F32, kind="ExternalInput").ap()
    xout = nc.dram_tensor("xout", [D, S], F32, kind="ExternalOutput").ap()
    xlnd = nc.dram_tensor("xln", [D, S], F32,
                          kind="ExternalOutput" if debug else "Internal").ap()
    dbg = {}
    if debug:
        dbg["comb_dbg"] = nc.dram_tensor("comb_dbg", [128, 8 * NT], F32, kind="ExternalOutput").ap()
    with ExitStack() as es:
        ctx = Ctx(nc, es)
        blk = es.enter_context(nc.Block())
        with ctx.scope() as sc:
            C = load_consts(ctx, sc, aps)
            comb_tm = sc.sb("comb_tm", [128, 8 * NT], F32)
            combT = sc.sb("combT", [8, S], F32)
            if "ln" in phases:
                emit_ln(ctx, sc, C, xT, xlnd)
            if "gate" in phases:
                emit_l1_gate(ctx, sc, C, aps, xlnd, comb_tm, combT)
            if "experts" in phases:
                kw = {}
                for p in phases:
                    if p.startswith("k:"):
                        k, v = p[2:].split("=")
                        kw[k] = int(v)
                emit_l1_experts(ctx, sc, C, aps, xlnd, comb_tm, combT, xT, xout, **kw)
            else:
                for c in range(NC):
                    ctx.inc("qs", nc.sync.dma_start(
                        out=xout[128*c:128*(c+1), :], in_=xlnd[128*c:128*(c+1), :]), 16)
            if debug:
                ctx.barrier()
                ctx.inc("qs", nc.sync.dma_start(out=dbg["comb_dbg"], in_=comb_tm[:]), 16)
        ctx.barrier()
    return nc

def build_sub0(H, ncores=8, phases=("ln", "sel", "experts")):
    nc = bass.Bass("TRN2", target_bir_lowering=False)
    aps = declare_inputs(nc, H, KEYS_SUB0)
    xT = nc.dram_tensor("xT", [D, S], F32, kind="ExternalInput").ap()
    xout = nc.dram_tensor("xout", [D, S], F32, kind="ExternalOutput").ap()
    xlnd = nc.dram_tensor("xln", [D, S], F32, kind="Internal").ap()
    cc_in = nc.dram_tensor("cc_in", [8, S], F32, kind="Internal").ap()
    cc_out = nc.dram_tensor("cc_out", [8, S], F32, kind="Internal").ap()
    dbg = {}
    if "debug" in phases:
        dbg["sel_dbg"] = nc.dram_tensor("sel_dbg", [8, S], F32, kind="ExternalOutput").ap()
    with ExitStack() as es:
        ctx = Ctx(nc, es)
        blk = es.enter_context(nc.Block())
        with ctx.scope() as sc:
            C = load_consts(ctx, sc, aps)
            sel05_tm = sc.sb("sel_tm", [128, 8 * NT], F32)
            sel05T = sc.sb("sel05T", [8, S], F32)
            if "ln" in phases:
                emit_ln(ctx, sc, C, xT, xlnd)
            if "sel" in phases:
                emit_l0_sel(ctx, sc, C, aps, xlnd, cc_in, cc_out, sel05_tm, sel05T,
                            ncores=ncores)
            kw = {}
            for p in phases:
                if p.startswith("k:"):
                    k, v = p[2:].split("=")
                    kw[k] = int(v)
            if "experts" in phases:
                emit_l0_experts(ctx, sc, C, aps, xlnd, sel05_tm, sel05T, xT, xout, **kw)
            if "debug" in phases:
                ctx.barrier()
                ctx.inc("qs", nc.sync.dma_start(out=dbg["sel_dbg"], in_=sel05T[:]), 16)
        ctx.barrier()
    return nc

def build_sub1(H, phases=("ln", "attn")):
    nc = bass.Bass("TRN2", target_bir_lowering=False)
    aps = declare_inputs(nc, H, KEYS_SUB1)
    xT = nc.dram_tensor("xT", [D, S], F32, kind="ExternalInput").ap()
    xout = nc.dram_tensor("xout", [D, S], F32, kind="ExternalOutput").ap()
    xlnd = nc.dram_tensor("xln", [D, S], F32, kind="Internal").ap()
    with ExitStack() as es:
        ctx = Ctx(nc, es)
        blk = es.enter_context(nc.Block())
        with ctx.scope() as sc:
            C = load_consts(ctx, sc, aps)
            if "ln" in phases:
                emit_ln(ctx, sc, C, xT, xlnd)
            kw = {}
            for p in phases:
                if p.startswith("k:"):
                    k, v = p[2:].split("=")
                    kw[k] = int(v)
            emit_l0_attn(ctx, sc, C, aps, xlnd, xT, xout, **kw)
        ctx.barrier()
    return nc

def build_sub4(H, phases=("ln", "attn")):
    nc = bass.Bass("TRN2", target_bir_lowering=False)
    aps = declare_inputs(nc, H, KEYS_SUB4)
    xT = nc.dram_tensor("xT", [D, S], F32, kind="ExternalInput").ap()
    xout = nc.dram_tensor("xout", [D, S], F32, kind="ExternalOutput").ap()
    xlnd = nc.dram_tensor("xln", [D, S], F32, kind="Internal").ap()
    with ExitStack() as es:
        ctx = Ctx(nc, es)
        blk = es.enter_context(nc.Block())
        with ctx.scope() as sc:
            C = load_consts(ctx, sc, aps)
            if "ln" in phases:
                emit_ln(ctx, sc, C, xT, xlnd)
            kw = {}
            for p in phases:
                if p.startswith("k:"):
                    k, v = p[2:].split("=")
                    kw[k] = int(v)
            emit_l1_attn(ctx, sc, C, aps, xlnd, xT, xout, **kw)
        ctx.barrier()
    return nc


def host_prepare(I):
    H = {}
    H.update(host_constants())
    H["wg1"] = np.ascontiguousarray(np.asarray(I["L1_ffn_Wg"], np.float32).reshape(8, 128, 8).transpose(1, 0, 2).reshape(128, 64))
    H["bg1"] = np.ascontiguousarray(np.asarray(I["L1_ffn_bg"], np.float32).reshape(1, 8))
    W1 = np.asarray(I["L1_ffn_W1"], np.float32)
    H["w1l1"] = np.stack([
        W1[e].reshape(8, 128, 32, 128).transpose(2, 1, 0, 3).reshape(32, 128, 1024)
        for e in range(EXP)])
    W2 = np.asarray(I["L1_ffn_W2"], np.float32)
    H["w2l1"] = np.ascontiguousarray(W2.reshape(EXP, 4, 8, 128, 1024))
    H["b1l1"] = np.ascontiguousarray(np.stack(
        [feat_cols(np.asarray(I["L1_ffn_b1"], np.float32)[e]) for e in range(EXP)]
    ).transpose(1, 0, 2).reshape(128, EXP * 32))
    H["b2l1"] = np.ascontiguousarray(np.asarray(I["L1_ffn_b2"], np.float32))
    H["gam1"] = feat_cols(np.asarray(I["L1_ffn_gamma"], np.float32))
    H["bet1"] = feat_cols(np.asarray(I["L1_ffn_beta"], np.float32))
    H["wg0"] = np.ascontiguousarray(np.asarray(I["L0_ffn_w_gate_w"], np.float32).reshape(8, 128, 8).transpose(1, 0, 2).reshape(128, 64))
    H["bg0"] = np.ascontiguousarray(np.asarray(I["L0_ffn_w_gate_b"], np.float32).reshape(1, 8))
    W1 = np.asarray(I["L0_ffn_W1"], np.float32)
    w1 = np.stack([
        W1[e].reshape(8, 128, 2, 128).transpose(2, 1, 0, 3).reshape(2, 128, 1024)
        for e in range(EXP)])
    H["w1l0r"] = np.ascontiguousarray(w1.transpose(2, 0, 1, 3).reshape(128, EXP * 2 * 1024))
    W2 = np.asarray(I["L0_ffn_W2"], np.float32)
    w2 = np.stack([W2[e].reshape(2, 128, 1024).transpose(1, 0, 2) for e in range(EXP)])
    H["w2l0r"] = np.ascontiguousarray(w2.transpose(1, 0, 2, 3).reshape(128, EXP * 2 * 1024))
    H["b1l0"] = np.ascontiguousarray(np.stack(
        [feat_cols(np.asarray(I["L0_ffn_b1"], np.float32)[e]) for e in range(EXP)]
    ).transpose(1, 0, 2).reshape(128, EXP * 2))
    H["b2l0"] = np.ascontiguousarray(np.asarray(I["L0_ffn_b2"], np.float32))
    H["wdq"] = lhsT_blocks(np.asarray(I["L0_attn_W_DQ"], np.float32))
    H["wdkv"] = lhsT_blocks(np.asarray(I["L0_attn_W_DKV"], np.float32))
    H["wuq"] = lhsT_blocks(np.asarray(I["L0_attn_W_UQ"], np.float32))
    H["wuk"] = lhsT_blocks(np.asarray(I["L0_attn_W_UK"], np.float32))
    H["wqr"] = lhsT_blocks(np.asarray(I["L0_attn_W_QR"], np.float32))
    wuv = np.asarray(I["L0_attn_W_UV"], np.float32)
    H["wuvr"] = np.ascontiguousarray(
        wuv.reshape(2, 128, 1024).transpose(1, 0, 2).reshape(128, 2048))
    H["wkr"] = lhsT_blocks(np.asarray(I["L0_attn_W_KR"], np.float32), mc=32)
    H["wo"] = lhsT_blocks(np.asarray(I["L0_attn_W_O"], np.float32))
    H["wqkv"] = lhsT_blocks(np.asarray(I["L1_attn_W_qkv"], np.float32))
    H["bqkv"] = feat_cols(np.asarray(I["L1_attn_b_qkv"], np.float32))
    H["wout"] = lhsT_blocks(np.asarray(I["L1_attn_W_out"], np.float32))
    H["bout"] = feat_cols(np.asarray(I["L1_attn_b_out"], np.float32))
    return H

def per_core_inputs(I, H):
    x = np.asarray(I["L0_attn_x"], np.float32)
    maps = []
    for b in range(8):
        m = dict(H)
        m["xT"] = np.ascontiguousarray(x[b].T)
        maps.append(m)
    return maps

def build_full(H, ncores=8):
    nc = bass.Bass("TRN2", target_bir_lowering=False)
    keys = sorted(set(KEYS_SUB0 + KEYS_SUB1 + KEYS_SUB3 + KEYS_SUB4))
    aps = declare_inputs(nc, H, keys)
    xT = nc.dram_tensor("xT", [D, S], F32, kind="ExternalInput").ap()
    xout = nc.dram_tensor("xout", [D, S], F32, kind="ExternalOutput").ap()
    xw = nc.dram_tensor("xw", [D, S], F32, kind="Internal").ap()
    xlnd = nc.dram_tensor("xln", [D, S], F32, kind="Internal").ap()
    cc_in = nc.dram_tensor("cc_in", [8, S], F32, kind="Internal").ap()
    cc_out = nc.dram_tensor("cc_out", [8, S], F32, kind="Internal").ap()
    w1bf = nc.dram_tensor("w1bf", [EXP, 32, 128, 1024], BF16, kind="Internal").ap()
    w2bf = nc.dram_tensor("w2bf", [EXP, 4, 8, 128, 1024], BF16, kind="Internal").ap()
    with ExitStack() as es:
        ctx = Ctx(nc, es)
        blk = es.enter_context(nc.Block())
        with ctx.scope() as sc:
            C = load_consts(ctx, sc, aps)
            wcstate = {}
            w1f = aps["w1l1"].rearrange("e h k m -> (e h) k m")
            w1d = w1bf.rearrange("e h k m -> (e h) k m")
            w2f = aps["w2l1"].rearrange("e g h k m -> (e g h) k m")
            w2d = w2bf.rearrange("e g h k m -> (e g h) k m")

            def ffn0(xsrc, xdst):
                with ctx.scope() as ss:
                    tm = ss.sb("sel_tm", [128, 8 * NT], F32)
                    sT = ss.sb("sel05T", [8, S], F32)
                    emit_l0_sel(ctx, sc, C, aps, xlnd, cc_in, cc_out, tm, sT, ncores=ncores)
                    emit_l0_experts(ctx, sc, C, aps, xlnd, tm, sT, xsrc, xdst)

            def ffn1(xsrc, xdst):
                with ctx.scope() as ss:
                    tm = ss.sb("comb_tm", [128, 8 * NT], F32)
                    cT = ss.sb("combT", [8, S], F32)
                    emit_l1_gate(ctx, sc, C, aps, xlnd, tm, cT)
                    emit_l1_experts(ctx, sc, C, aps, xlnd, tm, cT, xsrc, xdst,
                                    w1src=w1bf, w2src=w2bf)

            with ctx.scope() as early:
                wcbufs = [early.sb(f"wc{i}", [128, 2048], BF16) for i in range(2)]
                emit_ln(ctx, sc, C, xT, xlnd)
                emit_w_convert(ctx, wcbufs, wcstate, [(w1f[0:128], w1d[0:128])])
                ffn0(xT, xw)
                emit_w_convert(ctx, wcbufs, wcstate, [(w1f[128:256], w1d[128:256])])
                emit_ln(ctx, sc, C, xw, xlnd)
                emit_l0_attn(ctx, sc, C, aps, xlnd, xw, xw)
                emit_w_convert(ctx, wcbufs, wcstate, [(w2f[0:128], w2d[0:128])])
                emit_ln(ctx, sc, C, xw, xlnd)
                ffn0(xw, xw)
                emit_w_convert(ctx, wcbufs, wcstate, [(w2f[128:256], w2d[128:256])])
            emit_ln(ctx, sc, C, xw, xlnd)
            ffn1(xw, xw)
            emit_ln(ctx, sc, C, xw, xlnd)
            emit_l1_attn(ctx, sc, C, aps, xlnd, xw, xw)
            emit_ln(ctx, sc, C, xw, xlnd)
            ffn1(xw, xout)
        ctx.barrier()
    return nc

def kernel_multilaunch(**inputs):
    H = host_prepare(inputs)
    x = np.asarray(inputs["L0_attn_x"], np.float32)
    cur = [np.ascontiguousarray(x[b].T) for b in range(8)]
    progs = {
        "s0": (build_sub0(H), KEYS_SUB0),
        "s1": (build_sub1(H), KEYS_SUB1),
        "s3": (build_sub3(H), KEYS_SUB3),
        "s4": (build_sub4(H), KEYS_SUB4),
    }
    for tag in ["s0", "s1", "s0", "s3", "s4", "s3"]:
        prog, keys = progs[tag]
        maps = []
        for b in range(8):
            m = {k: H[k] for k in keys}
            m["xT"] = cur[b]
            maps.append(m)
        res = run_bass_kernel_spmd(prog, maps, core_ids=list(range(8)))
        cur = [np.ascontiguousarray(np.asarray(res.results[b]["xout"], np.float32))
               for b in range(8)]
    return np.stack([c.T for c in cur]).astype(np.float32)

def kernel(**inputs):
    H = host_prepare(inputs)
    x = np.asarray(inputs["L0_attn_x"], np.float32)
    keys = sorted(set(KEYS_SUB0 + KEYS_SUB1 + KEYS_SUB3 + KEYS_SUB4))
    prog = build_full(H)
    maps = []
    for b in range(8):
        m = {k: H[k] for k in keys}
        m["xT"] = np.ascontiguousarray(x[b].T)
        maps.append(m)
    res = run_bass_kernel_spmd(prog, maps, core_ids=list(range(8)))
    out = np.stack([np.asarray(res.results[b]["xout"], np.float32).T for b in range(8)])
    return np.ascontiguousarray(out).astype(np.float32)


# version 16 (final)
# speedup vs baseline: 1.0978x; 1.0041x over previous
import numpy as np
import ml_dtypes
from contextlib import ExitStack
import concourse.bass as bass
import concourse.mybir as mybir
from concourse.bass_utils import run_bass_kernel_spmd

F32 = mybir.dt.float32
BF16 = mybir.dt.bfloat16
AF = mybir.ActivationFunctionType
ALU = mybir.AluOpType
AX = mybir.AxisListType

D = 1024
S = 4096
NC = 8
NT = 32
EXP = 8
NEG = -1.0e38


def lhsT_blocks(W, mc=128):
    K, M = W.shape
    nK, nM = K // 128, M // mc
    return np.ascontiguousarray(W.reshape(nK, 128, nM, mc).transpose(2, 0, 1, 3))

def feat_cols(v):
    n = v.shape[0] // 128
    return np.ascontiguousarray(v.reshape(n, 128).T)


class EngWrap:

    def __init__(self, ctx, eng, sem, relaxed=False):
        self._ctx, self._eng, self._sem = ctx, eng, sem
        self._relaxed = relaxed

    def __getattr__(self, op):
        eng = self._eng
        f = getattr(eng, op)
        ctx, sem = self._ctx, self._sem
        relaxed = self._relaxed

        def g(*a, **k):
            if not relaxed and ctx.val(sem):
                ctx.wait(eng, sem, ctx.val(sem))
            ins = f(*a, **k)
            ctx.inc(sem, ins)
            return ins
        return g

class Scope:
    def __init__(self, ctx):
        self.ctx = ctx
        self.es = ExitStack()

    def sb(self, nm, shape, dtype):
        return self.es.enter_context(self.ctx.nc.sbuf_tensor(self.ctx.name(nm), shape, dtype))

    def ps(self, nm, shape, dtype=F32):
        return self.es.enter_context(self.ctx.nc.psum_tensor(self.ctx.name(nm), shape, dtype))

    def __enter__(self):
        return self

    def __exit__(self, *a):
        self.ctx.barrier()
        self.es.close()
        return False

class Ctx:
    def __init__(self, nc, es):
        self.nc = nc
        self.es = es
        self.sems = {}
        self.uid = 0
        self.V = EngWrap(self, nc.vector, "dve")
        self.A = EngWrap(self, nc.scalar, "act")
        self.G = EngWrap(self, nc.gpsimd, "pool")
        self.Vr = EngWrap(self, nc.vector, "dve", relaxed=True)
        self.Ar = EngWrap(self, nc.scalar, "act", relaxed=True)

    def name(self, p):
        self.uid += 1
        return f"{p}_{self.uid}"

    def scope(self):
        return Scope(self)

    def sem(self, nm):
        if nm not in self.sems:
            h = self.es.enter_context(self.nc.semaphore(nm))
            self.sems[nm] = [h, 0]
        return self.sems[nm]

    def inc(self, nm, instr, amt=1):
        s = self.sem(nm)
        instr.then_inc(s[0], amt)
        s[1] += amt
        return s[1]

    def val(self, nm):
        return self.sem(nm)[1]

    def wait(self, eng, nm, v=None):
        if isinstance(eng, EngWrap):
            eng = eng._eng
        s = self.sem(nm)
        eng.wait_ge(s[0], s[1] if v is None else v)

    def barrier(self):
        snap = [(h, c) for (h, c) in self.sems.values() if c > 0]
        nc = self.nc
        for eng in (nc.sync, nc.gpsimd, nc.tensor, nc.vector, nc.scalar):
            for h, c in snap:
                eng.wait_ge(h, c)


def host_constants():
    c = {}
    c["c_ones1_f32"] = np.ones((1, 128), np.float32)
    c["c_ones128_bf"] = np.ones((128, 1), np.float32)
    c["c_ident_f32"] = np.eye(128, dtype=np.float32)
    c["c_ident_bf"] = np.eye(128, dtype=np.float32)
    c["c_iotaT"] = np.broadcast_to(np.tile(np.arange(8, dtype=np.float32), 32), (128, 256)).copy()
    c["c_iotam1000T"] = c["c_iotaT"] - 1000.0
    tm = np.zeros((128, 128), np.float32)
    tm[np.tril_indices(128, -1)] = NEG
    c["c_trimask"] = tm
    t01 = np.ones((128, 128), np.float32)
    t01[np.tril_indices(128, -1)] = 0.0
    c["c_tri01"] = t01
    c["c_fullneg"] = np.full((128, 128), NEG, np.float32)
    inv = (10000.0 ** (-np.arange(0, 32, 2, dtype=np.float32) / 32.0))
    ang = np.arange(S, dtype=np.float32)[None, :] * inv[:, None]
    cos, sin = np.cos(ang), np.sin(ang)
    ct = np.zeros((32, S), np.float32)
    st = np.zeros((32, S), np.float32)
    ct[0::2] = cos; ct[1::2] = cos
    st[0::2] = -sin; st[1::2] = sin
    c["c_ropecs"] = np.concatenate([ct, st], axis=1)
    P = np.zeros((32, 32), np.float32)
    for i in range(32):
        P[i ^ 1, i] = 1.0
    c["c_pswap"] = P
    return c

CONST_KEYS = ["c_ones1_f32", "c_ones128_bf", "c_ident_f32", "c_ident_bf",
              "c_iotaT", "c_iotam1000T"]
KEYS_SUB0 = CONST_KEYS + ["wg0", "bg0", "w1l0r", "w2l0r", "b1l0", "b2l0"]
KEYS_SUB1 = CONST_KEYS + ["c_trimask", "c_tri01", "c_ropecs", "c_pswap", "c_fullneg",
                          "wdq", "wdkv", "wuq", "wuk", "wqr", "wuvr", "wkr", "wo"]
KEYS_SUB3 = CONST_KEYS + ["wg1", "bg1", "w1l1", "w2l1", "b1l1", "b2l1", "gam1", "bet1"]
KEYS_SUB4 = CONST_KEYS + ["wqkv", "bqkv", "wout", "bout"]

def declare_inputs(nc, host_arrays, keys=None):
    aps = {}
    for k, v in host_arrays.items():
        if keys is not None and k not in keys:
            continue
        dt = BF16 if v.dtype == ml_dtypes.bfloat16 else F32
        aps[k] = nc.dram_tensor(k, list(v.shape), dt, kind="ExternalInput").ap()
    return aps

def load_consts(ctx, sc, aps):
    nc = ctx.nc
    C = {}
    spec = [("ones1_f32", "c_ones1_f32", [1, 128], F32, False),
            ("ones128_bf", "c_ones128_bf", [128, 1], BF16, True),
            ("ident_f32", "c_ident_f32", [128, 128], F32, False),
            ("ident_bf", "c_ident_bf", [128, 128], BF16, True),
            ("iotaT", "c_iotaT", [128, 256], F32, False),
            ("iotam1000T", "c_iotam1000T", [128, 256], F32, False)]
    for nm, key, shape, dt, cast in spec:
        C[nm] = sc.sb(nm, shape, dt)
        eng, q = (nc.gpsimd, "qg") if cast else (nc.sync, "qs")
        ctx.inc(q, eng.dma_start(out=C[nm][:], in_=aps[key]), 16)
    ctx.barrier()
    return C

def load_attn_consts(ctx, ph, aps, rope=False):
    nc = ctx.nc
    C2 = {}
    spec = [("trimask", "c_trimask", [128, 128], F32, False),
            ("trimask_bf", "c_trimask", [128, 128], BF16, True),
            ("fullneg_bf", "c_fullneg", [128, 128], BF16, True)]
    if rope:
        spec += [("pswap", "c_pswap", [32, 32], BF16, True),
                 ("tri01", "c_tri01", [128, 128], BF16, True)]
    for nm, key, shape, dt, cast in spec:
        C2[nm] = ph.sb(nm, shape, dt)
        eng, q = (nc.gpsimd, "qg") if cast else (nc.sync, "qs")
        ctx.inc(q, eng.dma_start(out=C2[nm][:], in_=aps[key]), 16)
    return C2


def emit_w_convert(ctx, bufs, state, pairs):
    nc = ctx.nc
    st_after = state
    for src_ap, dst_ap in pairs:
        B = src_ap.shape[0]
        M = src_ap.shape[2]
        per = max(1, 2048 // M)
        for b0 in range(0, B, per):
            nb = min(per, B - b0)
            sl = state.setdefault("_i", 0) % 2
            state["_i"] += 1
            buf = bufs[sl][:, 0:nb*M].rearrange("k (b m) -> k b m", b=nb)
            if sl in st_after:
                ctx.wait(nc.gpsimd, "wcs", st_after[sl])
            ctx.inc("wcl", nc.gpsimd.dma_start(
                out=buf, in_=src_ap[b0:b0+nb].rearrange("b k m -> k b m")), 16)
            ctx.wait(nc.sync, "wcl", ctx.val("wcl"))
            ctx.inc("wcs", nc.sync.dma_start(
                out=dst_ap[b0:b0+nb].rearrange("b k m -> k b m"), in_=buf), 16)
            st_after[sl] = ctx.val("wcs")


class StatsKit:
    def __init__(self, ctx, sc, C, banks=None):
        self.ctx, self.C = ctx, C
        self.xb = sc.sb("st_xb", [128, NC * 512], BF16)
        self.xq = sc.sb("st_xq", [128, NC * 512], BF16)
        self.rw = sc.sb("st_rw", [1, 4 * 512], F32)
        if banks is None:
            self.ps_s = sc.ps("st_s", [1, 512])[:]
            self.ps_ss = sc.ps("st_ss", [1, 512])[:]
            self.pb_m = sc.ps("st_m", [128, 512])[:]
            self.pb_r = sc.ps("st_r", [128, 512])[:]
        else:
            self.ps_s, self.ps_ss, self.pb_m, self.pb_r = banks

    def emit(self, get_src, denom=1024.0, eps=1e-5):
        ctx, C = self.ctx, self.C
        nc, V, A = ctx.nc, ctx.V, ctx.A
        R = lambda i: self.rw[:, 512*i:512*(i+1)]
        Vr, Ar = ctx.Vr, ctx.Ar
        for c in range(NC):
            src = get_src(c)
            (V if c == 0 else Vr).tensor_copy(self.xb[:, 512*c:512*(c+1)], src)
            (A if c == 0 else Ar).activation(self.xq[:, 512*c:512*(c+1)], src, AF.Square)
        ctx.wait(nc.tensor, "dve")
        ctx.wait(nc.tensor, "act")
        for c in range(NC):
            nc.tensor.matmul(self.ps_s, C["ones128_bf"][:], self.xb[:, 512*c:512*(c+1)],
                             start=(c == 0), stop=(c == NC - 1))
            i = nc.tensor.matmul(self.ps_ss, C["ones128_bf"][:], self.xq[:, 512*c:512*(c+1)],
                                 start=(c == 0), stop=(c == NC - 1))
        pev = ctx.inc("pe", i)
        ctx.wait(nc.vector, "pe", pev)
        V.tensor_scalar_mul(R(0), self.ps_s, 1.0 / denom)
        V.tensor_scalar_mul(R(1), self.ps_ss, 1.0 / denom)
        V.tensor_mul(R(2), R(0), R(0))
        V.tensor_sub(R(1), R(1), R(2))
        V.tensor_scalar_add(R(1), R(1), eps)
        ctx.wait(nc.scalar, "dve")
        A.activation(R(2), R(1), AF.Sqrt)
        ctx.wait(nc.vector, "act")
        V.reciprocal(R(3), R(2))
        ctx.wait(nc.tensor, "dve")
        nc.tensor.matmul(self.pb_m, C["ones1_f32"][:], R(0), start=True, stop=True)
        i = nc.tensor.matmul(self.pb_r, C["ones1_f32"][:], R(3), start=True, stop=True)
        return ctx.inc("pe", i)


def emit_ln(ctx, sc, C, xsrc, xln):
    nc, V = ctx.nc, ctx.V
    with ctx.scope() as ph:
        xf = ph.sb("ln_xf", [128, 2 * NC * 512], F32)
        xo = ph.sb("ln_xo", [128, 2 * NC * 512], F32)
        st = StatsKit(ctx, ph, C)
        def ln_loads(g):
            if g >= 8:
                return
            sl = f"lnx{g % 2}"
            off = (g % 2) * NC * 512
            if g >= 2:
                ctx.wait(nc.sync, "dve", ctx.val("dve"))
                ctx.wait(nc.sync, "act", ctx.val("act"))
            for c in range(NC):
                ctx.inc(sl, nc.sync.dma_start(
                    out=xf[:, off + 512*c: off + 512*(c+1)],
                    in_=xsrc[128*c:128*(c+1), 512*g:512*(g+1)]), 16)

        ln_loads(0)
        ln_loads(1)
        for g in range(8):
            sl = f"lnx{g % 2}"
            off = (g % 2) * NC * 512
            X = lambda c: xf[:, off + 512*c: off + 512*(c+1)]
            O = lambda c: xo[:, off + 512*c: off + 512*(c+1)]
            ctx.wait(nc.vector, sl)
            ctx.wait(nc.scalar, sl)
            pev = st.emit(X)
            ctx.wait(nc.vector, "pe", pev)
            ctx.wait(nc.vector, "lnst")
            for c in range(NC):
                (ctx.Vr if c else V).tensor_sub(O(c), X(c), st.pb_m)
                V.tensor_mul(O(c), O(c), st.pb_r)
            dv = ctx.val("dve")
            ln_loads(g + 2)
            ctx.wait(nc.sync, "dve", dv)
            for c in range(NC):
                ctx.inc("lnst", nc.sync.dma_start(
                    out=xln[128*c:128*(c+1), 512*g:512*(g+1)], in_=O(c)), 16)


def emit_l1_gate(ctx, sc, C, aps, xln, comb_tm, combT,
                 wg_key="wg1", bg_key="bg1"):
    nc, V, A = ctx.nc, ctx.V, ctx.A
    with ctx.scope() as ph:
        wg = ph.sb("g_wg", [128, 8 * NC], F32)
        bgr = ph.sb("g_bg", [1, 8], F32)
        xg = ph.sb("g_xg", [128, 2 * NC * 512], F32)
        Lg = ph.sb("g_Lg", [128, 256], F32)
        tA = ph.sb("g_tA", [128, 256], F32)
        tB = ph.sb("g_tB", [128, 256], F32)
        tC = ph.sb("g_tC", [128, 256], F32)
        sm = ph.sb("g_sm", [128, 6 * 32], F32)
        pgall = ph.ps("g_pg", [128, 256])
        pt = ph.ps("g_pt", [8, 512])
        SM = lambda i: sm[:, 32*i:32*(i+1)]
        SMb = lambda i: sm[:, 32*i:32*(i+1)].rearrange("p (t o) -> p t o", o=1).broadcast_to([128, 32, 8])
        E3 = lambda ap: ap.rearrange("p (t e) -> p t e", e=8)
        ctx.inc("qs", nc.sync.dma_start(out=wg[:], in_=aps[wg_key]), 16)
        ctx.inc("qs", nc.sync.dma_start(out=bgr[:], in_=aps[bg_key]), 16)
        gpe = {}
        for g in range(8):
            sl = f"gx{g % 2}"
            off = (g % 2) * NC * 512
            X = lambda c: xg[:, off + 512*c: off + 512*(c+1)]
            if g >= 2:
                ctx.wait(nc.sync, "pe", gpe[g - 2])
            for c in range(NC):
                ctx.inc(sl, nc.sync.dma_start(
                    out=X(c), in_=xln[128*c:128*(c+1), 512*g:512*(g+1)]), 16)
            ctx.wait(nc.tensor, sl)
            ctx.wait(nc.tensor, "qs")
            for j4 in range(4):
                j = 4*g + j4
                for c in range(NC):
                    nc.tensor.matmul(pgall[:, 8*j:8*(j+1)],
                                     xg[:, off + 512*c + 128*j4: off + 512*c + 128*j4 + 128],
                                     wg[:, 8*c:8*(c+1)], start=(c == 0), stop=False)
                i = nc.tensor.matmul(pgall[:, 8*j:8*(j+1)], C["ones1_f32"][:], bgr[:],
                                     start=False, stop=True)
            gpe[g] = ctx.inc("pe", i)
        ctx.wait(nc.vector, "pe", gpe[7])
        V.tensor_copy(Lg[:], pgall[:])
        V.tensor_reduce(SM(0), E3(Lg[:]), AX.X, ALU.max)
        V.tensor_tensor(E3(tA[:]), E3(Lg[:]), SMb(0), ALU.is_ge)
        V.scalar_tensor_tensor(tB[:], tA[:], -1000.0, C["iotaT"][:], ALU.mult, ALU.add)
        V.tensor_reduce(SM(1), E3(tB[:]), AX.X, ALU.min)
        V.tensor_tensor(E3(tA[:]), E3(C["iotam1000T"][:]), SMb(1), ALU.is_equal)
        V.scalar_tensor_tensor(tB[:], tA[:], NEG, Lg[:], ALU.mult, ALU.add)
        V.tensor_reduce(SM(2), E3(tB[:]), AX.X, ALU.max)
        V.tensor_tensor(E3(tC[:]), E3(tB[:]), SMb(2), ALU.is_ge)
        V.scalar_tensor_tensor(tC[:], tC[:], -1000.0, C["iotaT"][:], ALU.mult, ALU.add)
        V.tensor_reduce(SM(3), E3(tC[:]), AX.X, ALU.min)
        V.tensor_tensor(E3(tC[:]), E3(C["iotam1000T"][:]), SMb(3), ALU.is_equal)
        V.tensor_sub(SM(4), SM(2), SM(0))
        ctx.wait(nc.scalar, "dve")
        A.activation(SM(5), SM(4), AF.Sigmoid)
        ctx.wait(nc.vector, "act")
        V.tensor_tensor(E3(tA[:]), E3(tA[:]), SMb(5), ALU.mult)
        V.tensor_tensor(E3(tC[:]), E3(tC[:]), SMb(5), ALU.mult)
        V.tensor_sub(tC[:], tC[:], tA[:])
        V.tensor_tensor(E3(tA[:]), E3(C["iotam1000T"][:]), SMb(1), ALU.is_equal)
        V.tensor_add(comb_tm[:], tC[:], tA[:])
        ctx.wait(nc.tensor, "dve")
        for j4 in range(NT // 4):
            if j4 >= 1:
                ctx.wait(nc.tensor, "dve", ctx.val("dve"))
            for u in range(4):
                j = 4*j4 + u
                i = nc.tensor.transpose(pt[:, 128*u:128*(u+1)], comb_tm[:, 8*j:8*(j+1)],
                                        C["ident_f32"][:])
            pev = ctx.inc("pe", i)
            ctx.wait(nc.vector, "pe", pev)
            for u in range(4):
                j = 4*j4 + u
                (V if u == 0 else ctx.Vr).tensor_copy(combT[:, 128*j:128*(j+1)],
                                                      pt[:, 128*u:128*(u+1)])


def emit_l1_experts(ctx, sc, C, aps, xln, comb_tm, combT, xsrc, xdst,
                    ntset=8, nexp=EXP, do_epi=1, do_b2=1, do_B=1,
                    w1src=None, w2src=None):
    nc, V, A = ctx.nc, ctx.V, ctx.A
    with ctx.scope() as ph:
        b1 = ph.sb("e1_b1", [128, EXP * 32], F32)
        b2 = ph.sb("e1_b2", [8, 1024], F32)
        gam = ph.sb("e1_gam", [128, 8], F32)
        bet = ph.sb("e1_bet", [128, 8], F32)
        xln_ts = ph.sb("e1_xts", [128, 2 * NC * 512], BF16)
        xres = ph.sb("e1_xres", [128, 2 * NC * 512], F32)
        preT = ph.sb("e1_preT", [128, 2 * NC * 512], F32)
        Hgel = ph.sb("e1_H", [128, 32 * 512], BF16)
        w1b = ph.sb("e1_w1", [128, 3 * 1024], BF16)
        w2b = ph.sb("e1_w2", [128, 3 * 4096], BF16)
        out_acc = ph.sb("e1_oa", [128, 4 * 1024], F32)
        py = [ph.ps(f"e1_py{j}", [128, 512]) for j in range(8)]
        st = StatsKit(ctx, ph, C, banks=(py[4][0:1, :], py[5][0:1, :],
                                         py[6][:], py[7][:]))
        ctx.inc("qs", nc.sync.dma_start(out=b1[:], in_=aps["b1l1"]), 16)
        ctx.inc("qs", nc.sync.dma_start(out=b2[:], in_=aps["b2l1"]), 16)
        ctx.inc("qs", nc.sync.dma_start(out=gam[:], in_=aps["gam1"]), 16)
        ctx.inc("qs", nc.sync.dma_start(out=bet[:], in_=aps["bet1"]), 16)
        resident_qs = ctx.val("qs")
        ctx.wait(nc.tensor, "qs", resident_qs)
        ctx.wait(nc.scalar, "qs", resident_qs)
        ctx.wait(nc.vector, "qs", resident_qs)

        w1ap = aps["w1l1"] if w1src is None else w1src
        w2ap = aps["w2l1"] if w2src is None else w2src
        w1eng = nc.gpsimd if w1src is None else nc.scalar
        w2eng = nc.gpsimd if w2src is None else nc.sync

        def w1_load(e, hc, guard_pe):
            sl = f"w1{hc % 3}"
            ctx.wait(w1eng, "pe", guard_pe)
            ctx.inc(sl, w1eng.dma_start(
                out=w1b[:, 1024*(hc % 3):1024*(hc % 3 + 1)],
                in_=w1ap[e, hc]), 16)
            return (sl, ctx.val(sl))

        def w2_load(e, s, guard_pe):
            sl = f"w2{s % 3}"
            dg, hg = divmod(s, 4)
            ctx.wait(w2eng, "pe", guard_pe)
            src = w2ap[e, hg, :, :, 512*dg:512*(dg+1)].rearrange("h k d -> k h d")
            o = 4096 * (s % 3)
            dst = w2b[:, o:o+4096].rearrange("k (h d) -> k h d", h=8)
            ctx.inc(sl, w2eng.dma_start(out=dst, in_=src), 16)
            return (sl, ctx.val(sl))

        def tset_loads(t):
            if t >= ntset:
                return (("qs", 0), ("qs", 0))
            to = 512 * t
            xo = (t % 2) * NC * 512
            ctx.wait(nc.gpsimd, "pe", ctx.val("pe"))
            ctx.wait(nc.sync, "dve", ctx.val("dve"))
            ctx.wait(nc.sync, "act", ctx.val("act"))
            ctx.wait(nc.sync, "qs", ctx.val("qs"))
            xs, ps = f"e1x{t % 2}", f"e1p{t % 2}"
            for c in range(NC):
                ctx.inc(xs, nc.gpsimd.dma_start(
                    out=xln_ts[:, xo + 512*c: xo + 512*(c+1)],
                    in_=xln[128*c:128*(c+1), to:to+512]), 16)
                ctx.inc(ps, nc.sync.dma_start(
                    out=preT[:, xo + 512*c: xo + 512*(c+1)],
                    in_=xln[128*c:128*(c+1), to:to+512]), 16)
                ctx.inc(ps, nc.sync.dma_start(
                    out=xres[:, xo + 512*c: xo + 512*(c+1)],
                    in_=xsrc[128*c:128*(c+1), to:to+512]), 16)
            return ((xs, ctx.val(xs)), (ps, ctx.val(ps)))

        for eng in (nc.sync, nc.gpsimd):
            for sname in ("dve", "act", "pe", "qs"):
                if sname in ctx.sems and ctx.val(sname):
                    ctx.wait(eng, sname, ctx.val(sname))
        ld_tok = {0: tset_loads(0)}
        for t in range(ntset):
            to = 512 * t
            xo = (t % 2) * NC * 512
            X = lambda c: xln_ts[:, xo + 512*c: xo + 512*(c+1)]
            PRE = lambda c: preT[:, xo + 512*c: xo + 512*(c+1)]
            XRES = lambda c: xres[:, xo + 512*c: xo + 512*(c+1)]
            ctx.wait(nc.tensor, "dve", ctx.val("dve"))
            if do_b2:
                for j8 in range(8):
                    j, dg = divmod(j8, 2)
                    jj = 4 * t + j
                    i = nc.tensor.matmul(py[j8][:], combT[:, 128*jj:128*(jj+1)],
                                         b2[:, 512*dg:512*(dg+1)], start=True, stop=True)
                pev = ctx.inc("pe", i)
                ctx.wait(nc.vector, "pe", pev)
                for j8 in range(8):
                    j, dg = divmod(j8, 2)
                    (V if j8 == 0 else ctx.Vr).tensor_copy(
                        out_acc[:, 1024*j + 512*dg: 1024*j + 512*(dg+1)], py[j8][:])
            ctx.wait(nc.tensor, "dve", ctx.val("dve"))
            ctx.wait(nc.tensor, *ld_tok[t][0])
            for e in range(nexp):
                pe_base = ctx.val("pe")
                qg_after = {0: w1_load(e, 0, pe_base), 1: w1_load(e, 1, pe_base),
                            2: w1_load(e, 2, pe_base)}
                act_after = {}
                pe_A = {}
                for hc in range(32):
                    pb = hc % 4
                    ctx.wait(nc.tensor, *qg_after[hc])
                    if hc >= 4:
                        ctx.wait(nc.tensor, "act", act_after[hc - 4])
                    for kc in range(NC):
                        i = nc.tensor.matmul(py[pb][:],
                                             w1b[:, 1024*(hc % 3) + 128*kc: 1024*(hc % 3) + 128*(kc+1)],
                                             X(kc), start=(kc == 0), stop=(kc == NC - 1))
                    pe_A[hc] = pev = ctx.inc("pe", i)
                    ctx.wait(nc.scalar, "pe", pev)
                    (A if hc == 0 else ctx.Ar).activation(
                        Hgel[:, 512*hc:512*(hc+1)], py[pb][:], AF.Gelu,
                        bias=b1[:, 32*e+hc: 32*e+hc+1])
                    act_after[hc] = ctx.val("act")
                    if hc + 3 < 32:
                        qg_after[hc + 3] = w1_load(e, hc + 3, pe_A[hc])
                ctx.wait(nc.tensor, "act", ctx.val("act"))
                if not do_B:
                    continue
                pe_base = ctx.val("pe")
                qg_after = {0: w2_load(e, 0, pe_base), 1: w2_load(e, 1, pe_base),
                            2: w2_load(e, 2, pe_base)}
                pe_B = {}
                for s in range(8):
                    dg, hg = divmod(s, 4)
                    wb = s % 3
                    ctx.wait(nc.tensor, *qg_after[s])
                    for h8 in range(8):
                        hc = 8 * hg + h8
                        for j in range(4):
                            i = nc.tensor.matmul(
                                py[4*dg + j][:],
                                Hgel[:, 512*hc + 128*j: 512*hc + 128*(j+1)],
                                w2b[:, 4096*wb + 512*h8: 4096*wb + 512*(h8+1)],
                                start=(hc == 0), stop=(hc == 31))
                    pe_B[s] = pev = ctx.inc("pe", i)
                    if s + 3 < 8:
                        qg_after[s + 3] = w2_load(e, s + 3, pe_B[s])
                    if hg == 3:
                        ctx.wait(nc.vector, "pe", pev)
                        for j in range(4):
                            jj = 4 * t + j
                            sl = out_acc[:, 1024*j + 512*dg: 1024*j + 512*(dg+1)]
                            (V if j == 0 else ctx.Vr).scalar_tensor_tensor(
                                sl, py[4*dg + j][:], comb_tm[:, 8*jj+e: 8*jj+e+1],
                                sl, ALU.mult, ALU.add)
                ctx.wait(nc.tensor, "dve", ctx.val("dve"))
                if e == 0:
                    ld_tok[t + 1] = tset_loads(t + 1)
            if not do_epi:
                continue
            ctx.wait(nc.vector, *ld_tok[t][1])
            grp_dve = {}
            for grp in range(8):
                tiles = [(j, dc) for j in range(4) for dc in range(NC)][4*grp:4*grp+4]
                if grp >= 1:
                    ctx.wait(nc.tensor, "dve", grp_dve.get(grp - 1, 0))
                for b4, (j, dc) in enumerate(tiles):
                    i = nc.tensor.transpose(
                        py[b4][:, 0:128],
                        out_acc[:, 1024*j + 128*dc: 1024*j + 128*(dc+1)],
                        C["ident_f32"][:])
                pev = ctx.inc("pe", i)
                ctx.wait(nc.vector, "pe", pev)
                for b4, (j, dc) in enumerate(tiles):
                    dst = preT[:, xo + 512*dc + 128*j: xo + 512*dc + 128*(j+1)]
                    (V if b4 == 0 else ctx.Vr).tensor_add(dst, py[b4][:, 0:128], dst)
                grp_dve[grp] = ctx.val("dve")
            ctx.wait(nc.scalar, "dve")
            ctx.wait(nc.tensor, "dve")
            pev = st.emit(lambda c: PRE(c))
            ctx.wait(nc.vector, "pe", pev)
            for c in range(NC):
                PT = PRE(c)
                V.tensor_sub(PT, PT, st.pb_m)
                V.tensor_mul(PT, PT, st.pb_r)
                V.tensor_scalar(PT, PT, gam[:, c:c+1], bet[:, c:c+1], ALU.mult, ALU.add)
                XR = XRES(c)
                V.scalar_tensor_tensor(XR, PT, 0.5, XR, ALU.mult, ALU.add)
                dv = ctx.val("dve")
                ctx.wait(nc.sync, "dve", dv)
                ctx.inc("qs", nc.sync.dma_start(
                    out=xdst[128*c:128*(c+1), to:to+512], in_=XR), 16)


def emit_l1_attn(ctx, sc, C, aps, xln, xsrc, xdst, npair=8, nqg=8):
    nc, V, A = ctx.nc, ctx.V, ctx.A
    with ctx.scope() as ph:
        bqkv = ph.sb("a1_bqkv", [128, 24], F32)
        bout = ph.sb("a1_bout", [128, 8], F32)
        xbf = ph.sb("a1_xbf", [128, 2 * NC * 512], BF16)
        wq = ph.sb("a1_wq", [128, 2 * 24 * 128], BF16)
        qT2 = ph.sb("a1_q", [128, S], BF16)
        kT2 = ph.sb("a1_k", [128, S], BF16)
        vT2 = ph.sb("a1_v", [128, S], BF16)
        vtok = [ph.sb(f"a1_vt{i}", [128, 32 * 65], BF16) for i in range(2)]
        pT = ph.sb("a1_pT", [128, 4 * 1024], BF16)
        pvs = ph.sb("a1_pvs", [128, 512], F32)
        rrow = ph.sb("a1_rr", [1, 512], F32)
        att = [ph.sb(f"a1_att{i}", [128, S], BF16) for i in range(8)]
        xres = ph.sb("a1_xres", [128, 2 * NC * 512], F32)
        ps_s = [ph.ps(f"a1_s{i}", [128, 1024]) for i in range(2)]
        ps_pv = ph.ps("a1_pv", [65, 512])
        ps_pb = ph.ps("a1_pb", [64, 512])
        ps_vt = ph.ps("a1_vtp", [128, 128], BF16)
        ps_qkv = ph.ps("a1_qkv", [128, 512])
        woutb = ph.sb("a1_wo", [128, 64 * 128], BF16)
        ctx.inc("qs", nc.sync.dma_start(out=bqkv[:], in_=aps["bqkv"]), 16)
        ctx.inc("qs", nc.sync.dma_start(out=bout[:], in_=aps["bout"]), 16)
        ctx.inc("qg", nc.gpsimd.dma_start(
            out=woutb[:].rearrange("k (m c f) -> k m c f", m=8, c=8),
            in_=aps["wout"].rearrange("m c k f -> k m c f")), 16)
        ctx.wait(nc.vector, "qs", ctx.val("qs"))
        V.memset(vtok[0][:], 1.0)
        V.memset(vtok[1][:], 1.0)
        bank_dve = {}
        for p in range(npair):
            wsl = f"a1w{p % 2}"
            ctx.wait(nc.gpsimd, "pe", ctx.val("pe"))
            ctx.inc(wsl, nc.gpsimd.dma_start(
                out=wq[:, 3072*(p % 2):3072*(p % 2 + 1)].rearrange("k (o c m) -> k o c m", o=3, c=8),
                in_=aps["wqkv"][3*p:3*p+3].rearrange("o c k m -> k o c m")), 16)
            ctx.wait(nc.tensor, wsl)
            for g in range(8):
                xsl = f"a1x{g % 2}"
                xo = (g % 2) * NC * 512
                if g >= 2:
                    ctx.wait(nc.gpsimd, "pe", ctx.val("pe"))
                for c in range(NC):
                    ctx.inc(xsl, nc.gpsimd.dma_start(
                        out=xbf[:, xo + 512*c: xo + 512*(c+1)],
                        in_=xln[128*c:128*(c+1), 512*g:512*(g+1)]), 16)
                ctx.wait(nc.tensor, xsl)
                for o3 in range(3):
                    oc = 3*p + o3
                    if "qkv" in bank_dve:
                        ctx.wait(nc.tensor, "dve", bank_dve["qkv"])
                    for dk in range(NC):
                        i = nc.tensor.matmul(
                            ps_qkv[:],
                            wq[:, 3072*(p % 2) + 1024*o3 + 128*dk: 3072*(p % 2) + 1024*o3 + 128*(dk+1)],
                            xbf[:, xo + 512*dk: xo + 512*(dk+1)],
                            start=(dk == 0), stop=(dk == NC - 1))
                    pev = ctx.inc("pe", i)
                    ctx.wait(nc.vector, "pe", pev)
                    for seg in range(2):
                        f0 = 128*oc + 64*seg
                        h = f0 // 192
                        j = f0 % 192
                        kind = j // 64
                        dstbuf = (qT2, kT2, vT2)[kind]
                        r0 = 64 * (h % 2)
                        V.tensor_scalar(dstbuf[r0:r0+64, 512*g:512*(g+1)],
                                        ps_qkv[64*seg:64*seg+64, :],
                                        bqkv[64*seg:64*seg+64, oc:oc+1], None, ALU.add)
                    bank_dve["qkv"] = ctx.val("dve")
            for hh in range(2):
                for ti in range(NT):
                    if "vt" in bank_dve:
                        ctx.wait(nc.tensor, "dve", bank_dve["vt"])
                    i = nc.tensor.transpose(ps_vt[:, 0:64],
                                            vT2[64*hh:64*hh+64, 128*ti:128*(ti+1)],
                                            C["ident_bf"][64*hh:64*hh+64, 64*hh:64*hh+64])
                    pev = ctx.inc("pe", i)
                    ctx.wait(nc.vector, "pe", pev)
                    V.tensor_copy(vtok[hh][:, 65*ti:65*ti+64], ps_vt[:, 0:64])
                    bank_dve["vt"] = ctx.val("dve")
            for hh in range(2):
                r0 = 64 * hh
                norm = None
                for qg in range(nqg):
                    act_exp = {}
                    pv_pe = {}

                    def emit_S(kp):
                        sb = kp % 2
                        pb = kp % 4
                        if kp >= 2:
                            ctx.wait(nc.tensor, "act", act_exp[kp - 2])
                        if kp >= 4:
                            ctx.wait(nc.scalar, "pe", pv_pe[kp - 4])
                        for u in range(2):
                            kt = 2*kp + u
                            i = nc.tensor.matmul(ps_s[sb][:, 512*u:512*(u+1)],
                                                 kT2[r0:r0+64, 128*kt:128*(kt+1)],
                                                 qT2[r0:r0+64, 512*qg:512*(qg+1)],
                                                 start=True, stop=True)
                        pev = ctx.inc("pe", i)
                        ctx.wait(nc.scalar, "pe", pev)
                        A.activation(pT[:, 1024*pb:1024*(pb+1)], ps_s[sb][:],
                                     AF.Exp, scale=0.125)
                        act_exp[kp] = ctx.val("act")

                    def emit_PV(kp):
                        pb = kp % 4
                        ctx.wait(nc.tensor, "act", act_exp[kp])
                        for u in range(2):
                            kt = 2*kp + u
                            i = nc.tensor.matmul(ps_pv[:], vtok[hh][:, 65*kt:65*(kt+1)],
                                                 pT[:, 1024*pb + 512*u: 1024*pb + 512*(u+1)],
                                                 start=(kt == 0), stop=(kt == NT - 1))
                        pv_pe[kp] = ctx.inc("pe", i)
                        return pv_pe[kp]

                    emit_S(0)
                    if norm is not None:
                        ctx.wait(nc.vector, "pe", norm[0])
                        V.reciprocal(rrow[:], ps_pv[64:65, :])
                        V.tensor_copy(pvs[0:64, :], ps_pv[0:64, :])
                        dv_copy = ctx.val("dve")
                    emit_S(1)
                    if norm is not None:
                        ctx.wait(nc.tensor, "dve", dv_copy)
                    emit_PV(0)
                    if norm is not None:
                        i = nc.tensor.matmul(ps_pb[:], C["ones1_f32"][:, 0:64],
                                             rrow[:], start=True, stop=True)
                        pev = ctx.inc("pe", i)
                        ctx.wait(nc.vector, "pe", pev)
                        V.tensor_mul(att[p][r0:r0+64, 512*norm[1]:512*(norm[1]+1)],
                                     pvs[0:64, :], ps_pb[:])
                    for kp in range(2, NT // 2):
                        emit_S(kp)
                        emit_PV(kp - 1)
                    norm = (emit_PV(NT // 2 - 1), qg)
                ctx.wait(nc.vector, "pe", norm[0])
                V.reciprocal(rrow[:], ps_pv[64:65, :])
                dv = ctx.val("dve")
                V.tensor_copy(pvs[0:64, :], ps_pv[0:64, :])
                ctx.wait(nc.tensor, "dve", dv)
                i = nc.tensor.matmul(ps_pb[:], C["ones1_f32"][:, 0:64], rrow[:],
                                     start=True, stop=True)
                pev = ctx.inc("pe", i)
                ctx.wait(nc.vector, "pe", pev)
                V.tensor_mul(att[p][r0:r0+64, 512*norm[1]:512*(norm[1]+1)],
                             pvs[0:64, :], ps_pb[:])
                ctx.wait(nc.tensor, "dve", ctx.val("dve"))
        ctx.barrier()
        for g in range(8):
            xo = (g % 2) * NC * 512
            if g >= 2:
                ctx.wait(nc.sync, "dve", ctx.val("dve"))
                ctx.wait(nc.sync, "qs", ctx.val("qs"))
            for c in range(NC):
                ctx.inc("qs", nc.sync.dma_start(
                    out=xres[:, xo + 512*c: xo + 512*(c+1)],
                    in_=xsrc[128*c:128*(c+1), 512*g:512*(g+1)]), 16)
            ctx.wait(nc.vector, "qs")
            if g == 0:
                ctx.wait(nc.tensor, "qg", ctx.val("qg"))
                pq = lambda b: ps_qkv[:] if b == 0 else ps_s[0][:, 0:512]
                bank_ep = {}
            for dm in range(NC):
                bk = dm % 2
                if bk in bank_ep:
                    ctx.wait(nc.tensor, "dve", bank_ep[bk])
                for hd in range(npair):
                    i = nc.tensor.matmul(pq(bk),
                                         woutb[:, 1024*dm + 128*hd: 1024*dm + 128*(hd+1)],
                                         att[hd][:, 512*g:512*(g+1)],
                                         start=(hd == 0), stop=(hd == npair - 1))
                pev = ctx.inc("pe", i)
                ctx.wait(nc.vector, "pe", pev)
                XR = xres[:, xo + 512*dm: xo + 512*(dm+1)]
                V.tensor_scalar(pvs[:], pq(bk), bout[:, dm:dm+1], None, ALU.add)
                V.tensor_add(XR, XR, pvs[:])
                bank_ep[bk] = ctx.val("dve")
                dv = bank_ep[bk]
                ctx.wait(nc.sync, "dve", dv)
                ctx.inc("qs", nc.sync.dma_start(
                    out=xdst[128*dm:128*(dm+1), 512*g:512*(g+1)], in_=XR), 16)


def emit_l0_sel(ctx, sc, C, aps, xln, cc_in, cc_out, sel05_tm, sel05T, ncores=8,
                Kcap=2048):
    nc, V, A = ctx.nc, ctx.V, ctx.A
    with ctx.scope() as ph:
        wg = ph.sb("s0_wg", [128, 8 * NC], F32)
        bgr = ph.sb("s0_bg", [1, 8], F32)
        xg = ph.sb("s0_xg", [128, 2 * NC * 512], F32)
        Lg = ph.sb("s0_Lg", [128, 256], F32)
        tA = ph.sb("s0_tA", [128, 256], F32)
        tB = ph.sb("s0_tB", [128, 256], F32)
        tC = ph.sb("s0_tC", [128, 256], F32)
        sm = ph.sb("s0_sm", [128, 6 * 32], F32)
        mT = ph.sb("s0_mT", [8, S], F32)
        routeT = ph.sb("s0_rt", [8, S], F32)
        scr = ph.sb("s0_scr", [8, S], F32)
        tieT = ph.sb("s0_tie", [8, S], F32)
        ones8 = ph.sb("s0_on", [8, S], F32)
        bis = ph.sb("s0_bis", [8, 10], F32)
        pgall = ph.ps("s0_pg", [128, 256])
        pt = ph.ps("s0_pt", [8, 512])
        ptB = ph.ps("s0_ptB", [128, 32])
        SM = lambda i: sm[:, 32*i:32*(i+1)]
        SMb = lambda i: sm[:, 32*i:32*(i+1)].rearrange("p (t o) -> p t o", o=1).broadcast_to([128, 32, 8])
        E3 = lambda ap: ap.rearrange("p (t e) -> p t e", e=8)
        B1 = lambda i: bis[:, i:i+1]
        ctx.inc("qs", nc.sync.dma_start(out=wg[:], in_=aps["wg0"]), 16)
        ctx.inc("qs", nc.sync.dma_start(out=bgr[:], in_=aps["bg0"]), 16)
        gpe = {}
        for g in range(8):
            sl = f"s0x{g % 2}"
            off = (g % 2) * NC * 512
            X = lambda c: xg[:, off + 512*c: off + 512*(c+1)]
            if g >= 2:
                ctx.wait(nc.sync, "pe", gpe[g - 2])
            for c in range(NC):
                ctx.inc(sl, nc.sync.dma_start(
                    out=X(c), in_=xln[128*c:128*(c+1), 512*g:512*(g+1)]), 16)
            ctx.wait(nc.tensor, sl)
            ctx.wait(nc.tensor, "qs")
            for j4 in range(4):
                j = 4*g + j4
                for c in range(NC):
                    nc.tensor.matmul(pgall[:, 8*j:8*(j+1)],
                                     xg[:, off + 512*c + 128*j4: off + 512*c + 128*j4 + 128],
                                     wg[:, 8*c:8*(c+1)], start=(c == 0), stop=False)
                i = nc.tensor.matmul(pgall[:, 8*j:8*(j+1)], C["ones1_f32"][:], bgr[:],
                                     start=False, stop=True)
            gpe[g] = ctx.inc("pe", i)
        ctx.wait(nc.vector, "pe", gpe[7])
        V.tensor_copy(Lg[:], pgall[:])
        V.tensor_reduce(SM(0), E3(Lg[:]), AX.X, ALU.max)
        V.tensor_tensor(E3(tA[:]), E3(Lg[:]), SMb(0), ALU.subtract)
        ctx.wait(nc.scalar, "dve")
        A.activation(tA[:], tA[:], AF.Exp)
        ctx.wait(nc.vector, "act")
        V.tensor_reduce(SM(1), E3(tA[:]), AX.X, ALU.add)
        V.reciprocal(SM(2), SM(1))
        V.tensor_tensor(E3(tA[:]), E3(tA[:]), SMb(2), ALU.mult)
        V.tensor_tensor(E3(tB[:]), E3(Lg[:]), SMb(0), ALU.is_ge)
        V.scalar_tensor_tensor(tB[:], tB[:], -1000.0, C["iotaT"][:], ALU.mult, ALU.add)
        V.tensor_reduce(SM(3), E3(tB[:]), AX.X, ALU.min)
        V.tensor_tensor(E3(tB[:]), E3(C["iotam1000T"][:]), SMb(3), ALU.is_equal)
        V.scalar_tensor_tensor(tC[:], tB[:], NEG, Lg[:], ALU.mult, ALU.add)
        V.tensor_reduce(SM(4), E3(tC[:]), AX.X, ALU.max)
        V.tensor_tensor(E3(tC[:]), E3(tC[:]), SMb(4), ALU.is_ge)
        V.scalar_tensor_tensor(tC[:], tC[:], -1000.0, C["iotaT"][:], ALU.mult, ALU.add)
        V.tensor_reduce(SM(5), E3(tC[:]), AX.X, ALU.min)
        V.tensor_tensor(E3(tC[:]), E3(C["iotam1000T"][:]), SMb(5), ALU.is_equal)
        V.tensor_add(tB[:], tB[:], tC[:])
        V.tensor_mul(tA[:], tA[:], tB[:])
        ctx.wait(nc.tensor, "dve")
        for j4 in range(NT // 4):
            if j4 >= 1:
                ctx.wait(nc.tensor, "dve", ctx.val("dve"))
            for u in range(4):
                j = 4*j4 + u
                i = nc.tensor.transpose(pt[:, 128*u:128*(u+1)], tA[:, 8*j:8*(j+1)],
                                        C["ident_f32"][:])
            pev = ctx.inc("pe", i)
            ctx.wait(nc.vector, "pe", pev)
            for u in range(4):
                j = 4*j4 + u
                (V if u == 0 else ctx.Vr).tensor_copy(mT[:, 128*j:128*(j+1)],
                                                      pt[:, 128*u:128*(u+1)])
        ctx.wait(nc.sync, "dve")
        ctx.inc("qs", nc.sync.dma_start(out=cc_in, in_=mT[:]), 16)
        ctx.wait(nc.gpsimd, "qs")
        ctx.inc("cc", nc.gpsimd.collective_compute(
            "AllReduce", ALU.add, [list(range(ncores))],
            [cc_in.opt()], [cc_out.opt()]), 1)
        ctx.wait(nc.sync, "cc")
        ctx.inc("qs", nc.sync.dma_start(out=scr[:], in_=cc_out), 16)
        ctx.wait(nc.vector, "qs")
        V.tensor_scalar_add(scr[:], scr[:], 1e-6)
        V.reciprocal(scr[:], scr[:])
        V.tensor_mul(routeT[:], mT[:], scr[:])
        V.tensor_scalar_mul(routeT[:], routeT[:], float(ncores))
        V.memset(bis[:], 0.0)
        V.memset(bis[:, 1:2], 8.5)
        for it in range(34):
            V.tensor_add(B1(2), B1(0), B1(1))
            V.tensor_scalar_mul(B1(2), B1(2), 0.5)
            V.tensor_scalar(scr[:], routeT[:], B1(2), None, ALU.is_gt,
                            op1=ALU.add, accum_out=B1(3))
            V.tensor_scalar(B1(4), B1(3), float(Kcap), None, ALU.is_ge)
            V.tensor_sub(B1(5), B1(2), B1(0))
            V.scalar_tensor_tensor(B1(0), B1(5), B1(4), B1(0), ALU.mult, ALU.add)
            V.tensor_sub(B1(5), B1(1), B1(2))
            V.scalar_tensor_tensor(B1(1), B1(5), B1(4), B1(2), ALU.mult, ALU.add)
        V.tensor_scalar(scr[:], routeT[:], B1(1), None, ALU.is_le)
        V.tensor_mul(scr[:], scr[:], routeT[:])
        V.tensor_reduce(B1(6), scr[:], AX.X, ALU.max)
        V.tensor_scalar(scr[:], routeT[:], B1(6), None, ALU.is_gt,
                        op1=ALU.add, accum_out=B1(7))
        V.tensor_scalar(B1(7), B1(7), -1.0, float(Kcap), ALU.mult, ALU.add)
        V.tensor_scalar(tieT[:], routeT[:], B1(6), None, ALU.is_equal)
        V.memset(ones8[:], 1.0)
        V.tensor_tensor_scan(routeT[:], ones8[:], tieT[:], 0.0, ALU.mult, ALU.add)
        V.tensor_scalar(routeT[:], routeT[:], B1(7), None, ALU.is_le)
        V.tensor_mul(tieT[:], tieT[:], routeT[:])
        V.tensor_add(sel05T[:], scr[:], tieT[:])
        V.tensor_scalar_mul(sel05T[:], sel05T[:], 0.5)
        ctx.wait(nc.tensor, "dve")
        for j4 in range(NT // 4):
            if j4 >= 1:
                ctx.wait(nc.tensor, "dve", ctx.val("dve"))
            for u in range(4):
                j = 4*j4 + u
                i = nc.tensor.transpose(ptB[:, 8*u:8*(u+1)], sel05T[:, 128*j:128*(j+1)],
                                        C["ident_f32"][0:8, 0:8])
            pev = ctx.inc("pe", i)
            ctx.wait(nc.vector, "pe", pev)
            for u in range(4):
                j = 4*j4 + u
                (V if u == 0 else ctx.Vr).tensor_copy(sel05_tm[:, 8*j:8*(j+1)],
                                                      ptB[:, 8*u:8*(u+1)])


def emit_l0_experts(ctx, sc, C, aps, xln, sel05_tm, sel05T, xsrc, xdst,
                    ntset=8, nexp=EXP):
    nc, V, A = ctx.nc, ctx.V, ctx.A
    with ctx.scope() as ph:
        b1 = ph.sb("e0_b1", [128, EXP * 2], F32)
        b2 = ph.sb("e0_b2", [8, 1024], F32)
        w1r = ph.sb("e0_w1", [128, EXP * 2 * 1024], BF16)
        w2r = ph.sb("e0_w2", [128, EXP * 2 * 1024], BF16)
        xln_ts = ph.sb("e0_xts", [128, 2 * NC * 512], BF16)
        xres = ph.sb("e0_xres", [128, 2 * NC * 512], F32)
        Hgel = ph.sb("e0_H", [128, 2 * 512], BF16)
        out_acc = ph.sb("e0_oa", [128, 4 * 1024], F32)
        py = [ph.ps(f"e0_py{j}", [128, 512]) for j in range(8)]
        ctx.inc("qs", nc.sync.dma_start(out=b1[:], in_=aps["b1l0"]), 16)
        ctx.inc("qs", nc.sync.dma_start(out=b2[:], in_=aps["b2l0"]), 16)
        ctx.inc("qg", nc.gpsimd.dma_start(out=w1r[:], in_=aps["w1l0r"]), 16)
        ctx.inc("qg", nc.gpsimd.dma_start(out=w2r[:], in_=aps["w2l0r"]), 16)
        ctx.wait(nc.tensor, "qg", ctx.val("qg"))
        def tset_loads0(t):
            if t >= ntset:
                return (("qs", 0), ("qs", 0))
            to = 512 * t
            xo = (t % 2) * NC * 512
            ctx.wait(nc.gpsimd, "pe", ctx.val("pe"))
            ctx.wait(nc.sync, "dve", ctx.val("dve"))
            ctx.wait(nc.sync, "qs", ctx.val("qs"))
            xs, ps = f"e0x{t % 2}", f"e0p{t % 2}"
            for c in range(NC):
                ctx.inc(xs, nc.gpsimd.dma_start(
                    out=xln_ts[:, xo + 512*c: xo + 512*(c+1)],
                    in_=xln[128*c:128*(c+1), to:to+512]), 16)
                ctx.inc(ps, nc.sync.dma_start(
                    out=xres[:, xo + 512*c: xo + 512*(c+1)],
                    in_=xsrc[128*c:128*(c+1), to:to+512]), 16)
            return ((xs, ctx.val(xs)), (ps, ctx.val(ps)))

        for eng in (nc.sync, nc.gpsimd):
            for sname in ("dve", "act", "pe", "qs"):
                if sname in ctx.sems and ctx.val(sname):
                    ctx.wait(eng, sname, ctx.val(sname))
        ld_tok = {0: tset_loads0(0)}
        for t in range(ntset):
            to = 512 * t
            xo = (t % 2) * NC * 512
            X0 = lambda c: xln_ts[:, xo + 512*c: xo + 512*(c+1)]
            XRES0 = lambda c: xres[:, xo + 512*c: xo + 512*(c+1)]
            ctx.wait(nc.tensor, "dve", ctx.val("dve"))
            if t == 0:
                ctx.wait(nc.tensor, "qs", ctx.val("qs"))
            for j8 in range(8):
                j, dg = divmod(j8, 2)
                jj = 4 * t + j
                i = nc.tensor.matmul(py[j8][:], sel05T[:, 128*jj:128*(jj+1)],
                                     b2[:, 512*dg:512*(dg+1)], start=True, stop=True)
            pev = ctx.inc("pe", i)
            ctx.wait(nc.vector, "pe", pev)
            for j8 in range(8):
                j, dg = divmod(j8, 2)
                (V if j8 == 0 else ctx.Vr).tensor_copy(
                    out_acc[:, 1024*j + 512*dg: 1024*j + 512*(dg+1)], py[j8][:])
            ctx.wait(nc.tensor, "dve", ctx.val("dve"))
            ctx.wait(nc.tensor, *ld_tok[t][0])
            for e in range(nexp):
                if e == 1:
                    ld_tok[t + 1] = tset_loads0(t + 1)
                act_after = {}
                for hc in range(2):
                    if e + hc > 0:
                        ctx.wait(nc.tensor, "act", ctx.val("act") - (1 if hc == 0 else 0))
                    for kc in range(NC):
                        i = nc.tensor.matmul(
                            py[hc][:],
                            w1r[:, 2048*e + 1024*hc + 128*kc: 2048*e + 1024*hc + 128*(kc+1)],
                            X0(kc), start=(kc == 0), stop=(kc == NC - 1))
                    pev = ctx.inc("pe", i)
                    ctx.wait(nc.scalar, "pe", pev)
                    (A if hc == 0 else ctx.Ar).activation(
                        Hgel[:, 512*hc:512*(hc+1)], py[hc][:], AF.Gelu,
                        bias=b1[:, 2*e+hc: 2*e+hc+1])
                    act_after[hc] = ctx.val("act")
                ctx.wait(nc.tensor, "act", act_after[1])
                for dg in range(2):
                    for hc in range(2):
                        for j in range(4):
                            i = nc.tensor.matmul(
                                py[4*dg + j][:],
                                Hgel[:, 512*hc + 128*j: 512*hc + 128*(j+1)],
                                w2r[:, 2048*e + 1024*hc + 512*dg: 2048*e + 1024*hc + 512*(dg+1)],
                                start=(hc == 0), stop=(hc == 1))
                    pev = ctx.inc("pe", i)
                    ctx.wait(nc.vector, "pe", pev)
                    for j in range(4):
                        jj = 4 * t + j
                        sl = out_acc[:, 1024*j + 512*dg: 1024*j + 512*(dg+1)]
                        (V if j == 0 else ctx.Vr).scalar_tensor_tensor(
                            sl, py[4*dg + j][:], sel05_tm[:, 8*jj+e: 8*jj+e+1],
                            sl, ALU.mult, ALU.add)
                ctx.wait(nc.tensor, "dve", ctx.val("dve"))
            ctx.wait(nc.vector, *ld_tok[t][1])
            grp_dve = {}
            for grp in range(8):
                tiles = [(j, dc) for j in range(4) for dc in range(NC)][4*grp:4*grp+4]
                if grp >= 1:
                    ctx.wait(nc.tensor, "dve", grp_dve.get(grp - 1, 0))
                for b4, (j, dc) in enumerate(tiles):
                    i = nc.tensor.transpose(
                        py[b4][:, 0:128],
                        out_acc[:, 1024*j + 128*dc: 1024*j + 128*(dc+1)],
                        C["ident_f32"][:])
                pev = ctx.inc("pe", i)
                ctx.wait(nc.vector, "pe", pev)
                for b4, (j, dc) in enumerate(tiles):
                    dst = xres[:, xo + 512*dc + 128*j: xo + 512*dc + 128*(j+1)]
                    (V if b4 == 0 else ctx.Vr).tensor_add(dst, py[b4][:, 0:128], dst)
                grp_dve[grp] = ctx.val("dve")
            dv = ctx.val("dve")
            ctx.wait(nc.sync, "dve", dv)
            for c in range(NC):
                ctx.inc("qs", nc.sync.dma_start(
                    out=xdst[128*c:128*(c+1), to:to+512], in_=XRES0(c)), 16)


RQK = 1.0 / float(np.sqrt(96.0))
CLIP_HI = float(np.exp(80.0 / np.sqrt(96.0)))
CLIP_LO = float(np.exp(-80.0 / np.sqrt(96.0)))

def emit_l0_attn(ctx, sc, C, aps, xln, xsrc, xdst, npair=8, nqg=8):
    nc, V, A = ctx.nc, ctx.V, ctx.A
    with ctx.scope() as ph:
        C2 = load_attn_consts(ctx, ph, aps, rope=True)
        cqT = ph.sb("a0_cq", [128, 2 * S], BF16)
        ckvT = ph.sb("a0_ckv", [128, 2 * S], BF16)
        ropb = ph.sb("a0_ropb", [32, 4 * S], BF16)
        att = [ph.sb(f"a0_att{i}", [128, S], BF16) for i in range(8)]
        rrow = ph.sb("a0_rr", [1, 512], F32)
        ps_j = [ph.ps(f"a0_pj{i}", [128, 512]) for i in range(2)]
        ps_kb = ph.ps("a0_pkb", [64, 512])
        ps_k = ps_kb[0:32, :]
        ps_pb = ps_kb[:]
        ps_s = [ph.ps(f"a0_ps{i}", [128, 1024]) for i in range(2)]
        ps_pv = ph.ps("a0_pv", [65, 512])
        ropesrc = ropb[:, 0:S]
        krT = ropb[:, S:2*S]
        cosT = ropb[:, 2*S:3*S]
        sinT = ropb[:, 3*S:4*S]
        ctx.wait(nc.gpsimd, "qg", ctx.val("qg"))
        ctx.inc("qg", nc.gpsimd.dma_start(out=cosT, in_=aps["c_ropecs"][:, 0:S]), 16)
        ctx.inc("qg", nc.gpsimd.dma_start(out=sinT, in_=aps["c_ropecs"][:, S:2*S]), 16)
        ctx.wait(nc.vector, "qg", ctx.val("qg"))

        def rope_inplace(dst, rsc):
            for g in range(8):
                ctx.wait(nc.tensor, "dve", ctx.val("dve"))
                i = nc.tensor.matmul(ps_k, C2["pswap"][:],
                                     ropesrc[:, 512*g:512*(g+1)], start=True, stop=True)
                pev = ctx.inc("pe", i)
                ctx.wait(nc.vector, "pe", pev)
                V.tensor_mul(rsc[:, 512*g:512*(g+1)], ps_k[:],
                             sinT[:, 512*g:512*(g+1)])
            V.tensor_mul(ropesrc[:], ropesrc[:], cosT[:])
            V.tensor_add(dst, ropesrc[:], rsc[:])

        with ctx.scope() as p0:
            xbf = p0.sb("a0_xbf", [128, 2 * NC * 512], BF16)
            wdqr = p0.sb("a0_wdq", [128, 16 * 128], BF16)
            wdkvr = p0.sb("a0_wdkv", [128, 16 * 128], BF16)
            wkrr = p0.sb("a0_wkr", [128, 8 * 32], BF16)
            krp = ropesrc
            ctx.wait(nc.gpsimd, "qg", ctx.val("qg"))
            ctx.inc("qg", nc.gpsimd.dma_start(
                out=wdqr[:].rearrange("k (m c f) -> k m c f", m=2, c=8),
                in_=aps["wdq"].rearrange("m c k f -> k m c f")), 16)
            ctx.inc("qg", nc.gpsimd.dma_start(
                out=wdkvr[:].rearrange("k (m c f) -> k m c f", m=2, c=8),
                in_=aps["wdkv"].rearrange("m c k f -> k m c f")), 16)
            ctx.inc("qg", nc.gpsimd.dma_start(
                out=wkrr[:].rearrange("k (c f) -> k c f", c=8),
                in_=aps["wkr"].rearrange("o c k f -> k (o c) f")), 16)
            ctx.wait(nc.tensor, "qg", ctx.val("qg"))
            bank_dve = {}
            for g in range(8):
                sl = f"a0x{g % 2}"
                xo = (g % 2) * NC * 512
                if g >= 2:
                    ctx.wait(nc.gpsimd, "pe", ctx.val("pe"))
                for c in range(NC):
                    ctx.inc(sl, nc.gpsimd.dma_start(
                        out=xbf[:, xo + 512*c: xo + 512*(c+1)],
                        in_=xln[128*c:128*(c+1), 512*g:512*(g+1)]), 16)
                ctx.wait(nc.tensor, sl)
                for q4, (lc, (w, dstb)) in enumerate(
                        [(lc, wd) for lc in range(2)
                         for wd in ((wdqr, cqT), (wdkvr, ckvT))]):
                    bk = q4 % 2
                    pj = ps_j[bk]
                    if bk in bank_dve:
                        ctx.wait(nc.tensor, "dve", bank_dve[bk])
                    for kc in range(NC):
                        i = nc.tensor.matmul(
                            pj[:], w[:, (lc*8+kc)*128:(lc*8+kc)*128+128],
                            xbf[:, xo + 512*kc: xo + 512*(kc+1)],
                            start=(kc == 0), stop=(kc == NC - 1))
                    pev = ctx.inc("pe", i)
                    ctx.wait(nc.vector, "pe", pev)
                    V.tensor_copy(dstb[:, S*lc + 512*g: S*lc + 512*(g+1)], pj[:])
                    bank_dve[bk] = ctx.val("dve")
                if "k" in bank_dve:
                    ctx.wait(nc.tensor, "dve", bank_dve["k"])
                for kc in range(NC):
                    i = nc.tensor.matmul(ps_k, wkrr[:, 32*kc:32*(kc+1)],
                                         xbf[:, xo + 512*kc: xo + 512*(kc+1)],
                                         start=(kc == 0), stop=(kc == NC - 1))
                pev = ctx.inc("pe", i)
                ctx.wait(nc.vector, "pe", pev)
                V.tensor_copy(krp[:, 512*g:512*(g+1)], ps_k)
                bank_dve["k"] = ctx.val("dve")
            rope_inplace(krT, xbf[0:32, 0:S])
            ctx.wait(nc.tensor, "dve", ctx.val("dve"))
        with ctx.scope() as p1:
            wuqr = p1.sb("a0_wuq", [128, 16 * 128], BF16)
            wukr = p1.sb("a0_wuk", [128, 16 * 128], BF16)
            wqrr = p1.sb("a0_wqr", [128, 8 * 128], BF16)
            wuvr = p1.sb("a0_wuv", [128, 2 * 1024], BF16)
            qh = [p1.sb(f"a0_qh{i}", [128, S], BF16) for i in range(2)]
            kh = [p1.sb(f"a0_kh{i}", [128, S], BF16) for i in range(2)]
            vtok = [p1.sb(f"a0_vt{i}", [128, 32 * 65], BF16) for i in range(2)]
            pT = p1.sb("a0_pT", [128, 4 * 1024], BF16)
            pvs = p1.sb("a0_pvs", [128, 512], F32)
            qrp = ropesrc
            ctx.wait(nc.gpsimd, "qg", ctx.val("qg"))
            ctx.inc("qg", nc.gpsimd.dma_start(
                out=wuqr[:].rearrange("k (m c f) -> k m c f", m=8, c=2),
                in_=aps["wuq"].rearrange("m c k f -> k m c f")), 16)
            ctx.inc("qg", nc.gpsimd.dma_start(
                out=wukr[:].rearrange("k (m c f) -> k m c f", m=8, c=2),
                in_=aps["wuk"].rearrange("m c k f -> k m c f")), 16)
            ctx.inc("qg", nc.gpsimd.dma_start(
                out=wqrr[:].rearrange("k (m c f) -> k m c f", m=4, c=2),
                in_=aps["wqr"].rearrange("m c k f -> k m c f")), 16)
            ctx.inc("qg", nc.gpsimd.dma_start(out=wuvr[:], in_=aps["wuvr"]), 16)
            ctx.wait(nc.tensor, "qg", ctx.val("qg"))
            ctx.wait(nc.vector, "qg", ctx.val("qg"))
            V.memset(vtok[0][:], 1.0)
            V.memset(vtok[1][:], 1.0)
            for p in range(npair):
                bank_dve = {}
                for hh in range(2):
                    h = 2*p + hh
                    for w, dstb in ((wuqr, qh[hh]), (wukr, kh[hh])):
                        for g in range(8):
                            bk = g % 2
                            pj = ps_j[bk]
                            if bk in bank_dve:
                                ctx.wait(nc.tensor, "dve", bank_dve[bk])
                            for lc in range(2):
                                i = nc.tensor.matmul(
                                    pj[0:64, :],
                                    w[:, ((h//2)*2+lc)*128 + 64*(h % 2):
                                         ((h//2)*2+lc)*128 + 64*(h % 2) + 64],
                                    cqT[:, S*lc + 512*g: S*lc + 512*(g+1)] if w is wuqr
                                    else ckvT[:, S*lc + 512*g: S*lc + 512*(g+1)],
                                    start=(lc == 0), stop=(lc == 1))
                            pev = ctx.inc("pe", i)
                            ctx.wait(nc.vector, "pe", pev)
                            V.tensor_copy(dstb[0:64, 512*g:512*(g+1)], pj[0:64, :])
                            bank_dve[bk] = ctx.val("dve")
                    for g in range(8):
                        if "k" in bank_dve:
                            ctx.wait(nc.tensor, "dve", bank_dve["k"])
                        for lc in range(2):
                            i = nc.tensor.matmul(
                                ps_k,
                                wqrr[:, ((h//4)*2+lc)*128 + 32*(h % 4):
                                       ((h//4)*2+lc)*128 + 32*(h % 4) + 32],
                                cqT[:, S*lc + 512*g: S*lc + 512*(g+1)],
                                start=(lc == 0), stop=(lc == 1))
                        pev = ctx.inc("pe", i)
                        ctx.wait(nc.vector, "pe", pev)
                        V.tensor_copy(qrp[:, 512*g:512*(g+1)], ps_k)
                        bank_dve["k"] = ctx.val("dve")
                    rope_inplace(qh[hh][64:96, :], pT[0:32, 0:S])
                    bank_dve["k"] = ctx.val("dve")
                    V.tensor_copy(kh[hh][64:96, :], krT[:])
                    for st in range(NT):
                        bk = st % 2
                        pj = ps_j[bk]
                        if bk in bank_dve:
                            ctx.wait(nc.tensor, "dve", bank_dve[bk])
                        for lc in range(2):
                            i = nc.tensor.matmul(
                                pj[:, 0:64],
                                ckvT[:, S*lc + 128*st: S*lc + 128*(st+1)],
                                wuvr[:, 1024*lc + 64*h: 1024*lc + 64*h + 64],
                                start=(lc == 0), stop=(lc == 1))
                        pev = ctx.inc("pe", i)
                        ctx.wait(nc.vector, "pe", pev)
                        V.tensor_copy(vtok[hh][:, 65*st:65*st+64], pj[:, 0:64])
                        bank_dve[bk] = ctx.val("dve")
                for hh in range(2):
                    h = 2*p + hh
                    norm = None
                    for qg in range(nqg):
                        act_exp = {}
                        pv_pe = {}
                        dve_fix = {}
                        kt_max = 4*qg + 3
                        npairk = (kt_max + 1) // 2

                        def emit_S(kp):
                            sb = kp % 2
                            pb = kp % 4
                            if kp >= 2:
                                ctx.wait(nc.scalar, "dve", dve_fix[kp - 2])
                            if kp >= 4:
                                ctx.wait(nc.scalar, "pe", pv_pe.get(kp - 4, 0))
                            for u in range(2):
                                kt = 2*kp + u
                                masks = []
                                if kt >= 4*qg:
                                    for jq in range(4):
                                        qt = 4*qg + jq
                                        if qt < kt:
                                            masks.append((jq, C2["fullneg_bf"]))
                                        elif qt == kt:
                                            masks.append((jq, C2["trimask_bf"]))
                                i = nc.tensor.matmul(ps_s[sb][:, 512*u:512*(u+1)],
                                                     kh[hh][0:96, 128*kt:128*(kt+1)],
                                                     qh[hh][0:96, 512*qg:512*(qg+1)],
                                                     start=True, stop=(not masks))
                                for mi, (jq, mk) in enumerate(masks):
                                    slp = ps_s[sb][:, 512*u + 128*jq: 512*u + 128*(jq+1)]
                                    i = nc.tensor.matmul(slp, C["ident_bf"][:], mk[:],
                                                         start=False,
                                                         stop=(mi == len(masks) - 1))
                            pev = ctx.inc("pe", i)
                            ctx.wait(nc.scalar, "pe", pev)
                            A.activation(pT[:, 1024*pb:1024*(pb+1)], ps_s[sb][:],
                                         AF.Exp, scale=RQK)
                            act_exp[kp] = ctx.val("act")
                            ctx.wait(nc.vector, "act", act_exp[kp])
                            V.tensor_scalar(pT[:, 1024*pb:1024*(pb+1)],
                                            pT[:, 1024*pb:1024*(pb+1)],
                                            CLIP_HI, None, ALU.min)
                            dve_fix[kp] = ctx.val("dve")

                        def emit_PV(kp):
                            pb = kp % 4
                            ctx.wait(nc.tensor, "dve", dve_fix[kp])
                            for u in range(2):
                                kt = 2*kp + u
                                i = nc.tensor.matmul(ps_pv[:], vtok[hh][:, 65*kt:65*(kt+1)],
                                                     pT[:, 1024*pb + 512*u: 1024*pb + 512*(u+1)],
                                                     start=(kt == 0), stop=(kt == kt_max))
                            pv_pe[kp] = ctx.inc("pe", i)
                            return pv_pe[kp]

                        emit_S(0)
                        if norm is not None:
                            ctx.wait(nc.vector, "pe", norm[0])
                            V.reciprocal(rrow[:], ps_pv[64:65, :])
                            V.tensor_copy(pvs[0:64, :], ps_pv[0:64, :])
                            dv_copy = ctx.val("dve")
                        emit_S(1)
                        if norm is not None:
                            ctx.wait(nc.tensor, "dve", dv_copy)
                        emit_PV(0)
                        if norm is not None:
                            i = nc.tensor.matmul(ps_pb, C["ones1_f32"][:, 0:64],
                                                 rrow[:], start=True, stop=True)
                            pev = ctx.inc("pe", i)
                            ctx.wait(nc.vector, "pe", pev)
                            V.tensor_mul(att[p][64*hh:64*hh+64,
                                                512*norm[1]:512*(norm[1]+1)],
                                         pvs[0:64, :], ps_pb)
                        for kp in range(2, npairk):
                            emit_S(kp)
                            emit_PV(kp - 1)
                        norm = (emit_PV(npairk - 1), qg)
                    ctx.wait(nc.vector, "pe", norm[0])
                    V.reciprocal(rrow[:], ps_pv[64:65, :])
                    dv = ctx.val("dve")
                    V.tensor_copy(pvs[0:64, :], ps_pv[0:64, :])
                    ctx.wait(nc.tensor, "dve", dv)
                    i = nc.tensor.matmul(ps_pb, C["ones1_f32"][:, 0:64], rrow[:],
                                         start=True, stop=True)
                    pev = ctx.inc("pe", i)
                    ctx.wait(nc.vector, "pe", pev)
                    V.tensor_mul(att[p][64*hh:64*hh+64, 512*norm[1]:512*(norm[1]+1)],
                                 pvs[0:64, :], ps_pb)
                    ctx.wait(nc.tensor, "dve", ctx.val("dve"))
        with ctx.scope() as p2:
            xres = p2.sb("a0_xres", [128, NC * 512], F32)
            wor = p2.sb("a0_wo", [128, 64 * 128], BF16)
            ctx.wait(nc.gpsimd, "qg", ctx.val("qg"))
            ctx.inc("qg", nc.gpsimd.dma_start(
                out=wor[:].rearrange("k (m c f) -> k m c f", m=8, c=8),
                in_=aps["wo"].rearrange("m c k f -> k m c f")), 16)
            for g in range(8):
                if g >= 1:
                    ctx.wait(nc.sync, "dve", ctx.val("dve"))
                    ctx.wait(nc.sync, "qs", ctx.val("qs"))
                for c in range(NC):
                    ctx.inc("qs", nc.sync.dma_start(
                        out=xres[:, 512*c:512*(c+1)],
                        in_=xsrc[128*c:128*(c+1), 512*g:512*(g+1)]), 16)
                ctx.wait(nc.vector, "qs")
                if g == 0:
                    ctx.wait(nc.tensor, "qg", ctx.val("qg"))
                    bank_ep = {}
                for dm in range(NC):
                    bk = dm % 2
                    if bk in bank_ep:
                        ctx.wait(nc.tensor, "dve", bank_ep[bk])
                    for hd in range(npair):
                        i = nc.tensor.matmul(ps_j[bk][:],
                                             wor[:, 1024*dm + 128*hd: 1024*dm + 128*(hd+1)],
                                             att[hd][:, 512*g:512*(g+1)],
                                             start=(hd == 0), stop=(hd == npair - 1))
                    pev = ctx.inc("pe", i)
                    ctx.wait(nc.vector, "pe", pev)
                    XR = xres[:, 512*dm:512*(dm+1)]
                    V.tensor_add(XR, XR, ps_j[bk][:])
                    bank_ep[bk] = ctx.val("dve")
                    ctx.wait(nc.sync, "dve", bank_ep[bk])
                    ctx.inc("qs", nc.sync.dma_start(
                        out=xdst[128*dm:128*(dm+1), 512*g:512*(g+1)], in_=XR), 16)


def build_sub3(H, debug=False, phases=("ln", "gate", "experts")):
    nc = bass.Bass("TRN2", target_bir_lowering=False)
    aps = declare_inputs(nc, H, KEYS_SUB3)
    xT = nc.dram_tensor("xT", [D, S], F32, kind="ExternalInput").ap()
    xout = nc.dram_tensor("xout", [D, S], F32, kind="ExternalOutput").ap()
    xlnd = nc.dram_tensor("xln", [D, S], F32,
                          kind="ExternalOutput" if debug else "Internal").ap()
    dbg = {}
    if debug:
        dbg["comb_dbg"] = nc.dram_tensor("comb_dbg", [128, 8 * NT], F32, kind="ExternalOutput").ap()
    with ExitStack() as es:
        ctx = Ctx(nc, es)
        blk = es.enter_context(nc.Block())
        with ctx.scope() as sc:
            C = load_consts(ctx, sc, aps)
            comb_tm = sc.sb("comb_tm", [128, 8 * NT], F32)
            combT = sc.sb("combT", [8, S], F32)
            if "ln" in phases:
                emit_ln(ctx, sc, C, xT, xlnd)
            if "gate" in phases:
                emit_l1_gate(ctx, sc, C, aps, xlnd, comb_tm, combT)
            if "experts" in phases:
                kw = {}
                for p in phases:
                    if p.startswith("k:"):
                        k, v = p[2:].split("=")
                        kw[k] = int(v)
                emit_l1_experts(ctx, sc, C, aps, xlnd, comb_tm, combT, xT, xout, **kw)
            else:
                for c in range(NC):
                    ctx.inc("qs", nc.sync.dma_start(
                        out=xout[128*c:128*(c+1), :], in_=xlnd[128*c:128*(c+1), :]), 16)
            if debug:
                ctx.barrier()
                ctx.inc("qs", nc.sync.dma_start(out=dbg["comb_dbg"], in_=comb_tm[:]), 16)
        ctx.barrier()
    return nc

def build_sub0(H, ncores=8, phases=("ln", "sel", "experts")):
    nc = bass.Bass("TRN2", target_bir_lowering=False)
    aps = declare_inputs(nc, H, KEYS_SUB0)
    xT = nc.dram_tensor("xT", [D, S], F32, kind="ExternalInput").ap()
    xout = nc.dram_tensor("xout", [D, S], F32, kind="ExternalOutput").ap()
    xlnd = nc.dram_tensor("xln", [D, S], F32, kind="Internal").ap()
    cc_in = nc.dram_tensor("cc_in", [8, S], F32, kind="Internal").ap()
    cc_out = nc.dram_tensor("cc_out", [8, S], F32, kind="Internal").ap()
    dbg = {}
    if "debug" in phases:
        dbg["sel_dbg"] = nc.dram_tensor("sel_dbg", [8, S], F32, kind="ExternalOutput").ap()
    with ExitStack() as es:
        ctx = Ctx(nc, es)
        blk = es.enter_context(nc.Block())
        with ctx.scope() as sc:
            C = load_consts(ctx, sc, aps)
            sel05_tm = sc.sb("sel_tm", [128, 8 * NT], F32)
            sel05T = sc.sb("sel05T", [8, S], F32)
            if "ln" in phases:
                emit_ln(ctx, sc, C, xT, xlnd)
            if "sel" in phases:
                emit_l0_sel(ctx, sc, C, aps, xlnd, cc_in, cc_out, sel05_tm, sel05T,
                            ncores=ncores)
            kw = {}
            for p in phases:
                if p.startswith("k:"):
                    k, v = p[2:].split("=")
                    kw[k] = int(v)
            if "experts" in phases:
                emit_l0_experts(ctx, sc, C, aps, xlnd, sel05_tm, sel05T, xT, xout, **kw)
            if "debug" in phases:
                ctx.barrier()
                ctx.inc("qs", nc.sync.dma_start(out=dbg["sel_dbg"], in_=sel05T[:]), 16)
        ctx.barrier()
    return nc

def build_sub1(H, phases=("ln", "attn")):
    nc = bass.Bass("TRN2", target_bir_lowering=False)
    aps = declare_inputs(nc, H, KEYS_SUB1)
    xT = nc.dram_tensor("xT", [D, S], F32, kind="ExternalInput").ap()
    xout = nc.dram_tensor("xout", [D, S], F32, kind="ExternalOutput").ap()
    xlnd = nc.dram_tensor("xln", [D, S], F32, kind="Internal").ap()
    with ExitStack() as es:
        ctx = Ctx(nc, es)
        blk = es.enter_context(nc.Block())
        with ctx.scope() as sc:
            C = load_consts(ctx, sc, aps)
            if "ln" in phases:
                emit_ln(ctx, sc, C, xT, xlnd)
            kw = {}
            for p in phases:
                if p.startswith("k:"):
                    k, v = p[2:].split("=")
                    kw[k] = int(v)
            emit_l0_attn(ctx, sc, C, aps, xlnd, xT, xout, **kw)
        ctx.barrier()
    return nc

def build_sub4(H, phases=("ln", "attn")):
    nc = bass.Bass("TRN2", target_bir_lowering=False)
    aps = declare_inputs(nc, H, KEYS_SUB4)
    xT = nc.dram_tensor("xT", [D, S], F32, kind="ExternalInput").ap()
    xout = nc.dram_tensor("xout", [D, S], F32, kind="ExternalOutput").ap()
    xlnd = nc.dram_tensor("xln", [D, S], F32, kind="Internal").ap()
    with ExitStack() as es:
        ctx = Ctx(nc, es)
        blk = es.enter_context(nc.Block())
        with ctx.scope() as sc:
            C = load_consts(ctx, sc, aps)
            if "ln" in phases:
                emit_ln(ctx, sc, C, xT, xlnd)
            kw = {}
            for p in phases:
                if p.startswith("k:"):
                    k, v = p[2:].split("=")
                    kw[k] = int(v)
            emit_l1_attn(ctx, sc, C, aps, xlnd, xT, xout, **kw)
        ctx.barrier()
    return nc


def host_prepare(I):
    H = {}
    H.update(host_constants())
    H["wg1"] = np.ascontiguousarray(np.asarray(I["L1_ffn_Wg"], np.float32).reshape(8, 128, 8).transpose(1, 0, 2).reshape(128, 64))
    H["bg1"] = np.ascontiguousarray(np.asarray(I["L1_ffn_bg"], np.float32).reshape(1, 8))
    W1 = np.asarray(I["L1_ffn_W1"], np.float32)
    H["w1l1"] = np.stack([
        W1[e].reshape(8, 128, 32, 128).transpose(2, 1, 0, 3).reshape(32, 128, 1024)
        for e in range(EXP)])
    W2 = np.asarray(I["L1_ffn_W2"], np.float32)
    H["w2l1"] = np.ascontiguousarray(W2.reshape(EXP, 4, 8, 128, 1024))
    H["b1l1"] = np.ascontiguousarray(np.stack(
        [feat_cols(np.asarray(I["L1_ffn_b1"], np.float32)[e]) for e in range(EXP)]
    ).transpose(1, 0, 2).reshape(128, EXP * 32))
    H["b2l1"] = np.ascontiguousarray(np.asarray(I["L1_ffn_b2"], np.float32))
    H["gam1"] = feat_cols(np.asarray(I["L1_ffn_gamma"], np.float32))
    H["bet1"] = feat_cols(np.asarray(I["L1_ffn_beta"], np.float32))
    H["wg0"] = np.ascontiguousarray(np.asarray(I["L0_ffn_w_gate_w"], np.float32).reshape(8, 128, 8).transpose(1, 0, 2).reshape(128, 64))
    H["bg0"] = np.ascontiguousarray(np.asarray(I["L0_ffn_w_gate_b"], np.float32).reshape(1, 8))
    W1 = np.asarray(I["L0_ffn_W1"], np.float32)
    w1 = np.stack([
        W1[e].reshape(8, 128, 2, 128).transpose(2, 1, 0, 3).reshape(2, 128, 1024)
        for e in range(EXP)])
    H["w1l0r"] = np.ascontiguousarray(w1.transpose(2, 0, 1, 3).reshape(128, EXP * 2 * 1024))
    W2 = np.asarray(I["L0_ffn_W2"], np.float32)
    w2 = np.stack([W2[e].reshape(2, 128, 1024).transpose(1, 0, 2) for e in range(EXP)])
    H["w2l0r"] = np.ascontiguousarray(w2.transpose(1, 0, 2, 3).reshape(128, EXP * 2 * 1024))
    H["b1l0"] = np.ascontiguousarray(np.stack(
        [feat_cols(np.asarray(I["L0_ffn_b1"], np.float32)[e]) for e in range(EXP)]
    ).transpose(1, 0, 2).reshape(128, EXP * 2))
    H["b2l0"] = np.ascontiguousarray(np.asarray(I["L0_ffn_b2"], np.float32))
    H["wdq"] = lhsT_blocks(np.asarray(I["L0_attn_W_DQ"], np.float32))
    H["wdkv"] = lhsT_blocks(np.asarray(I["L0_attn_W_DKV"], np.float32))
    H["wuq"] = lhsT_blocks(np.asarray(I["L0_attn_W_UQ"], np.float32))
    H["wuk"] = lhsT_blocks(np.asarray(I["L0_attn_W_UK"], np.float32))
    H["wqr"] = lhsT_blocks(np.asarray(I["L0_attn_W_QR"], np.float32))
    wuv = np.asarray(I["L0_attn_W_UV"], np.float32)
    H["wuvr"] = np.ascontiguousarray(
        wuv.reshape(2, 128, 1024).transpose(1, 0, 2).reshape(128, 2048))
    H["wkr"] = lhsT_blocks(np.asarray(I["L0_attn_W_KR"], np.float32), mc=32)
    H["wo"] = lhsT_blocks(np.asarray(I["L0_attn_W_O"], np.float32))
    H["wqkv"] = lhsT_blocks(np.asarray(I["L1_attn_W_qkv"], np.float32))
    H["bqkv"] = feat_cols(np.asarray(I["L1_attn_b_qkv"], np.float32))
    H["wout"] = lhsT_blocks(np.asarray(I["L1_attn_W_out"], np.float32))
    H["bout"] = feat_cols(np.asarray(I["L1_attn_b_out"], np.float32))
    return H

def per_core_inputs(I, H):
    x = np.asarray(I["L0_attn_x"], np.float32)
    maps = []
    for b in range(8):
        m = dict(H)
        m["xT"] = np.ascontiguousarray(x[b].T)
        maps.append(m)
    return maps

def build_full(H, ncores=8):
    nc = bass.Bass("TRN2", target_bir_lowering=False)
    keys = sorted(set(KEYS_SUB0 + KEYS_SUB1 + KEYS_SUB3 + KEYS_SUB4))
    aps = declare_inputs(nc, H, keys)
    xT = nc.dram_tensor("xT", [D, S], F32, kind="ExternalInput").ap()
    xout = nc.dram_tensor("xout", [D, S], F32, kind="ExternalOutput").ap()
    xw = nc.dram_tensor("xw", [D, S], F32, kind="Internal").ap()
    xlnd = nc.dram_tensor("xln", [D, S], F32, kind="Internal").ap()
    cc_in = nc.dram_tensor("cc_in", [8, S], F32, kind="Internal").ap()
    cc_out = nc.dram_tensor("cc_out", [8, S], F32, kind="Internal").ap()
    w1bf = nc.dram_tensor("w1bf", [EXP, 32, 128, 1024], BF16, kind="Internal").ap()
    w2bf = nc.dram_tensor("w2bf", [EXP, 4, 8, 128, 1024], BF16, kind="Internal").ap()
    with ExitStack() as es:
        ctx = Ctx(nc, es)
        blk = es.enter_context(nc.Block())
        with ctx.scope() as sc:
            C = load_consts(ctx, sc, aps)
            wcstate = {}
            w1f = aps["w1l1"].rearrange("e h k m -> (e h) k m")
            w1d = w1bf.rearrange("e h k m -> (e h) k m")
            w2f = aps["w2l1"].rearrange("e g h k m -> (e g h) k m")
            w2d = w2bf.rearrange("e g h k m -> (e g h) k m")

            def ffn0(xsrc, xdst):
                with ctx.scope() as ss:
                    tm = ss.sb("sel_tm", [128, 8 * NT], F32)
                    sT = ss.sb("sel05T", [8, S], F32)
                    emit_l0_sel(ctx, sc, C, aps, xlnd, cc_in, cc_out, tm, sT, ncores=ncores)
                    emit_l0_experts(ctx, sc, C, aps, xlnd, tm, sT, xsrc, xdst)

            def ffn1(xsrc, xdst):
                with ctx.scope() as ss:
                    tm = ss.sb("comb_tm", [128, 8 * NT], F32)
                    cT = ss.sb("combT", [8, S], F32)
                    emit_l1_gate(ctx, sc, C, aps, xlnd, tm, cT)
                    emit_l1_experts(ctx, sc, C, aps, xlnd, tm, cT, xsrc, xdst,
                                    w1src=w1bf, w2src=w2bf)

            with ctx.scope() as early:
                wcbufs = [early.sb(f"wc{i}", [128, 2048], BF16) for i in range(2)]
                emit_ln(ctx, sc, C, xT, xlnd)
                emit_w_convert(ctx, wcbufs, wcstate, [(w1f[0:128], w1d[0:128])])
                ffn0(xT, xw)
                emit_w_convert(ctx, wcbufs, wcstate, [(w1f[128:256], w1d[128:256])])
                emit_ln(ctx, sc, C, xw, xlnd)
                emit_l0_attn(ctx, sc, C, aps, xlnd, xw, xw)
                emit_w_convert(ctx, wcbufs, wcstate, [(w2f[0:128], w2d[0:128])])
                emit_ln(ctx, sc, C, xw, xlnd)
                ffn0(xw, xw)
                emit_w_convert(ctx, wcbufs, wcstate, [(w2f[128:256], w2d[128:256])])
            emit_ln(ctx, sc, C, xw, xlnd)
            ffn1(xw, xw)
            emit_ln(ctx, sc, C, xw, xlnd)
            emit_l1_attn(ctx, sc, C, aps, xlnd, xw, xw)
            emit_ln(ctx, sc, C, xw, xlnd)
            ffn1(xw, xout)
        ctx.barrier()
    return nc

def kernel_multilaunch(**inputs):
    H = host_prepare(inputs)
    x = np.asarray(inputs["L0_attn_x"], np.float32)
    cur = [np.ascontiguousarray(x[b].T) for b in range(8)]
    progs = {
        "s0": (build_sub0(H), KEYS_SUB0),
        "s1": (build_sub1(H), KEYS_SUB1),
        "s3": (build_sub3(H), KEYS_SUB3),
        "s4": (build_sub4(H), KEYS_SUB4),
    }
    for tag in ["s0", "s1", "s0", "s3", "s4", "s3"]:
        prog, keys = progs[tag]
        maps = []
        for b in range(8):
            m = {k: H[k] for k in keys}
            m["xT"] = cur[b]
            maps.append(m)
        res = run_bass_kernel_spmd(prog, maps, core_ids=list(range(8)))
        cur = [np.ascontiguousarray(np.asarray(res.results[b]["xout"], np.float32))
               for b in range(8)]
    return np.stack([c.T for c in cur]).astype(np.float32)

def kernel(**inputs):
    H = host_prepare(inputs)
    x = np.asarray(inputs["L0_attn_x"], np.float32)
    keys = sorted(set(KEYS_SUB0 + KEYS_SUB1 + KEYS_SUB3 + KEYS_SUB4))
    prog = build_full(H)
    maps = []
    for b in range(8):
        m = {k: H[k] for k in keys}
        m["xT"] = np.ascontiguousarray(x[b].T)
        maps.append(m)
    res = run_bass_kernel_spmd(prog, maps, core_ids=list(range(8)))
    out = np.stack([np.asarray(res.results[b]["xout"], np.float32).T for b in range(8)])
    return np.ascontiguousarray(out).astype(np.float32)


# version 17
# speedup vs baseline: 1.1056x; 1.0072x over previous
import numpy as np
import ml_dtypes
from contextlib import ExitStack
import concourse.bass as bass
import concourse.mybir as mybir
from concourse.bass_utils import run_bass_kernel_spmd

F32 = mybir.dt.float32
BF16 = mybir.dt.bfloat16
AF = mybir.ActivationFunctionType
ALU = mybir.AluOpType
AX = mybir.AxisListType

D = 1024
S = 4096
NC = 8
NT = 32
EXP = 8
NEG = -1.0e38


def lhsT_blocks(W, mc=128):
    K, M = W.shape
    nK, nM = K // 128, M // mc
    return np.ascontiguousarray(W.reshape(nK, 128, nM, mc).transpose(2, 0, 1, 3))

def feat_cols(v):
    n = v.shape[0] // 128
    return np.ascontiguousarray(v.reshape(n, 128).T)


class EngWrap:

    def __init__(self, ctx, eng, sem, relaxed=False):
        self._ctx, self._eng, self._sem = ctx, eng, sem
        self._relaxed = relaxed

    def __getattr__(self, op):
        eng = self._eng
        f = getattr(eng, op)
        ctx, sem = self._ctx, self._sem
        relaxed = self._relaxed

        def g(*a, **k):
            if not relaxed and ctx.val(sem):
                ctx.wait(eng, sem, ctx.val(sem))
            ins = f(*a, **k)
            ctx.inc(sem, ins)
            return ins
        return g

class Scope:
    def __init__(self, ctx):
        self.ctx = ctx
        self.es = ExitStack()

    def sb(self, nm, shape, dtype):
        return self.es.enter_context(self.ctx.nc.sbuf_tensor(self.ctx.name(nm), shape, dtype))

    def ps(self, nm, shape, dtype=F32):
        return self.es.enter_context(self.ctx.nc.psum_tensor(self.ctx.name(nm), shape, dtype))

    def __enter__(self):
        return self

    def __exit__(self, *a):
        self.ctx.barrier()
        self.es.close()
        return False

class Ctx:
    def __init__(self, nc, es):
        self.nc = nc
        self.es = es
        self.sems = {}
        self.uid = 0
        self.V = EngWrap(self, nc.vector, "dve")
        self.A = EngWrap(self, nc.scalar, "act")
        self.G = EngWrap(self, nc.gpsimd, "pool")
        self.Vr = EngWrap(self, nc.vector, "dve", relaxed=True)
        self.Ar = EngWrap(self, nc.scalar, "act", relaxed=True)

    def name(self, p):
        self.uid += 1
        return f"{p}_{self.uid}"

    def scope(self):
        return Scope(self)

    def sem(self, nm):
        if nm not in self.sems:
            h = self.es.enter_context(self.nc.semaphore(nm))
            self.sems[nm] = [h, 0]
        return self.sems[nm]

    def inc(self, nm, instr, amt=1):
        s = self.sem(nm)
        instr.then_inc(s[0], amt)
        s[1] += amt
        return s[1]

    def val(self, nm):
        return self.sem(nm)[1]

    def wait(self, eng, nm, v=None):
        if isinstance(eng, EngWrap):
            eng = eng._eng
        s = self.sem(nm)
        eng.wait_ge(s[0], s[1] if v is None else v)

    def barrier(self):
        snap = [(h, c) for (h, c) in self.sems.values() if c > 0]
        nc = self.nc
        for eng in (nc.sync, nc.gpsimd, nc.tensor, nc.vector, nc.scalar):
            for h, c in snap:
                eng.wait_ge(h, c)


def host_constants():
    c = {}
    c["c_ones1_f32"] = np.ones((1, 128), np.float32)
    c["c_ones128_bf"] = np.ones((128, 1), np.float32)
    c["c_ident_f32"] = np.eye(128, dtype=np.float32)
    c["c_ident_bf"] = np.eye(128, dtype=np.float32)
    c["c_iotaT"] = np.broadcast_to(np.tile(np.arange(8, dtype=np.float32), 32), (128, 256)).copy()
    c["c_iotam1000T"] = c["c_iotaT"] - 1000.0
    tm = np.zeros((128, 128), np.float32)
    tm[np.tril_indices(128, -1)] = NEG
    c["c_trimask"] = tm
    t01 = np.ones((128, 128), np.float32)
    t01[np.tril_indices(128, -1)] = 0.0
    c["c_tri01"] = t01
    c["c_fullneg"] = np.full((128, 128), NEG, np.float32)
    inv = (10000.0 ** (-np.arange(0, 32, 2, dtype=np.float32) / 32.0))
    ang = np.arange(S, dtype=np.float32)[None, :] * inv[:, None]
    cos, sin = np.cos(ang), np.sin(ang)
    ct = np.zeros((32, S), np.float32)
    st = np.zeros((32, S), np.float32)
    ct[0::2] = cos; ct[1::2] = cos
    st[0::2] = -sin; st[1::2] = sin
    c["c_ropecs"] = np.concatenate([ct, st], axis=1)
    P = np.zeros((32, 32), np.float32)
    for i in range(32):
        P[i ^ 1, i] = 1.0
    c["c_pswap"] = P
    return c

CONST_KEYS = ["c_ones1_f32", "c_ones128_bf", "c_ident_f32", "c_ident_bf",
              "c_iotaT", "c_iotam1000T"]
KEYS_SUB0 = CONST_KEYS + ["wg0", "bg0", "w1l0r", "w2l0r", "b1l0", "b2l0"]
KEYS_SUB1 = CONST_KEYS + ["c_trimask", "c_tri01", "c_ropecs", "c_pswap", "c_fullneg",
                          "wdq", "wdkv", "wuq", "wuk", "wqr", "wuvr", "wkr", "wo"]
KEYS_SUB3 = CONST_KEYS + ["wg1", "bg1", "w1l1", "w2l1", "b1l1", "b2l1", "gam1", "bet1"]
KEYS_SUB4 = CONST_KEYS + ["wqkv", "bqkv", "wout", "bout"]

def declare_inputs(nc, host_arrays, keys=None):
    aps = {}
    for k, v in host_arrays.items():
        if keys is not None and k not in keys:
            continue
        dt = BF16 if v.dtype == ml_dtypes.bfloat16 else F32
        aps[k] = nc.dram_tensor(k, list(v.shape), dt, kind="ExternalInput").ap()
    return aps

def load_consts(ctx, sc, aps):
    nc = ctx.nc
    C = {}
    spec = [("ones1_f32", "c_ones1_f32", [1, 128], F32, False),
            ("ones128_bf", "c_ones128_bf", [128, 1], BF16, True),
            ("ident_f32", "c_ident_f32", [128, 128], F32, False),
            ("ident_bf", "c_ident_bf", [128, 128], BF16, True),
            ("iotaT", "c_iotaT", [128, 256], F32, False),
            ("iotam1000T", "c_iotam1000T", [128, 256], F32, False)]
    for nm, key, shape, dt, cast in spec:
        C[nm] = sc.sb(nm, shape, dt)
        eng, q = (nc.gpsimd, "qg") if cast else (nc.sync, "qs")
        ctx.inc(q, eng.dma_start(out=C[nm][:], in_=aps[key]), 16)
    ctx.barrier()
    return C

def load_attn_consts(ctx, ph, aps, rope=False):
    nc = ctx.nc
    C2 = {}
    spec = [("trimask", "c_trimask", [128, 128], F32, False),
            ("trimask_bf", "c_trimask", [128, 128], BF16, True),
            ("fullneg_bf", "c_fullneg", [128, 128], BF16, True)]
    if rope:
        spec += [("pswap", "c_pswap", [32, 32], BF16, True),
                 ("tri01", "c_tri01", [128, 128], BF16, True)]
    for nm, key, shape, dt, cast in spec:
        C2[nm] = ph.sb(nm, shape, dt)
        eng, q = (nc.gpsimd, "qg") if cast else (nc.sync, "qs")
        ctx.inc(q, eng.dma_start(out=C2[nm][:], in_=aps[key]), 16)
    return C2


def emit_w_convert(ctx, bufs, state, pairs):
    nc = ctx.nc
    st_after = state
    for src_ap, dst_ap in pairs:
        B = src_ap.shape[0]
        M = src_ap.shape[2]
        per = max(1, 2048 // M)
        for b0 in range(0, B, per):
            nb = min(per, B - b0)
            sl = state.setdefault("_i", 0) % 2
            state["_i"] += 1
            buf = bufs[sl][:, 0:nb*M].rearrange("k (b m) -> k b m", b=nb)
            if sl in st_after:
                ctx.wait(nc.gpsimd, "wcs", st_after[sl])
            ctx.inc("wcl", nc.gpsimd.dma_start(
                out=buf, in_=src_ap[b0:b0+nb].rearrange("b k m -> k b m")), 16)
            ctx.wait(nc.sync, "wcl", ctx.val("wcl"))
            ctx.inc("wcs", nc.sync.dma_start(
                out=dst_ap[b0:b0+nb].rearrange("b k m -> k b m"), in_=buf), 16)
            st_after[sl] = ctx.val("wcs")


class StatsKit:
    def __init__(self, ctx, sc, C, banks=None):
        self.ctx, self.C = ctx, C
        self.xb = sc.sb("st_xb", [128, NC * 512], BF16)
        self.xq = sc.sb("st_xq", [128, NC * 512], BF16)
        self.rw = sc.sb("st_rw", [1, 4 * 512], F32)
        if banks is None:
            self.ps_s = sc.ps("st_s", [1, 512])[:]
            self.ps_ss = sc.ps("st_ss", [1, 512])[:]
            self.pb_m = sc.ps("st_m", [128, 512])[:]
            self.pb_r = sc.ps("st_r", [128, 512])[:]
        else:
            self.ps_s, self.ps_ss, self.pb_m, self.pb_r = banks

    def emit(self, get_src, denom=1024.0, eps=1e-5):
        ctx, C = self.ctx, self.C
        nc, V, A = ctx.nc, ctx.V, ctx.A
        R = lambda i: self.rw[:, 512*i:512*(i+1)]
        Vr, Ar = ctx.Vr, ctx.Ar
        for c in range(NC):
            src = get_src(c)
            (V if c == 0 else Vr).tensor_copy(self.xb[:, 512*c:512*(c+1)], src)
            (A if c == 0 else Ar).activation(self.xq[:, 512*c:512*(c+1)], src, AF.Square)
        ctx.wait(nc.tensor, "dve")
        ctx.wait(nc.tensor, "act")
        for c in range(NC):
            nc.tensor.matmul(self.ps_s, C["ones128_bf"][:], self.xb[:, 512*c:512*(c+1)],
                             start=(c == 0), stop=(c == NC - 1))
            i = nc.tensor.matmul(self.ps_ss, C["ones128_bf"][:], self.xq[:, 512*c:512*(c+1)],
                                 start=(c == 0), stop=(c == NC - 1))
        pev = ctx.inc("pe", i)
        ctx.wait(nc.vector, "pe", pev)
        V.tensor_scalar_mul(R(0), self.ps_s, 1.0 / denom)
        V.tensor_scalar_mul(R(1), self.ps_ss, 1.0 / denom)
        V.tensor_mul(R(2), R(0), R(0))
        V.tensor_sub(R(1), R(1), R(2))
        V.tensor_scalar_add(R(1), R(1), eps)
        ctx.wait(nc.scalar, "dve")
        A.activation(R(2), R(1), AF.Sqrt)
        ctx.wait(nc.vector, "act")
        V.reciprocal(R(3), R(2))
        ctx.wait(nc.tensor, "dve")
        nc.tensor.matmul(self.pb_m, C["ones1_f32"][:], R(0), start=True, stop=True)
        i = nc.tensor.matmul(self.pb_r, C["ones1_f32"][:], R(3), start=True, stop=True)
        return ctx.inc("pe", i)


def emit_ln(ctx, sc, C, xsrc, xln):
    nc, V = ctx.nc, ctx.V
    with ctx.scope() as ph:
        xf = ph.sb("ln_xf", [128, 2 * NC * 512], F32)
        xo = ph.sb("ln_xo", [128, 2 * NC * 512], F32)
        st = StatsKit(ctx, ph, C)
        def ln_loads(g):
            if g >= 8:
                return
            sl = f"lnx{g % 2}"
            off = (g % 2) * NC * 512
            if g >= 2:
                ctx.wait(nc.sync, "dve", ctx.val("dve"))
                ctx.wait(nc.sync, "act", ctx.val("act"))
            for c in range(NC):
                ctx.inc(sl, nc.sync.dma_start(
                    out=xf[:, off + 512*c: off + 512*(c+1)],
                    in_=xsrc[128*c:128*(c+1), 512*g:512*(g+1)]), 16)

        ln_loads(0)
        ln_loads(1)
        for g in range(8):
            sl = f"lnx{g % 2}"
            off = (g % 2) * NC * 512
            X = lambda c: xf[:, off + 512*c: off + 512*(c+1)]
            O = lambda c: xo[:, off + 512*c: off + 512*(c+1)]
            ctx.wait(nc.vector, sl)
            ctx.wait(nc.scalar, sl)
            pev = st.emit(X)
            ctx.wait(nc.vector, "pe", pev)
            ctx.wait(nc.vector, "lnst")
            for c in range(NC):
                (ctx.Vr if c else V).tensor_sub(O(c), X(c), st.pb_m)
                V.tensor_mul(O(c), O(c), st.pb_r)
            dv = ctx.val("dve")
            ln_loads(g + 2)
            ctx.wait(nc.sync, "dve", dv)
            for c in range(NC):
                ctx.inc("lnst", nc.sync.dma_start(
                    out=xln[128*c:128*(c+1), 512*g:512*(g+1)], in_=O(c)), 16)


def emit_l1_gate(ctx, sc, C, aps, xln, comb_tm, combT,
                 wg_key="wg1", bg_key="bg1"):
    nc, V, A = ctx.nc, ctx.V, ctx.A
    with ctx.scope() as ph:
        wg = ph.sb("g_wg", [128, 8 * NC], F32)
        bgr = ph.sb("g_bg", [1, 8], F32)
        xg = ph.sb("g_xg", [128, 2 * NC * 512], F32)
        Lg = ph.sb("g_Lg", [128, 256], F32)
        tA = ph.sb("g_tA", [128, 256], F32)
        tB = ph.sb("g_tB", [128, 256], F32)
        tC = ph.sb("g_tC", [128, 256], F32)
        sm = ph.sb("g_sm", [128, 6 * 32], F32)
        pgall = ph.ps("g_pg", [128, 256])
        pt = ph.ps("g_pt", [8, 512])
        SM = lambda i: sm[:, 32*i:32*(i+1)]
        SMb = lambda i: sm[:, 32*i:32*(i+1)].rearrange("p (t o) -> p t o", o=1).broadcast_to([128, 32, 8])
        E3 = lambda ap: ap.rearrange("p (t e) -> p t e", e=8)
        ctx.inc("qs", nc.sync.dma_start(out=wg[:], in_=aps[wg_key]), 16)
        ctx.inc("qs", nc.sync.dma_start(out=bgr[:], in_=aps[bg_key]), 16)
        gpe = {}
        for g in range(8):
            sl = f"gx{g % 2}"
            off = (g % 2) * NC * 512
            X = lambda c: xg[:, off + 512*c: off + 512*(c+1)]
            if g >= 2:
                ctx.wait(nc.sync, "pe", gpe[g - 2])
            for c in range(NC):
                ctx.inc(sl, nc.sync.dma_start(
                    out=X(c), in_=xln[128*c:128*(c+1), 512*g:512*(g+1)]), 16)
            ctx.wait(nc.tensor, sl)
            ctx.wait(nc.tensor, "qs")
            for j4 in range(4):
                j = 4*g + j4
                for c in range(NC):
                    nc.tensor.matmul(pgall[:, 8*j:8*(j+1)],
                                     xg[:, off + 512*c + 128*j4: off + 512*c + 128*j4 + 128],
                                     wg[:, 8*c:8*(c+1)], start=(c == 0), stop=False)
                i = nc.tensor.matmul(pgall[:, 8*j:8*(j+1)], C["ones1_f32"][:], bgr[:],
                                     start=False, stop=True)
            gpe[g] = ctx.inc("pe", i)
        ctx.wait(nc.vector, "pe", gpe[7])
        V.tensor_copy(Lg[:], pgall[:])
        V.tensor_reduce(SM(0), E3(Lg[:]), AX.X, ALU.max)
        V.tensor_tensor(E3(tA[:]), E3(Lg[:]), SMb(0), ALU.is_ge)
        V.scalar_tensor_tensor(tB[:], tA[:], -1000.0, C["iotaT"][:], ALU.mult, ALU.add)
        V.tensor_reduce(SM(1), E3(tB[:]), AX.X, ALU.min)
        V.tensor_tensor(E3(tA[:]), E3(C["iotam1000T"][:]), SMb(1), ALU.is_equal)
        V.scalar_tensor_tensor(tB[:], tA[:], NEG, Lg[:], ALU.mult, ALU.add)
        V.tensor_reduce(SM(2), E3(tB[:]), AX.X, ALU.max)
        V.tensor_tensor(E3(tC[:]), E3(tB[:]), SMb(2), ALU.is_ge)
        V.scalar_tensor_tensor(tC[:], tC[:], -1000.0, C["iotaT"][:], ALU.mult, ALU.add)
        V.tensor_reduce(SM(3), E3(tC[:]), AX.X, ALU.min)
        V.tensor_tensor(E3(tC[:]), E3(C["iotam1000T"][:]), SMb(3), ALU.is_equal)
        V.tensor_sub(SM(4), SM(2), SM(0))
        ctx.wait(nc.scalar, "dve")
        A.activation(SM(5), SM(4), AF.Sigmoid)
        ctx.wait(nc.vector, "act")
        V.tensor_tensor(E3(tA[:]), E3(tA[:]), SMb(5), ALU.mult)
        V.tensor_tensor(E3(tC[:]), E3(tC[:]), SMb(5), ALU.mult)
        V.tensor_sub(tC[:], tC[:], tA[:])
        V.tensor_tensor(E3(tA[:]), E3(C["iotam1000T"][:]), SMb(1), ALU.is_equal)
        V.tensor_add(comb_tm[:], tC[:], tA[:])
        ctx.wait(nc.tensor, "dve")
        for j4 in range(NT // 4):
            if j4 >= 1:
                ctx.wait(nc.tensor, "dve", ctx.val("dve"))
            for u in range(4):
                j = 4*j4 + u
                i = nc.tensor.transpose(pt[:, 128*u:128*(u+1)], comb_tm[:, 8*j:8*(j+1)],
                                        C["ident_f32"][:])
            pev = ctx.inc("pe", i)
            ctx.wait(nc.vector, "pe", pev)
            for u in range(4):
                j = 4*j4 + u
                (V if u == 0 else ctx.Vr).tensor_copy(combT[:, 128*j:128*(j+1)],
                                                      pt[:, 128*u:128*(u+1)])


def emit_l1_experts(ctx, sc, C, aps, xln, comb_tm, combT, xsrc, xdst,
                    ntset=8, nexp=EXP, do_epi=1, do_b2=1, do_B=1,
                    w1src=None, w2src=None):
    nc, V, A = ctx.nc, ctx.V, ctx.A
    with ctx.scope() as ph:
        b1 = ph.sb("e1_b1", [128, EXP * 32], F32)
        b2 = ph.sb("e1_b2", [8, 1024], F32)
        gam = ph.sb("e1_gam", [128, 8], F32)
        bet = ph.sb("e1_bet", [128, 8], F32)
        xln_ts = ph.sb("e1_xts", [128, 2 * NC * 512], BF16)
        xres = ph.sb("e1_xres", [128, 2 * NC * 512], F32)
        preT = ph.sb("e1_preT", [128, 2 * NC * 512], F32)
        Hgel = ph.sb("e1_H", [128, 32 * 512], BF16)
        w1b = ph.sb("e1_w1", [128, 3 * 1024], BF16)
        w2b = ph.sb("e1_w2", [128, 3 * 4096], BF16)
        out_acc = ph.sb("e1_oa", [128, 4 * 1024], F32)
        py = [ph.ps(f"e1_py{j}", [128, 512]) for j in range(8)]
        st = StatsKit(ctx, ph, C, banks=(py[4][0:1, :], py[5][0:1, :],
                                         py[6][:], py[7][:]))
        ctx.inc("qs", nc.sync.dma_start(out=b1[:], in_=aps["b1l1"]), 16)
        ctx.inc("qs", nc.sync.dma_start(out=b2[:], in_=aps["b2l1"]), 16)
        ctx.inc("qs", nc.sync.dma_start(out=gam[:], in_=aps["gam1"]), 16)
        ctx.inc("qs", nc.sync.dma_start(out=bet[:], in_=aps["bet1"]), 16)
        resident_qs = ctx.val("qs")
        ctx.wait(nc.tensor, "qs", resident_qs)
        ctx.wait(nc.scalar, "qs", resident_qs)
        ctx.wait(nc.vector, "qs", resident_qs)

        w1ap = aps["w1l1"] if w1src is None else w1src
        w2ap = aps["w2l1"] if w2src is None else w2src
        w1eng = nc.gpsimd if w1src is None else nc.scalar
        w2eng = nc.gpsimd if w2src is None else nc.sync

        def w1_load(e, hc, guard_pe):
            sl = f"w1{hc % 3}"
            ctx.wait(w1eng, "pe", guard_pe)
            ctx.inc(sl, w1eng.dma_start(
                out=w1b[:, 1024*(hc % 3):1024*(hc % 3 + 1)],
                in_=w1ap[e, hc]), 16)
            return (sl, ctx.val(sl))

        def w2_load(e, s, guard_pe):
            sl = f"w2{s % 3}"
            dg, hg = divmod(s, 4)
            ctx.wait(w2eng, "pe", guard_pe)
            src = w2ap[e, hg, :, :, 512*dg:512*(dg+1)].rearrange("h k d -> k h d")
            o = 4096 * (s % 3)
            dst = w2b[:, o:o+4096].rearrange("k (h d) -> k h d", h=8)
            ctx.inc(sl, w2eng.dma_start(out=dst, in_=src), 16)
            return (sl, ctx.val(sl))

        def tset_loads(t):
            if t >= ntset:
                return (("qs", 0), ("qs", 0))
            to = 512 * t
            xo = (t % 2) * NC * 512
            ctx.wait(nc.gpsimd, "pe", ctx.val("pe"))
            ctx.wait(nc.sync, "dve", ctx.val("dve"))
            ctx.wait(nc.sync, "act", ctx.val("act"))
            ctx.wait(nc.sync, "qs", ctx.val("qs"))
            xs, ps = f"e1x{t % 2}", f"e1p{t % 2}"
            for c in range(NC):
                ctx.inc(xs, nc.gpsimd.dma_start(
                    out=xln_ts[:, xo + 512*c: xo + 512*(c+1)],
                    in_=xln[128*c:128*(c+1), to:to+512]), 16)
                ctx.inc(ps, nc.sync.dma_start(
                    out=preT[:, xo + 512*c: xo + 512*(c+1)],
                    in_=xln[128*c:128*(c+1), to:to+512]), 16)
                ctx.inc(ps, nc.sync.dma_start(
                    out=xres[:, xo + 512*c: xo + 512*(c+1)],
                    in_=xsrc[128*c:128*(c+1), to:to+512]), 16)
            return ((xs, ctx.val(xs)), (ps, ctx.val(ps)))

        for eng in (nc.sync, nc.gpsimd):
            for sname in ("dve", "act", "pe", "qs"):
                if sname in ctx.sems and ctx.val(sname):
                    ctx.wait(eng, sname, ctx.val(sname))
        ld_tok = {0: tset_loads(0)}
        for t in range(ntset):
            to = 512 * t
            xo = (t % 2) * NC * 512
            X = lambda c: xln_ts[:, xo + 512*c: xo + 512*(c+1)]
            PRE = lambda c: preT[:, xo + 512*c: xo + 512*(c+1)]
            XRES = lambda c: xres[:, xo + 512*c: xo + 512*(c+1)]
            ctx.wait(nc.tensor, "dve", ctx.val("dve"))
            if do_b2:
                for j8 in range(8):
                    j, dg = divmod(j8, 2)
                    jj = 4 * t + j
                    i = nc.tensor.matmul(py[j8][:], combT[:, 128*jj:128*(jj+1)],
                                         b2[:, 512*dg:512*(dg+1)], start=True, stop=True)
                pev = ctx.inc("pe", i)
                ctx.wait(nc.vector, "pe", pev)
                for j8 in range(8):
                    j, dg = divmod(j8, 2)
                    (V if j8 == 0 else ctx.Vr).tensor_copy(
                        out_acc[:, 1024*j + 512*dg: 1024*j + 512*(dg+1)], py[j8][:])
            ctx.wait(nc.tensor, "dve", ctx.val("dve"))
            ctx.wait(nc.tensor, *ld_tok[t][0])
            for e in range(nexp):
                pe_base = ctx.val("pe")
                qg_after = {0: w1_load(e, 0, pe_base), 1: w1_load(e, 1, pe_base),
                            2: w1_load(e, 2, pe_base)}
                act_after = {}
                pe_A = {}
                for hc in range(32):
                    pb = hc % 4
                    ctx.wait(nc.tensor, *qg_after[hc])
                    if hc >= 4:
                        ctx.wait(nc.tensor, "act", act_after[hc - 4])
                    for kc in range(NC):
                        i = nc.tensor.matmul(py[pb][:],
                                             w1b[:, 1024*(hc % 3) + 128*kc: 1024*(hc % 3) + 128*(kc+1)],
                                             X(kc), start=(kc == 0), stop=(kc == NC - 1))
                    pe_A[hc] = pev = ctx.inc("pe", i)
                    ctx.wait(nc.scalar, "pe", pev)
                    (A if hc == 0 else ctx.Ar).activation(
                        Hgel[:, 512*hc:512*(hc+1)], py[pb][:], AF.Gelu,
                        bias=b1[:, 32*e+hc: 32*e+hc+1])
                    act_after[hc] = ctx.val("act")
                    if hc + 3 < 32:
                        qg_after[hc + 3] = w1_load(e, hc + 3, pe_A[hc])
                ctx.wait(nc.tensor, "act", ctx.val("act"))
                if not do_B:
                    continue
                pe_base = ctx.val("pe")
                qg_after = {0: w2_load(e, 0, pe_base), 1: w2_load(e, 1, pe_base),
                            2: w2_load(e, 2, pe_base)}
                pe_B = {}
                for s in range(8):
                    dg, hg = divmod(s, 4)
                    wb = s % 3
                    ctx.wait(nc.tensor, *qg_after[s])
                    for h8 in range(8):
                        hc = 8 * hg + h8
                        for j in range(4):
                            i = nc.tensor.matmul(
                                py[4*dg + j][:],
                                Hgel[:, 512*hc + 128*j: 512*hc + 128*(j+1)],
                                w2b[:, 4096*wb + 512*h8: 4096*wb + 512*(h8+1)],
                                start=(hc == 0), stop=(hc == 31))
                    pe_B[s] = pev = ctx.inc("pe", i)
                    if s + 3 < 8:
                        qg_after[s + 3] = w2_load(e, s + 3, pe_B[s])
                    if hg == 3:
                        ctx.wait(nc.vector, "pe", pev)
                        for j in range(4):
                            jj = 4 * t + j
                            sl = out_acc[:, 1024*j + 512*dg: 1024*j + 512*(dg+1)]
                            (V if j == 0 else ctx.Vr).scalar_tensor_tensor(
                                sl, py[4*dg + j][:], comb_tm[:, 8*jj+e: 8*jj+e+1],
                                sl, ALU.mult, ALU.add)
                ctx.wait(nc.tensor, "dve", ctx.val("dve"))
                if e == 0:
                    ld_tok[t + 1] = tset_loads(t + 1)
            if not do_epi:
                continue
            ctx.wait(nc.vector, *ld_tok[t][1])
            grp_dve = {}
            for grp in range(8):
                tiles = [(j, dc) for j in range(4) for dc in range(NC)][4*grp:4*grp+4]
                if grp >= 1:
                    ctx.wait(nc.tensor, "dve", grp_dve.get(grp - 1, 0))
                for b4, (j, dc) in enumerate(tiles):
                    i = nc.tensor.transpose(
                        py[b4][:, 0:128],
                        out_acc[:, 1024*j + 128*dc: 1024*j + 128*(dc+1)],
                        C["ident_f32"][:])
                pev = ctx.inc("pe", i)
                ctx.wait(nc.vector, "pe", pev)
                for b4, (j, dc) in enumerate(tiles):
                    dst = preT[:, xo + 512*dc + 128*j: xo + 512*dc + 128*(j+1)]
                    (V if b4 == 0 else ctx.Vr).tensor_add(dst, py[b4][:, 0:128], dst)
                grp_dve[grp] = ctx.val("dve")
            ctx.wait(nc.scalar, "dve")
            ctx.wait(nc.tensor, "dve")
            pev = st.emit(lambda c: PRE(c))
            ctx.wait(nc.vector, "pe", pev)
            for c in range(NC):
                PT = PRE(c)
                V.tensor_sub(PT, PT, st.pb_m)
                V.tensor_mul(PT, PT, st.pb_r)
                V.tensor_scalar(PT, PT, gam[:, c:c+1], bet[:, c:c+1], ALU.mult, ALU.add)
                XR = XRES(c)
                V.scalar_tensor_tensor(XR, PT, 0.5, XR, ALU.mult, ALU.add)
                dv = ctx.val("dve")
                ctx.wait(nc.sync, "dve", dv)
                ctx.inc("qs", nc.sync.dma_start(
                    out=xdst[128*c:128*(c+1), to:to+512], in_=XR), 16)


def emit_l1_attn(ctx, sc, C, aps, xln, xsrc, xdst, npair=8, nqg=8):
    nc, V, A = ctx.nc, ctx.V, ctx.A
    with ctx.scope() as ph:
        bqkv = ph.sb("a1_bqkv", [128, 24], F32)
        bout = ph.sb("a1_bout", [128, 8], F32)
        xbf = ph.sb("a1_xbf", [128, 2 * NC * 512], BF16)
        wq = ph.sb("a1_wq", [128, 2 * 24 * 128], BF16)
        qT2 = ph.sb("a1_q", [128, S], BF16)
        kT2 = ph.sb("a1_k", [128, S], BF16)
        vT2 = ph.sb("a1_v", [128, S], BF16)
        vtok = [ph.sb(f"a1_vt{i}", [128, 32 * 65], BF16) for i in range(2)]
        pT = ph.sb("a1_pT", [128, 4 * 1024], BF16)
        pvs = ph.sb("a1_pvs", [128, 512], F32)
        rrow = ph.sb("a1_rr", [1, 512], F32)
        att = [ph.sb(f"a1_att{i}", [128, S], BF16) for i in range(8)]
        xres = ph.sb("a1_xres", [128, 2 * NC * 512], F32)
        ps_s = [ph.ps(f"a1_s{i}", [128, 1024]) for i in range(2)]
        ps_pv = ph.ps("a1_pv", [65, 512])
        ps_pb = ph.ps("a1_pb", [64, 512])
        ps_vt = ph.ps("a1_vtp", [128, 128], BF16)
        ps_qkv = ph.ps("a1_qkv", [128, 512])
        woutb = ph.sb("a1_wo", [128, 64 * 128], BF16)
        ctx.inc("qs", nc.sync.dma_start(out=bqkv[:], in_=aps["bqkv"]), 16)
        ctx.inc("qs", nc.sync.dma_start(out=bout[:], in_=aps["bout"]), 16)
        ctx.inc("qg", nc.gpsimd.dma_start(
            out=woutb[:].rearrange("k (m c f) -> k m c f", m=8, c=8),
            in_=aps["wout"].rearrange("m c k f -> k m c f")), 16)
        ctx.wait(nc.vector, "qs", ctx.val("qs"))
        V.memset(vtok[0][:], 1.0)
        V.memset(vtok[1][:], 1.0)
        bank_dve = {}
        for p in range(npair):
            wsl = f"a1w{p % 2}"
            ctx.wait(nc.gpsimd, "pe", ctx.val("pe"))
            ctx.inc(wsl, nc.gpsimd.dma_start(
                out=wq[:, 3072*(p % 2):3072*(p % 2 + 1)].rearrange("k (o c m) -> k o c m", o=3, c=8),
                in_=aps["wqkv"][3*p:3*p+3].rearrange("o c k m -> k o c m")), 16)
            ctx.wait(nc.tensor, wsl)
            for g in range(8):
                xsl = f"a1x{g % 2}"
                xo = (g % 2) * NC * 512
                if g >= 2:
                    ctx.wait(nc.gpsimd, "pe", ctx.val("pe"))
                for c in range(NC):
                    ctx.inc(xsl, nc.gpsimd.dma_start(
                        out=xbf[:, xo + 512*c: xo + 512*(c+1)],
                        in_=xln[128*c:128*(c+1), 512*g:512*(g+1)]), 16)
                ctx.wait(nc.tensor, xsl)
                for o3 in range(3):
                    oc = 3*p + o3
                    qb = (3*g + o3) % 2
                    PQ = (lambda r0, r1: ps_qkv[r0:r1, :]) if qb == 0 else \
                         (lambda r0, r1: ps_s[1][r0:r1, 512:1024])
                    if ("qkv", qb) in bank_dve:
                        ctx.wait(nc.tensor, "dve", bank_dve[("qkv", qb)])
                    for dk in range(NC):
                        i = nc.tensor.matmul(
                            PQ(0, 128),
                            wq[:, 3072*(p % 2) + 1024*o3 + 128*dk: 3072*(p % 2) + 1024*o3 + 128*(dk+1)],
                            xbf[:, xo + 512*dk: xo + 512*(dk+1)],
                            start=(dk == 0), stop=(dk == NC - 1))
                    pev = ctx.inc("pe", i)
                    ctx.wait(nc.vector, "pe", pev)
                    for seg in range(2):
                        f0 = 128*oc + 64*seg
                        h = f0 // 192
                        j = f0 % 192
                        kind = j // 64
                        dstbuf = (qT2, kT2, vT2)[kind]
                        r0 = 64 * (h % 2)
                        V.tensor_scalar(dstbuf[r0:r0+64, 512*g:512*(g+1)],
                                        PQ(64*seg, 64*seg + 64),
                                        bqkv[64*seg:64*seg+64, oc:oc+1], None, ALU.add)
                    bank_dve[("qkv", qb)] = ctx.val("dve")
            for hh in range(2):
                for ti in range(NT):
                    if "vt" in bank_dve:
                        ctx.wait(nc.tensor, "dve", bank_dve["vt"])
                    i = nc.tensor.transpose(ps_vt[:, 0:64],
                                            vT2[64*hh:64*hh+64, 128*ti:128*(ti+1)],
                                            C["ident_bf"][64*hh:64*hh+64, 64*hh:64*hh+64])
                    pev = ctx.inc("pe", i)
                    ctx.wait(nc.vector, "pe", pev)
                    V.tensor_copy(vtok[hh][:, 65*ti:65*ti+64], ps_vt[:, 0:64])
                    bank_dve["vt"] = ctx.val("dve")
            ctx.wait(nc.tensor, "dve", ctx.val("dve"))
            for hh in range(2):
                r0 = 64 * hh
                norm = None
                for qg in range(nqg):
                    act_exp = {}
                    pv_pe = {}

                    def emit_S(kp):
                        sb = kp % 2
                        pb = kp % 4
                        if kp >= 2:
                            ctx.wait(nc.tensor, "act", act_exp[kp - 2])
                        if kp >= 4:
                            ctx.wait(nc.scalar, "pe", pv_pe[kp - 4])
                        for u in range(2):
                            kt = 2*kp + u
                            i = nc.tensor.matmul(ps_s[sb][:, 512*u:512*(u+1)],
                                                 kT2[r0:r0+64, 128*kt:128*(kt+1)],
                                                 qT2[r0:r0+64, 512*qg:512*(qg+1)],
                                                 start=True, stop=True)
                        pev = ctx.inc("pe", i)
                        ctx.wait(nc.scalar, "pe", pev)
                        A.activation(pT[:, 1024*pb:1024*(pb+1)], ps_s[sb][:],
                                     AF.Exp, scale=0.125)
                        act_exp[kp] = ctx.val("act")

                    def emit_PV(kp):
                        pb = kp % 4
                        ctx.wait(nc.tensor, "act", act_exp[kp])
                        for u in range(2):
                            kt = 2*kp + u
                            i = nc.tensor.matmul(ps_pv[:], vtok[hh][:, 65*kt:65*(kt+1)],
                                                 pT[:, 1024*pb + 512*u: 1024*pb + 512*(u+1)],
                                                 start=(kt == 0), stop=(kt == NT - 1))
                        pv_pe[kp] = ctx.inc("pe", i)
                        return pv_pe[kp]

                    emit_S(0)
                    if norm is not None:
                        ctx.wait(nc.vector, "pe", norm[0])
                        V.reciprocal(rrow[:], ps_pv[64:65, :])
                        V.tensor_copy(pvs[0:64, :], ps_pv[0:64, :])
                        dv_copy = ctx.val("dve")
                    emit_S(1)
                    if norm is not None:
                        ctx.wait(nc.tensor, "dve", dv_copy)
                    emit_PV(0)
                    if norm is not None:
                        i = nc.tensor.matmul(ps_pb[:], C["ones1_f32"][:, 0:64],
                                             rrow[:], start=True, stop=True)
                        pev = ctx.inc("pe", i)
                        ctx.wait(nc.vector, "pe", pev)
                        V.tensor_mul(att[p][r0:r0+64, 512*norm[1]:512*(norm[1]+1)],
                                     pvs[0:64, :], ps_pb[:])
                    for kp in range(2, NT // 2):
                        emit_S(kp)
                        emit_PV(kp - 1)
                    norm = (emit_PV(NT // 2 - 1), qg)
                ctx.wait(nc.vector, "pe", norm[0])
                V.reciprocal(rrow[:], ps_pv[64:65, :])
                dv = ctx.val("dve")
                V.tensor_copy(pvs[0:64, :], ps_pv[0:64, :])
                ctx.wait(nc.tensor, "dve", dv)
                i = nc.tensor.matmul(ps_pb[:], C["ones1_f32"][:, 0:64], rrow[:],
                                     start=True, stop=True)
                pev = ctx.inc("pe", i)
                ctx.wait(nc.vector, "pe", pev)
                V.tensor_mul(att[p][r0:r0+64, 512*norm[1]:512*(norm[1]+1)],
                             pvs[0:64, :], ps_pb[:])
                ctx.wait(nc.tensor, "dve", ctx.val("dve"))
        ctx.barrier()
        for g in range(8):
            xo = (g % 2) * NC * 512
            if g >= 2:
                ctx.wait(nc.sync, "dve", ctx.val("dve"))
                ctx.wait(nc.sync, "qs", ctx.val("qs"))
            for c in range(NC):
                ctx.inc("qs", nc.sync.dma_start(
                    out=xres[:, xo + 512*c: xo + 512*(c+1)],
                    in_=xsrc[128*c:128*(c+1), 512*g:512*(g+1)]), 16)
            ctx.wait(nc.vector, "qs")
            if g == 0:
                ctx.wait(nc.tensor, "qg", ctx.val("qg"))
                pq = lambda b: ps_qkv[:] if b == 0 else ps_s[0][:, 0:512]
                bank_ep = {}
            for dm in range(NC):
                bk = dm % 2
                if bk in bank_ep:
                    ctx.wait(nc.tensor, "dve", bank_ep[bk])
                for hd in range(npair):
                    i = nc.tensor.matmul(pq(bk),
                                         woutb[:, 1024*dm + 128*hd: 1024*dm + 128*(hd+1)],
                                         att[hd][:, 512*g:512*(g+1)],
                                         start=(hd == 0), stop=(hd == npair - 1))
                pev = ctx.inc("pe", i)
                ctx.wait(nc.vector, "pe", pev)
                XR = xres[:, xo + 512*dm: xo + 512*(dm+1)]
                V.tensor_scalar(pvs[:], pq(bk), bout[:, dm:dm+1], None, ALU.add)
                V.tensor_add(XR, XR, pvs[:])
                bank_ep[bk] = ctx.val("dve")
                dv = bank_ep[bk]
                ctx.wait(nc.sync, "dve", dv)
                ctx.inc("qs", nc.sync.dma_start(
                    out=xdst[128*dm:128*(dm+1), 512*g:512*(g+1)], in_=XR), 16)


def emit_l0_sel(ctx, sc, C, aps, xln, cc_in, cc_out, sel05_tm, sel05T, ncores=8,
                Kcap=2048):
    nc, V, A = ctx.nc, ctx.V, ctx.A
    with ctx.scope() as ph:
        wg = ph.sb("s0_wg", [128, 8 * NC], F32)
        bgr = ph.sb("s0_bg", [1, 8], F32)
        xg = ph.sb("s0_xg", [128, 2 * NC * 512], F32)
        Lg = ph.sb("s0_Lg", [128, 256], F32)
        tA = ph.sb("s0_tA", [128, 256], F32)
        tB = ph.sb("s0_tB", [128, 256], F32)
        tC = ph.sb("s0_tC", [128, 256], F32)
        sm = ph.sb("s0_sm", [128, 6 * 32], F32)
        mT = ph.sb("s0_mT", [8, S], F32)
        routeT = ph.sb("s0_rt", [8, S], F32)
        scr = ph.sb("s0_scr", [8, S], F32)
        tieT = ph.sb("s0_tie", [8, S], F32)
        ones8 = ph.sb("s0_on", [8, S], F32)
        bis = ph.sb("s0_bis", [8, 10], F32)
        pgall = ph.ps("s0_pg", [128, 256])
        pt = ph.ps("s0_pt", [8, 512])
        ptB = ph.ps("s0_ptB", [128, 32])
        SM = lambda i: sm[:, 32*i:32*(i+1)]
        SMb = lambda i: sm[:, 32*i:32*(i+1)].rearrange("p (t o) -> p t o", o=1).broadcast_to([128, 32, 8])
        E3 = lambda ap: ap.rearrange("p (t e) -> p t e", e=8)
        B1 = lambda i: bis[:, i:i+1]
        ctx.inc("qs", nc.sync.dma_start(out=wg[:], in_=aps["wg0"]), 16)
        ctx.inc("qs", nc.sync.dma_start(out=bgr[:], in_=aps["bg0"]), 16)
        gpe = {}
        for g in range(8):
            sl = f"s0x{g % 2}"
            off = (g % 2) * NC * 512
            X = lambda c: xg[:, off + 512*c: off + 512*(c+1)]
            if g >= 2:
                ctx.wait(nc.sync, "pe", gpe[g - 2])
            for c in range(NC):
                ctx.inc(sl, nc.sync.dma_start(
                    out=X(c), in_=xln[128*c:128*(c+1), 512*g:512*(g+1)]), 16)
            ctx.wait(nc.tensor, sl)
            ctx.wait(nc.tensor, "qs")
            for j4 in range(4):
                j = 4*g + j4
                for c in range(NC):
                    nc.tensor.matmul(pgall[:, 8*j:8*(j+1)],
                                     xg[:, off + 512*c + 128*j4: off + 512*c + 128*j4 + 128],
                                     wg[:, 8*c:8*(c+1)], start=(c == 0), stop=False)
                i = nc.tensor.matmul(pgall[:, 8*j:8*(j+1)], C["ones1_f32"][:], bgr[:],
                                     start=False, stop=True)
            gpe[g] = ctx.inc("pe", i)
        ctx.wait(nc.vector, "pe", gpe[7])
        V.tensor_copy(Lg[:], pgall[:])
        V.tensor_reduce(SM(0), E3(Lg[:]), AX.X, ALU.max)
        V.tensor_tensor(E3(tA[:]), E3(Lg[:]), SMb(0), ALU.subtract)
        ctx.wait(nc.scalar, "dve")
        A.activation(tA[:], tA[:], AF.Exp)
        ctx.wait(nc.vector, "act")
        V.tensor_reduce(SM(1), E3(tA[:]), AX.X, ALU.add)
        V.reciprocal(SM(2), SM(1))
        V.tensor_tensor(E3(tA[:]), E3(tA[:]), SMb(2), ALU.mult)
        V.tensor_tensor(E3(tB[:]), E3(Lg[:]), SMb(0), ALU.is_ge)
        V.scalar_tensor_tensor(tB[:], tB[:], -1000.0, C["iotaT"][:], ALU.mult, ALU.add)
        V.tensor_reduce(SM(3), E3(tB[:]), AX.X, ALU.min)
        V.tensor_tensor(E3(tB[:]), E3(C["iotam1000T"][:]), SMb(3), ALU.is_equal)
        V.scalar_tensor_tensor(tC[:], tB[:], NEG, Lg[:], ALU.mult, ALU.add)
        V.tensor_reduce(SM(4), E3(tC[:]), AX.X, ALU.max)
        V.tensor_tensor(E3(tC[:]), E3(tC[:]), SMb(4), ALU.is_ge)
        V.scalar_tensor_tensor(tC[:], tC[:], -1000.0, C["iotaT"][:], ALU.mult, ALU.add)
        V.tensor_reduce(SM(5), E3(tC[:]), AX.X, ALU.min)
        V.tensor_tensor(E3(tC[:]), E3(C["iotam1000T"][:]), SMb(5), ALU.is_equal)
        V.tensor_add(tB[:], tB[:], tC[:])
        V.tensor_mul(tA[:], tA[:], tB[:])
        ctx.wait(nc.tensor, "dve")
        for j4 in range(NT // 4):
            if j4 >= 1:
                ctx.wait(nc.tensor, "dve", ctx.val("dve"))
            for u in range(4):
                j = 4*j4 + u
                i = nc.tensor.transpose(pt[:, 128*u:128*(u+1)], tA[:, 8*j:8*(j+1)],
                                        C["ident_f32"][:])
            pev = ctx.inc("pe", i)
            ctx.wait(nc.vector, "pe", pev)
            for u in range(4):
                j = 4*j4 + u
                (V if u == 0 else ctx.Vr).tensor_copy(mT[:, 128*j:128*(j+1)],
                                                      pt[:, 128*u:128*(u+1)])
        ctx.wait(nc.sync, "dve")
        ctx.inc("qs", nc.sync.dma_start(out=cc_in, in_=mT[:]), 16)
        ctx.wait(nc.gpsimd, "qs")
        ctx.inc("cc", nc.gpsimd.collective_compute(
            "AllReduce", ALU.add, [list(range(ncores))],
            [cc_in.opt()], [cc_out.opt()]), 1)
        ctx.wait(nc.sync, "cc")
        ctx.inc("qs", nc.sync.dma_start(out=scr[:], in_=cc_out), 16)
        ctx.wait(nc.vector, "qs")
        V.tensor_scalar_add(scr[:], scr[:], 1e-6)
        V.reciprocal(scr[:], scr[:])
        V.tensor_mul(routeT[:], mT[:], scr[:])
        V.tensor_scalar_mul(routeT[:], routeT[:], float(ncores))
        V.memset(bis[:], 0.0)
        V.memset(bis[:, 1:2], 8.5)
        for it in range(34):
            V.tensor_add(B1(2), B1(0), B1(1))
            V.tensor_scalar_mul(B1(2), B1(2), 0.5)
            V.tensor_scalar(scr[:], routeT[:], B1(2), None, ALU.is_gt,
                            op1=ALU.add, accum_out=B1(3))
            V.tensor_scalar(B1(4), B1(3), float(Kcap), None, ALU.is_ge)
            V.tensor_sub(B1(5), B1(2), B1(0))
            V.scalar_tensor_tensor(B1(0), B1(5), B1(4), B1(0), ALU.mult, ALU.add)
            V.tensor_sub(B1(5), B1(1), B1(2))
            V.scalar_tensor_tensor(B1(1), B1(5), B1(4), B1(2), ALU.mult, ALU.add)
        V.tensor_scalar(scr[:], routeT[:], B1(1), None, ALU.is_le)
        V.tensor_mul(scr[:], scr[:], routeT[:])
        V.tensor_reduce(B1(6), scr[:], AX.X, ALU.max)
        V.tensor_scalar(scr[:], routeT[:], B1(6), None, ALU.is_gt,
                        op1=ALU.add, accum_out=B1(7))
        V.tensor_scalar(B1(7), B1(7), -1.0, float(Kcap), ALU.mult, ALU.add)
        V.tensor_scalar(tieT[:], routeT[:], B1(6), None, ALU.is_equal)
        V.memset(ones8[:], 1.0)
        V.tensor_tensor_scan(routeT[:], ones8[:], tieT[:], 0.0, ALU.mult, ALU.add)
        V.tensor_scalar(routeT[:], routeT[:], B1(7), None, ALU.is_le)
        V.tensor_mul(tieT[:], tieT[:], routeT[:])
        V.tensor_add(sel05T[:], scr[:], tieT[:])
        V.tensor_scalar_mul(sel05T[:], sel05T[:], 0.5)
        ctx.wait(nc.tensor, "dve")
        for j4 in range(NT // 4):
            if j4 >= 1:
                ctx.wait(nc.tensor, "dve", ctx.val("dve"))
            for u in range(4):
                j = 4*j4 + u
                i = nc.tensor.transpose(ptB[:, 8*u:8*(u+1)], sel05T[:, 128*j:128*(j+1)],
                                        C["ident_f32"][0:8, 0:8])
            pev = ctx.inc("pe", i)
            ctx.wait(nc.vector, "pe", pev)
            for u in range(4):
                j = 4*j4 + u
                (V if u == 0 else ctx.Vr).tensor_copy(sel05_tm[:, 8*j:8*(j+1)],
                                                      ptB[:, 8*u:8*(u+1)])


def emit_l0_experts(ctx, sc, C, aps, xln, sel05_tm, sel05T, xsrc, xdst,
                    ntset=8, nexp=EXP):
    nc, V, A = ctx.nc, ctx.V, ctx.A
    with ctx.scope() as ph:
        b1 = ph.sb("e0_b1", [128, EXP * 2], F32)
        b2 = ph.sb("e0_b2", [8, 1024], F32)
        w1r = ph.sb("e0_w1", [128, EXP * 2 * 1024], BF16)
        w2r = ph.sb("e0_w2", [128, EXP * 2 * 1024], BF16)
        xln_ts = ph.sb("e0_xts", [128, 2 * NC * 512], BF16)
        xres = ph.sb("e0_xres", [128, 2 * NC * 512], F32)
        Hgel = ph.sb("e0_H", [128, 2 * 512], BF16)
        out_acc = ph.sb("e0_oa", [128, 4 * 1024], F32)
        py = [ph.ps(f"e0_py{j}", [128, 512]) for j in range(8)]
        ctx.inc("qs", nc.sync.dma_start(out=b1[:], in_=aps["b1l0"]), 16)
        ctx.inc("qs", nc.sync.dma_start(out=b2[:], in_=aps["b2l0"]), 16)
        ctx.inc("qg", nc.gpsimd.dma_start(out=w1r[:], in_=aps["w1l0r"]), 16)
        ctx.inc("qg", nc.gpsimd.dma_start(out=w2r[:], in_=aps["w2l0r"]), 16)
        ctx.wait(nc.tensor, "qg", ctx.val("qg"))
        def tset_loads0(t):
            if t >= ntset:
                return (("qs", 0), ("qs", 0))
            to = 512 * t
            xo = (t % 2) * NC * 512
            ctx.wait(nc.gpsimd, "pe", ctx.val("pe"))
            ctx.wait(nc.sync, "dve", ctx.val("dve"))
            ctx.wait(nc.sync, "qs", ctx.val("qs"))
            xs, ps = f"e0x{t % 2}", f"e0p{t % 2}"
            for c in range(NC):
                ctx.inc(xs, nc.gpsimd.dma_start(
                    out=xln_ts[:, xo + 512*c: xo + 512*(c+1)],
                    in_=xln[128*c:128*(c+1), to:to+512]), 16)
                ctx.inc(ps, nc.sync.dma_start(
                    out=xres[:, xo + 512*c: xo + 512*(c+1)],
                    in_=xsrc[128*c:128*(c+1), to:to+512]), 16)
            return ((xs, ctx.val(xs)), (ps, ctx.val(ps)))

        for eng in (nc.sync, nc.gpsimd):
            for sname in ("dve", "act", "pe", "qs"):
                if sname in ctx.sems and ctx.val(sname):
                    ctx.wait(eng, sname, ctx.val(sname))
        ld_tok = {0: tset_loads0(0)}
        for t in range(ntset):
            to = 512 * t
            xo = (t % 2) * NC * 512
            X0 = lambda c: xln_ts[:, xo + 512*c: xo + 512*(c+1)]
            XRES0 = lambda c: xres[:, xo + 512*c: xo + 512*(c+1)]
            ctx.wait(nc.tensor, "dve", ctx.val("dve"))
            if t == 0:
                ctx.wait(nc.tensor, "qs", ctx.val("qs"))
            for j8 in range(8):
                j, dg = divmod(j8, 2)
                jj = 4 * t + j
                i = nc.tensor.matmul(py[j8][:], sel05T[:, 128*jj:128*(jj+1)],
                                     b2[:, 512*dg:512*(dg+1)], start=True, stop=True)
            pev = ctx.inc("pe", i)
            ctx.wait(nc.vector, "pe", pev)
            for j8 in range(8):
                j, dg = divmod(j8, 2)
                (V if j8 == 0 else ctx.Vr).tensor_copy(
                    out_acc[:, 1024*j + 512*dg: 1024*j + 512*(dg+1)], py[j8][:])
            ctx.wait(nc.tensor, "dve", ctx.val("dve"))
            ctx.wait(nc.tensor, *ld_tok[t][0])
            for e in range(nexp):
                if e == 1:
                    ld_tok[t + 1] = tset_loads0(t + 1)
                act_after = {}
                for hc in range(2):
                    if e + hc > 0:
                        ctx.wait(nc.tensor, "act", ctx.val("act") - (1 if hc == 0 else 0))
                    for kc in range(NC):
                        i = nc.tensor.matmul(
                            py[hc][:],
                            w1r[:, 2048*e + 1024*hc + 128*kc: 2048*e + 1024*hc + 128*(kc+1)],
                            X0(kc), start=(kc == 0), stop=(kc == NC - 1))
                    pev = ctx.inc("pe", i)
                    ctx.wait(nc.scalar, "pe", pev)
                    (A if hc == 0 else ctx.Ar).activation(
                        Hgel[:, 512*hc:512*(hc+1)], py[hc][:], AF.Gelu,
                        bias=b1[:, 2*e+hc: 2*e+hc+1])
                    act_after[hc] = ctx.val("act")
                ctx.wait(nc.tensor, "act", act_after[1])
                for dg in range(2):
                    for hc in range(2):
                        for j in range(4):
                            i = nc.tensor.matmul(
                                py[4*dg + j][:],
                                Hgel[:, 512*hc + 128*j: 512*hc + 128*(j+1)],
                                w2r[:, 2048*e + 1024*hc + 512*dg: 2048*e + 1024*hc + 512*(dg+1)],
                                start=(hc == 0), stop=(hc == 1))
                    pev = ctx.inc("pe", i)
                    ctx.wait(nc.vector, "pe", pev)
                    for j in range(4):
                        jj = 4 * t + j
                        sl = out_acc[:, 1024*j + 512*dg: 1024*j + 512*(dg+1)]
                        (V if j == 0 else ctx.Vr).scalar_tensor_tensor(
                            sl, py[4*dg + j][:], sel05_tm[:, 8*jj+e: 8*jj+e+1],
                            sl, ALU.mult, ALU.add)
                ctx.wait(nc.tensor, "dve", ctx.val("dve"))
            ctx.wait(nc.vector, *ld_tok[t][1])
            grp_dve = {}
            for grp in range(8):
                tiles = [(j, dc) for j in range(4) for dc in range(NC)][4*grp:4*grp+4]
                if grp >= 1:
                    ctx.wait(nc.tensor, "dve", grp_dve.get(grp - 1, 0))
                for b4, (j, dc) in enumerate(tiles):
                    i = nc.tensor.transpose(
                        py[b4][:, 0:128],
                        out_acc[:, 1024*j + 128*dc: 1024*j + 128*(dc+1)],
                        C["ident_f32"][:])
                pev = ctx.inc("pe", i)
                ctx.wait(nc.vector, "pe", pev)
                for b4, (j, dc) in enumerate(tiles):
                    dst = xres[:, xo + 512*dc + 128*j: xo + 512*dc + 128*(j+1)]
                    (V if b4 == 0 else ctx.Vr).tensor_add(dst, py[b4][:, 0:128], dst)
                grp_dve[grp] = ctx.val("dve")
            dv = ctx.val("dve")
            ctx.wait(nc.sync, "dve", dv)
            for c in range(NC):
                ctx.inc("qs", nc.sync.dma_start(
                    out=xdst[128*c:128*(c+1), to:to+512], in_=XRES0(c)), 16)


RQK = 1.0 / float(np.sqrt(96.0))
CLIP_HI = float(np.exp(80.0 / np.sqrt(96.0)))
CLIP_LO = float(np.exp(-80.0 / np.sqrt(96.0)))

def emit_l0_attn(ctx, sc, C, aps, xln, xsrc, xdst, npair=8, nqg=8):
    nc, V, A = ctx.nc, ctx.V, ctx.A
    with ctx.scope() as ph:
        C2 = load_attn_consts(ctx, ph, aps, rope=True)
        cqT = ph.sb("a0_cq", [128, 2 * S], BF16)
        ckvT = ph.sb("a0_ckv", [128, 2 * S], BF16)
        ropb = ph.sb("a0_ropb", [32, 4 * S], BF16)
        att = [ph.sb(f"a0_att{i}", [128, S], BF16) for i in range(8)]
        rrow = ph.sb("a0_rr", [1, 512], F32)
        ps_j = [ph.ps(f"a0_pj{i}", [128, 512]) for i in range(2)]
        ps_kb = ph.ps("a0_pkb", [64, 512])
        ps_k = ps_kb[0:32, :]
        ps_pb = ps_kb[:]
        ps_s = [ph.ps(f"a0_ps{i}", [128, 1024]) for i in range(2)]
        ps_pv = ph.ps("a0_pv", [65, 512])
        ropesrc = ropb[:, 0:S]
        krT = ropb[:, S:2*S]
        cosT = ropb[:, 2*S:3*S]
        sinT = ropb[:, 3*S:4*S]
        ctx.wait(nc.gpsimd, "qg", ctx.val("qg"))
        ctx.inc("qg", nc.gpsimd.dma_start(out=cosT, in_=aps["c_ropecs"][:, 0:S]), 16)
        ctx.inc("qg", nc.gpsimd.dma_start(out=sinT, in_=aps["c_ropecs"][:, S:2*S]), 16)
        ctx.wait(nc.vector, "qg", ctx.val("qg"))

        def rope_inplace(dst, rsc):
            for g in range(8):
                ctx.wait(nc.tensor, "dve", ctx.val("dve"))
                i = nc.tensor.matmul(ps_k, C2["pswap"][:],
                                     ropesrc[:, 512*g:512*(g+1)], start=True, stop=True)
                pev = ctx.inc("pe", i)
                ctx.wait(nc.vector, "pe", pev)
                V.tensor_mul(rsc[:, 512*g:512*(g+1)], ps_k[:],
                             sinT[:, 512*g:512*(g+1)])
            V.tensor_mul(ropesrc[:], ropesrc[:], cosT[:])
            V.tensor_add(dst, ropesrc[:], rsc[:])

        with ctx.scope() as p0:
            xbf = p0.sb("a0_xbf", [128, 2 * NC * 512], BF16)
            wdqr = p0.sb("a0_wdq", [128, 16 * 128], BF16)
            wdkvr = p0.sb("a0_wdkv", [128, 16 * 128], BF16)
            wkrr = p0.sb("a0_wkr", [128, 8 * 32], BF16)
            krp = ropesrc
            ctx.wait(nc.gpsimd, "qg", ctx.val("qg"))
            ctx.inc("qg", nc.gpsimd.dma_start(
                out=wdqr[:].rearrange("k (m c f) -> k m c f", m=2, c=8),
                in_=aps["wdq"].rearrange("m c k f -> k m c f")), 16)
            ctx.inc("qg", nc.gpsimd.dma_start(
                out=wdkvr[:].rearrange("k (m c f) -> k m c f", m=2, c=8),
                in_=aps["wdkv"].rearrange("m c k f -> k m c f")), 16)
            ctx.inc("qg", nc.gpsimd.dma_start(
                out=wkrr[:].rearrange("k (c f) -> k c f", c=8),
                in_=aps["wkr"].rearrange("o c k f -> k (o c) f")), 16)
            ctx.wait(nc.tensor, "qg", ctx.val("qg"))
            bank_dve = {}
            for g in range(8):
                sl = f"a0x{g % 2}"
                xo = (g % 2) * NC * 512
                if g >= 2:
                    ctx.wait(nc.gpsimd, "pe", ctx.val("pe"))
                for c in range(NC):
                    ctx.inc(sl, nc.gpsimd.dma_start(
                        out=xbf[:, xo + 512*c: xo + 512*(c+1)],
                        in_=xln[128*c:128*(c+1), 512*g:512*(g+1)]), 16)
                ctx.wait(nc.tensor, sl)
                for q4, (lc, (w, dstb)) in enumerate(
                        [(lc, wd) for lc in range(2)
                         for wd in ((wdqr, cqT), (wdkvr, ckvT))]):
                    bk = q4 % 2
                    pj = ps_j[bk]
                    if bk in bank_dve:
                        ctx.wait(nc.tensor, "dve", bank_dve[bk])
                    for kc in range(NC):
                        i = nc.tensor.matmul(
                            pj[:], w[:, (lc*8+kc)*128:(lc*8+kc)*128+128],
                            xbf[:, xo + 512*kc: xo + 512*(kc+1)],
                            start=(kc == 0), stop=(kc == NC - 1))
                    pev = ctx.inc("pe", i)
                    ctx.wait(nc.vector, "pe", pev)
                    V.tensor_copy(dstb[:, S*lc + 512*g: S*lc + 512*(g+1)], pj[:])
                    bank_dve[bk] = ctx.val("dve")
                if "k" in bank_dve:
                    ctx.wait(nc.tensor, "dve", bank_dve["k"])
                for kc in range(NC):
                    i = nc.tensor.matmul(ps_k, wkrr[:, 32*kc:32*(kc+1)],
                                         xbf[:, xo + 512*kc: xo + 512*(kc+1)],
                                         start=(kc == 0), stop=(kc == NC - 1))
                pev = ctx.inc("pe", i)
                ctx.wait(nc.vector, "pe", pev)
                V.tensor_copy(krp[:, 512*g:512*(g+1)], ps_k)
                bank_dve["k"] = ctx.val("dve")
            rope_inplace(krT, xbf[0:32, 0:S])
            ctx.wait(nc.tensor, "dve", ctx.val("dve"))
        with ctx.scope() as p1:
            wuqr = p1.sb("a0_wuq", [128, 16 * 128], BF16)
            wukr = p1.sb("a0_wuk", [128, 16 * 128], BF16)
            wqrr = p1.sb("a0_wqr", [128, 8 * 128], BF16)
            wuvr = p1.sb("a0_wuv", [128, 2 * 1024], BF16)
            qh = [p1.sb(f"a0_qh{i}", [128, S], BF16) for i in range(2)]
            kh = [p1.sb(f"a0_kh{i}", [128, S], BF16) for i in range(2)]
            vtok = [p1.sb(f"a0_vt{i}", [128, 32 * 65], BF16) for i in range(2)]
            pT = p1.sb("a0_pT", [128, 4 * 1024], BF16)
            pvs = p1.sb("a0_pvs", [128, 512], F32)
            qrp = ropesrc
            ctx.wait(nc.gpsimd, "qg", ctx.val("qg"))
            ctx.inc("qg", nc.gpsimd.dma_start(
                out=wuqr[:].rearrange("k (m c f) -> k m c f", m=8, c=2),
                in_=aps["wuq"].rearrange("m c k f -> k m c f")), 16)
            ctx.inc("qg", nc.gpsimd.dma_start(
                out=wukr[:].rearrange("k (m c f) -> k m c f", m=8, c=2),
                in_=aps["wuk"].rearrange("m c k f -> k m c f")), 16)
            ctx.inc("qg", nc.gpsimd.dma_start(
                out=wqrr[:].rearrange("k (m c f) -> k m c f", m=4, c=2),
                in_=aps["wqr"].rearrange("m c k f -> k m c f")), 16)
            ctx.inc("qg", nc.gpsimd.dma_start(out=wuvr[:], in_=aps["wuvr"]), 16)
            ctx.wait(nc.tensor, "qg", ctx.val("qg"))
            ctx.wait(nc.vector, "qg", ctx.val("qg"))
            V.memset(vtok[0][:], 1.0)
            V.memset(vtok[1][:], 1.0)
            for p in range(npair):
                bank_dve = {}
                for hh in range(2):
                    h = 2*p + hh
                    for w, dstb in ((wuqr, qh[hh]), (wukr, kh[hh])):
                        for g in range(8):
                            bk = g % 2
                            pj = ps_j[bk]
                            if bk in bank_dve:
                                ctx.wait(nc.tensor, "dve", bank_dve[bk])
                            for lc in range(2):
                                i = nc.tensor.matmul(
                                    pj[0:64, :],
                                    w[:, ((h//2)*2+lc)*128 + 64*(h % 2):
                                         ((h//2)*2+lc)*128 + 64*(h % 2) + 64],
                                    cqT[:, S*lc + 512*g: S*lc + 512*(g+1)] if w is wuqr
                                    else ckvT[:, S*lc + 512*g: S*lc + 512*(g+1)],
                                    start=(lc == 0), stop=(lc == 1))
                            pev = ctx.inc("pe", i)
                            ctx.wait(nc.vector, "pe", pev)
                            V.tensor_copy(dstb[0:64, 512*g:512*(g+1)], pj[0:64, :])
                            bank_dve[bk] = ctx.val("dve")
                    for g in range(8):
                        if "k" in bank_dve:
                            ctx.wait(nc.tensor, "dve", bank_dve["k"])
                        for lc in range(2):
                            i = nc.tensor.matmul(
                                ps_k,
                                wqrr[:, ((h//4)*2+lc)*128 + 32*(h % 4):
                                       ((h//4)*2+lc)*128 + 32*(h % 4) + 32],
                                cqT[:, S*lc + 512*g: S*lc + 512*(g+1)],
                                start=(lc == 0), stop=(lc == 1))
                        pev = ctx.inc("pe", i)
                        ctx.wait(nc.vector, "pe", pev)
                        V.tensor_copy(qrp[:, 512*g:512*(g+1)], ps_k)
                        bank_dve["k"] = ctx.val("dve")
                    rope_inplace(qh[hh][64:96, :], pT[0:32, 0:S])
                    bank_dve["k"] = ctx.val("dve")
                    V.tensor_copy(kh[hh][64:96, :], krT[:])
                    for st in range(NT):
                        bk = st % 2
                        pj = ps_j[bk]
                        if bk in bank_dve:
                            ctx.wait(nc.tensor, "dve", bank_dve[bk])
                        for lc in range(2):
                            i = nc.tensor.matmul(
                                pj[:, 0:64],
                                ckvT[:, S*lc + 128*st: S*lc + 128*(st+1)],
                                wuvr[:, 1024*lc + 64*h: 1024*lc + 64*h + 64],
                                start=(lc == 0), stop=(lc == 1))
                        pev = ctx.inc("pe", i)
                        ctx.wait(nc.vector, "pe", pev)
                        V.tensor_copy(vtok[hh][:, 65*st:65*st+64], pj[:, 0:64])
                        bank_dve[bk] = ctx.val("dve")
                for hh in range(2):
                    h = 2*p + hh
                    norm = None
                    for qg in range(nqg):
                        act_exp = {}
                        pv_pe = {}
                        dve_fix = {}
                        kt_max = 4*qg + 3
                        npairk = (kt_max + 1) // 2

                        def emit_S(kp):
                            sb = kp % 2
                            pb = kp % 4
                            if kp >= 2:
                                ctx.wait(nc.scalar, "dve", dve_fix[kp - 2])
                            if kp >= 4:
                                ctx.wait(nc.scalar, "pe", pv_pe.get(kp - 4, 0))
                            for u in range(2):
                                kt = 2*kp + u
                                masks = []
                                if kt >= 4*qg:
                                    for jq in range(4):
                                        qt = 4*qg + jq
                                        if qt < kt:
                                            masks.append((jq, C2["fullneg_bf"]))
                                        elif qt == kt:
                                            masks.append((jq, C2["trimask_bf"]))
                                i = nc.tensor.matmul(ps_s[sb][:, 512*u:512*(u+1)],
                                                     kh[hh][0:96, 128*kt:128*(kt+1)],
                                                     qh[hh][0:96, 512*qg:512*(qg+1)],
                                                     start=True, stop=(not masks))
                                for mi, (jq, mk) in enumerate(masks):
                                    slp = ps_s[sb][:, 512*u + 128*jq: 512*u + 128*(jq+1)]
                                    i = nc.tensor.matmul(slp, C["ident_bf"][:], mk[:],
                                                         start=False,
                                                         stop=(mi == len(masks) - 1))
                            pev = ctx.inc("pe", i)
                            ctx.wait(nc.scalar, "pe", pev)
                            A.activation(pT[:, 1024*pb:1024*(pb+1)], ps_s[sb][:],
                                         AF.Exp, scale=RQK)
                            act_exp[kp] = ctx.val("act")
                            ctx.wait(nc.vector, "act", act_exp[kp])
                            V.tensor_scalar(pT[:, 1024*pb:1024*(pb+1)],
                                            pT[:, 1024*pb:1024*(pb+1)],
                                            CLIP_HI, None, ALU.min)
                            dve_fix[kp] = ctx.val("dve")

                        def emit_PV(kp):
                            pb = kp % 4
                            ctx.wait(nc.tensor, "dve", dve_fix[kp])
                            for u in range(2):
                                kt = 2*kp + u
                                i = nc.tensor.matmul(ps_pv[:], vtok[hh][:, 65*kt:65*(kt+1)],
                                                     pT[:, 1024*pb + 512*u: 1024*pb + 512*(u+1)],
                                                     start=(kt == 0), stop=(kt == kt_max))
                            pv_pe[kp] = ctx.inc("pe", i)
                            return pv_pe[kp]

                        emit_S(0)
                        if norm is not None:
                            ctx.wait(nc.vector, "pe", norm[0])
                            V.reciprocal(rrow[:], ps_pv[64:65, :])
                            V.tensor_copy(pvs[0:64, :], ps_pv[0:64, :])
                            dv_copy = ctx.val("dve")
                        emit_S(1)
                        if norm is not None:
                            ctx.wait(nc.tensor, "dve", dv_copy)
                        emit_PV(0)
                        if norm is not None:
                            i = nc.tensor.matmul(ps_pb, C["ones1_f32"][:, 0:64],
                                                 rrow[:], start=True, stop=True)
                            pev = ctx.inc("pe", i)
                            ctx.wait(nc.vector, "pe", pev)
                            V.tensor_mul(att[p][64*hh:64*hh+64,
                                                512*norm[1]:512*(norm[1]+1)],
                                         pvs[0:64, :], ps_pb)
                        for kp in range(2, npairk):
                            emit_S(kp)
                            emit_PV(kp - 1)
                        norm = (emit_PV(npairk - 1), qg)
                    ctx.wait(nc.vector, "pe", norm[0])
                    V.reciprocal(rrow[:], ps_pv[64:65, :])
                    dv = ctx.val("dve")
                    V.tensor_copy(pvs[0:64, :], ps_pv[0:64, :])
                    ctx.wait(nc.tensor, "dve", dv)
                    i = nc.tensor.matmul(ps_pb, C["ones1_f32"][:, 0:64], rrow[:],
                                         start=True, stop=True)
                    pev = ctx.inc("pe", i)
                    ctx.wait(nc.vector, "pe", pev)
                    V.tensor_mul(att[p][64*hh:64*hh+64, 512*norm[1]:512*(norm[1]+1)],
                                 pvs[0:64, :], ps_pb)
                    ctx.wait(nc.tensor, "dve", ctx.val("dve"))
        with ctx.scope() as p2:
            xres = p2.sb("a0_xres", [128, NC * 512], F32)
            wor = p2.sb("a0_wo", [128, 64 * 128], BF16)
            ctx.wait(nc.gpsimd, "qg", ctx.val("qg"))
            ctx.inc("qg", nc.gpsimd.dma_start(
                out=wor[:].rearrange("k (m c f) -> k m c f", m=8, c=8),
                in_=aps["wo"].rearrange("m c k f -> k m c f")), 16)
            for g in range(8):
                if g >= 1:
                    ctx.wait(nc.sync, "dve", ctx.val("dve"))
                    ctx.wait(nc.sync, "qs", ctx.val("qs"))
                for c in range(NC):
                    ctx.inc("qs", nc.sync.dma_start(
                        out=xres[:, 512*c:512*(c+1)],
                        in_=xsrc[128*c:128*(c+1), 512*g:512*(g+1)]), 16)
                ctx.wait(nc.vector, "qs")
                if g == 0:
                    ctx.wait(nc.tensor, "qg", ctx.val("qg"))
                    bank_ep = {}
                for dm in range(NC):
                    bk = dm % 2
                    if bk in bank_ep:
                        ctx.wait(nc.tensor, "dve", bank_ep[bk])
                    for hd in range(npair):
                        i = nc.tensor.matmul(ps_j[bk][:],
                                             wor[:, 1024*dm + 128*hd: 1024*dm + 128*(hd+1)],
                                             att[hd][:, 512*g:512*(g+1)],
                                             start=(hd == 0), stop=(hd == npair - 1))
                    pev = ctx.inc("pe", i)
                    ctx.wait(nc.vector, "pe", pev)
                    XR = xres[:, 512*dm:512*(dm+1)]
                    V.tensor_add(XR, XR, ps_j[bk][:])
                    bank_ep[bk] = ctx.val("dve")
                    ctx.wait(nc.sync, "dve", bank_ep[bk])
                    ctx.inc("qs", nc.sync.dma_start(
                        out=xdst[128*dm:128*(dm+1), 512*g:512*(g+1)], in_=XR), 16)


def build_sub3(H, debug=False, phases=("ln", "gate", "experts")):
    nc = bass.Bass("TRN2", target_bir_lowering=False)
    aps = declare_inputs(nc, H, KEYS_SUB3)
    xT = nc.dram_tensor("xT", [D, S], F32, kind="ExternalInput").ap()
    xout = nc.dram_tensor("xout", [D, S], F32, kind="ExternalOutput").ap()
    xlnd = nc.dram_tensor("xln", [D, S], F32,
                          kind="ExternalOutput" if debug else "Internal").ap()
    dbg = {}
    if debug:
        dbg["comb_dbg"] = nc.dram_tensor("comb_dbg", [128, 8 * NT], F32, kind="ExternalOutput").ap()
    with ExitStack() as es:
        ctx = Ctx(nc, es)
        blk = es.enter_context(nc.Block())
        with ctx.scope() as sc:
            C = load_consts(ctx, sc, aps)
            comb_tm = sc.sb("comb_tm", [128, 8 * NT], F32)
            combT = sc.sb("combT", [8, S], F32)
            if "ln" in phases:
                emit_ln(ctx, sc, C, xT, xlnd)
            if "gate" in phases:
                emit_l1_gate(ctx, sc, C, aps, xlnd, comb_tm, combT)
            if "experts" in phases:
                kw = {}
                for p in phases:
                    if p.startswith("k:"):
                        k, v = p[2:].split("=")
                        kw[k] = int(v)
                emit_l1_experts(ctx, sc, C, aps, xlnd, comb_tm, combT, xT, xout, **kw)
            else:
                for c in range(NC):
                    ctx.inc("qs", nc.sync.dma_start(
                        out=xout[128*c:128*(c+1), :], in_=xlnd[128*c:128*(c+1), :]), 16)
            if debug:
                ctx.barrier()
                ctx.inc("qs", nc.sync.dma_start(out=dbg["comb_dbg"], in_=comb_tm[:]), 16)
        ctx.barrier()
    return nc

def build_sub0(H, ncores=8, phases=("ln", "sel", "experts")):
    nc = bass.Bass("TRN2", target_bir_lowering=False)
    aps = declare_inputs(nc, H, KEYS_SUB0)
    xT = nc.dram_tensor("xT", [D, S], F32, kind="ExternalInput").ap()
    xout = nc.dram_tensor("xout", [D, S], F32, kind="ExternalOutput").ap()
    xlnd = nc.dram_tensor("xln", [D, S], F32, kind="Internal").ap()
    cc_in = nc.dram_tensor("cc_in", [8, S], F32, kind="Internal").ap()
    cc_out = nc.dram_tensor("cc_out", [8, S], F32, kind="Internal").ap()
    dbg = {}
    if "debug" in phases:
        dbg["sel_dbg"] = nc.dram_tensor("sel_dbg", [8, S], F32, kind="ExternalOutput").ap()
    with ExitStack() as es:
        ctx = Ctx(nc, es)
        blk = es.enter_context(nc.Block())
        with ctx.scope() as sc:
            C = load_consts(ctx, sc, aps)
            sel05_tm = sc.sb("sel_tm", [128, 8 * NT], F32)
            sel05T = sc.sb("sel05T", [8, S], F32)
            if "ln" in phases:
                emit_ln(ctx, sc, C, xT, xlnd)
            if "sel" in phases:
                emit_l0_sel(ctx, sc, C, aps, xlnd, cc_in, cc_out, sel05_tm, sel05T,
                            ncores=ncores)
            kw = {}
            for p in phases:
                if p.startswith("k:"):
                    k, v = p[2:].split("=")
                    kw[k] = int(v)
            if "experts" in phases:
                emit_l0_experts(ctx, sc, C, aps, xlnd, sel05_tm, sel05T, xT, xout, **kw)
            if "debug" in phases:
                ctx.barrier()
                ctx.inc("qs", nc.sync.dma_start(out=dbg["sel_dbg"], in_=sel05T[:]), 16)
        ctx.barrier()
    return nc

def build_sub1(H, phases=("ln", "attn")):
    nc = bass.Bass("TRN2", target_bir_lowering=False)
    aps = declare_inputs(nc, H, KEYS_SUB1)
    xT = nc.dram_tensor("xT", [D, S], F32, kind="ExternalInput").ap()
    xout = nc.dram_tensor("xout", [D, S], F32, kind="ExternalOutput").ap()
    xlnd = nc.dram_tensor("xln", [D, S], F32, kind="Internal").ap()
    with ExitStack() as es:
        ctx = Ctx(nc, es)
        blk = es.enter_context(nc.Block())
        with ctx.scope() as sc:
            C = load_consts(ctx, sc, aps)
            if "ln" in phases:
                emit_ln(ctx, sc, C, xT, xlnd)
            kw = {}
            for p in phases:
                if p.startswith("k:"):
                    k, v = p[2:].split("=")
                    kw[k] = int(v)
            emit_l0_attn(ctx, sc, C, aps, xlnd, xT, xout, **kw)
        ctx.barrier()
    return nc

def build_sub4(H, phases=("ln", "attn")):
    nc = bass.Bass("TRN2", target_bir_lowering=False)
    aps = declare_inputs(nc, H, KEYS_SUB4)
    xT = nc.dram_tensor("xT", [D, S], F32, kind="ExternalInput").ap()
    xout = nc.dram_tensor("xout", [D, S], F32, kind="ExternalOutput").ap()
    xlnd = nc.dram_tensor("xln", [D, S], F32, kind="Internal").ap()
    with ExitStack() as es:
        ctx = Ctx(nc, es)
        blk = es.enter_context(nc.Block())
        with ctx.scope() as sc:
            C = load_consts(ctx, sc, aps)
            if "ln" in phases:
                emit_ln(ctx, sc, C, xT, xlnd)
            kw = {}
            for p in phases:
                if p.startswith("k:"):
                    k, v = p[2:].split("=")
                    kw[k] = int(v)
            emit_l1_attn(ctx, sc, C, aps, xlnd, xT, xout, **kw)
        ctx.barrier()
    return nc


def host_prepare(I):
    H = {}
    H.update(host_constants())
    H["wg1"] = np.ascontiguousarray(np.asarray(I["L1_ffn_Wg"], np.float32).reshape(8, 128, 8).transpose(1, 0, 2).reshape(128, 64))
    H["bg1"] = np.ascontiguousarray(np.asarray(I["L1_ffn_bg"], np.float32).reshape(1, 8))
    W1 = np.asarray(I["L1_ffn_W1"], np.float32)
    H["w1l1"] = np.stack([
        W1[e].reshape(8, 128, 32, 128).transpose(2, 1, 0, 3).reshape(32, 128, 1024)
        for e in range(EXP)])
    W2 = np.asarray(I["L1_ffn_W2"], np.float32)
    H["w2l1"] = np.ascontiguousarray(W2.reshape(EXP, 4, 8, 128, 1024))
    H["b1l1"] = np.ascontiguousarray(np.stack(
        [feat_cols(np.asarray(I["L1_ffn_b1"], np.float32)[e]) for e in range(EXP)]
    ).transpose(1, 0, 2).reshape(128, EXP * 32))
    H["b2l1"] = np.ascontiguousarray(np.asarray(I["L1_ffn_b2"], np.float32))
    H["gam1"] = feat_cols(np.asarray(I["L1_ffn_gamma"], np.float32))
    H["bet1"] = feat_cols(np.asarray(I["L1_ffn_beta"], np.float32))
    H["wg0"] = np.ascontiguousarray(np.asarray(I["L0_ffn_w_gate_w"], np.float32).reshape(8, 128, 8).transpose(1, 0, 2).reshape(128, 64))
    H["bg0"] = np.ascontiguousarray(np.asarray(I["L0_ffn_w_gate_b"], np.float32).reshape(1, 8))
    W1 = np.asarray(I["L0_ffn_W1"], np.float32)
    w1 = np.stack([
        W1[e].reshape(8, 128, 2, 128).transpose(2, 1, 0, 3).reshape(2, 128, 1024)
        for e in range(EXP)])
    H["w1l0r"] = np.ascontiguousarray(w1.transpose(2, 0, 1, 3).reshape(128, EXP * 2 * 1024))
    W2 = np.asarray(I["L0_ffn_W2"], np.float32)
    w2 = np.stack([W2[e].reshape(2, 128, 1024).transpose(1, 0, 2) for e in range(EXP)])
    H["w2l0r"] = np.ascontiguousarray(w2.transpose(1, 0, 2, 3).reshape(128, EXP * 2 * 1024))
    H["b1l0"] = np.ascontiguousarray(np.stack(
        [feat_cols(np.asarray(I["L0_ffn_b1"], np.float32)[e]) for e in range(EXP)]
    ).transpose(1, 0, 2).reshape(128, EXP * 2))
    H["b2l0"] = np.ascontiguousarray(np.asarray(I["L0_ffn_b2"], np.float32))
    H["wdq"] = lhsT_blocks(np.asarray(I["L0_attn_W_DQ"], np.float32))
    H["wdkv"] = lhsT_blocks(np.asarray(I["L0_attn_W_DKV"], np.float32))
    H["wuq"] = lhsT_blocks(np.asarray(I["L0_attn_W_UQ"], np.float32))
    H["wuk"] = lhsT_blocks(np.asarray(I["L0_attn_W_UK"], np.float32))
    H["wqr"] = lhsT_blocks(np.asarray(I["L0_attn_W_QR"], np.float32))
    wuv = np.asarray(I["L0_attn_W_UV"], np.float32)
    H["wuvr"] = np.ascontiguousarray(
        wuv.reshape(2, 128, 1024).transpose(1, 0, 2).reshape(128, 2048))
    H["wkr"] = lhsT_blocks(np.asarray(I["L0_attn_W_KR"], np.float32), mc=32)
    H["wo"] = lhsT_blocks(np.asarray(I["L0_attn_W_O"], np.float32))
    H["wqkv"] = lhsT_blocks(np.asarray(I["L1_attn_W_qkv"], np.float32))
    H["bqkv"] = feat_cols(np.asarray(I["L1_attn_b_qkv"], np.float32))
    H["wout"] = lhsT_blocks(np.asarray(I["L1_attn_W_out"], np.float32))
    H["bout"] = feat_cols(np.asarray(I["L1_attn_b_out"], np.float32))
    return H

def per_core_inputs(I, H):
    x = np.asarray(I["L0_attn_x"], np.float32)
    maps = []
    for b in range(8):
        m = dict(H)
        m["xT"] = np.ascontiguousarray(x[b].T)
        maps.append(m)
    return maps

def build_full(H, ncores=8):
    nc = bass.Bass("TRN2", target_bir_lowering=False)
    keys = sorted(set(KEYS_SUB0 + KEYS_SUB1 + KEYS_SUB3 + KEYS_SUB4))
    aps = declare_inputs(nc, H, keys)
    xT = nc.dram_tensor("xT", [D, S], F32, kind="ExternalInput").ap()
    xout = nc.dram_tensor("xout", [D, S], F32, kind="ExternalOutput").ap()
    xw = nc.dram_tensor("xw", [D, S], F32, kind="Internal").ap()
    xlnd = nc.dram_tensor("xln", [D, S], F32, kind="Internal").ap()
    cc_in = nc.dram_tensor("cc_in", [8, S], F32, kind="Internal").ap()
    cc_out = nc.dram_tensor("cc_out", [8, S], F32, kind="Internal").ap()
    w1bf = nc.dram_tensor("w1bf", [EXP, 32, 128, 1024], BF16, kind="Internal").ap()
    w2bf = nc.dram_tensor("w2bf", [EXP, 4, 8, 128, 1024], BF16, kind="Internal").ap()
    with ExitStack() as es:
        ctx = Ctx(nc, es)
        blk = es.enter_context(nc.Block())
        with ctx.scope() as sc:
            C = load_consts(ctx, sc, aps)
            wcstate = {}
            w1f = aps["w1l1"].rearrange("e h k m -> (e h) k m")
            w1d = w1bf.rearrange("e h k m -> (e h) k m")
            w2f = aps["w2l1"].rearrange("e g h k m -> (e g h) k m")
            w2d = w2bf.rearrange("e g h k m -> (e g h) k m")

            def ffn0(xsrc, xdst):
                with ctx.scope() as ss:
                    tm = ss.sb("sel_tm", [128, 8 * NT], F32)
                    sT = ss.sb("sel05T", [8, S], F32)
                    emit_l0_sel(ctx, sc, C, aps, xlnd, cc_in, cc_out, tm, sT, ncores=ncores)
                    emit_l0_experts(ctx, sc, C, aps, xlnd, tm, sT, xsrc, xdst)

            def ffn1(xsrc, xdst):
                with ctx.scope() as ss:
                    tm = ss.sb("comb_tm", [128, 8 * NT], F32)
                    cT = ss.sb("combT", [8, S], F32)
                    emit_l1_gate(ctx, sc, C, aps, xlnd, tm, cT)
                    emit_l1_experts(ctx, sc, C, aps, xlnd, tm, cT, xsrc, xdst,
                                    w1src=w1bf, w2src=w2bf)

            with ctx.scope() as early:
                wcbufs = [early.sb(f"wc{i}", [128, 2048], BF16) for i in range(2)]
                emit_ln(ctx, sc, C, xT, xlnd)
                emit_w_convert(ctx, wcbufs, wcstate, [(w1f[0:128], w1d[0:128])])
                ffn0(xT, xw)
                emit_w_convert(ctx, wcbufs, wcstate, [(w1f[128:256], w1d[128:256])])
                emit_ln(ctx, sc, C, xw, xlnd)
                emit_l0_attn(ctx, sc, C, aps, xlnd, xw, xw)
                emit_w_convert(ctx, wcbufs, wcstate, [(w2f[0:128], w2d[0:128])])
                emit_ln(ctx, sc, C, xw, xlnd)
                ffn0(xw, xw)
                emit_w_convert(ctx, wcbufs, wcstate, [(w2f[128:256], w2d[128:256])])
            emit_ln(ctx, sc, C, xw, xlnd)
            ffn1(xw, xw)
            emit_ln(ctx, sc, C, xw, xlnd)
            emit_l1_attn(ctx, sc, C, aps, xlnd, xw, xw)
            emit_ln(ctx, sc, C, xw, xlnd)
            ffn1(xw, xout)
        ctx.barrier()
    return nc

def kernel_multilaunch(**inputs):
    H = host_prepare(inputs)
    x = np.asarray(inputs["L0_attn_x"], np.float32)
    cur = [np.ascontiguousarray(x[b].T) for b in range(8)]
    progs = {
        "s0": (build_sub0(H), KEYS_SUB0),
        "s1": (build_sub1(H), KEYS_SUB1),
        "s3": (build_sub3(H), KEYS_SUB3),
        "s4": (build_sub4(H), KEYS_SUB4),
    }
    for tag in ["s0", "s1", "s0", "s3", "s4", "s3"]:
        prog, keys = progs[tag]
        maps = []
        for b in range(8):
            m = {k: H[k] for k in keys}
            m["xT"] = cur[b]
            maps.append(m)
        res = run_bass_kernel_spmd(prog, maps, core_ids=list(range(8)))
        cur = [np.ascontiguousarray(np.asarray(res.results[b]["xout"], np.float32))
               for b in range(8)]
    return np.stack([c.T for c in cur]).astype(np.float32)

def kernel(**inputs):
    H = host_prepare(inputs)
    x = np.asarray(inputs["L0_attn_x"], np.float32)
    keys = sorted(set(KEYS_SUB0 + KEYS_SUB1 + KEYS_SUB3 + KEYS_SUB4))
    prog = build_full(H)
    maps = []
    for b in range(8):
        m = {k: H[k] for k in keys}
        m["xT"] = np.ascontiguousarray(x[b].T)
        maps.append(m)
    res = run_bass_kernel_spmd(prog, maps, core_ids=list(range(8)))
    out = np.stack([np.asarray(res.results[b]["xout"], np.float32).T for b in range(8)])
    return np.ascontiguousarray(out).astype(np.float32)
